# version 18
# baseline (speedup 1.0000x reference)
import numpy as np
import ml_dtypes
import concourse.bass as bass
import concourse.mybir as mybir
from concourse.bass_utils import run_bass_kernel_spmd
from contextlib import ExitStack

F32 = mybir.dt.float32
BF16 = mybir.dt.bfloat16
AF = mybir.ActivationFunctionType
ALU = mybir.AluOpType
AX = mybir.AxisListType
f32 = np.float32

D = 1024
FF = 4096
II = 2816
NH = 8
WIN = 256
SEQ = 2048
BLOC = 2
T = BLOC * SEQ
NT = T // 128
TG = 512
NG = T // TG
DC = D // 128
FC = FF // 128
IC = II // 128
NCH = SEQ // WIN
KPAD = WIN
TK = T + 2 * KPAD
SCALE = float(128.0) ** -0.5
EPS = 1e-5
QMAX = 127.0
MAGIC = float(f32(3 * 2 ** 22))
MASKV = -30000.0
LN127 = float(np.log(f32(127.0)))
RG8 = [[0, 1, 2, 3, 4, 5, 6, 7]]

ENGS = ("sync", "tensor", "vector", "scalar", "gpsimd")


def _build_masks():
    kr = np.arange(768)
    qi = np.arange(256)
    band = (kr[None, :] >= qi[:, None]) & (kr[None, :] <= qi[:, None] + 2 * WIN)
    masks = np.zeros((3, 2, 128, 768), np.float32)
    nonzero = {}
    for v in range(3):
        valid = band.copy()
        if v == 0:
            valid = valid & (kr[None, :] >= WIN)
        elif v == 2:
            valid = valid & (kr[None, :] < 2 * WIN)
        add = np.where(valid, 0.0, MASKV).astype(np.float32)
        for qs in range(2):
            masks[v, qs] = add[qs * 128:(qs + 1) * 128]
            nonzero[(v, qs)] = [
                kc for kc in range(6)
                if np.any(masks[v, qs][:, kc * 128:(kc + 1) * 128] != 0.0)]
    return masks.astype(ml_dtypes.bfloat16), nonzero


MASKS_NP, MASK_NZ = _build_masks()


class Prog:

    def __init__(self, nc):
        self.nc = nc
        self.ops = {e: [] for e in ENGS}
        self.sem_h = {}
        self.cnt = {}
        self.waited = {e: {} for e in ENGS}

    def _prep(self, eng, waits, sig, inc):
        nw = []
        for t in waits:
            if t is None:
                continue
            s, v = t
            if self.waited[eng].get(s, 0) >= v:
                continue
            nw.append((s, v))
            self.waited[eng][s] = v
        tok = None
        if sig is not None:
            self.cnt[sig] = self.cnt.get(sig, 0) + inc
            tok = (sig, self.cnt[sig])
        return tuple(nw), tok

    def emit(self, eng, fn, waits=(), sig=None, inc=1):
        nw, tok = self._prep(eng, waits, sig, inc)

        def closure(nw=nw, sig=sig, fn=fn, inc=inc, eng=eng):
            e = getattr(self.nc, eng)
            for (s, v) in nw:
                e.wait_ge(self.sem_h[s], v)
            inst = fn(e)
            if sig is not None:
                inst.then_inc(self.sem_h[sig], inc)

        self.ops[eng].append(closure)
        return tok

    def join(self, eng, waits, sig=None):
        if sig is None:
            sig = eng
        nw, tok = self._prep(eng, waits, sig, 1)

        def closure(nw=nw, sig=sig, eng=eng):
            e = getattr(self.nc, eng)
            for (s, v) in nw:
                e.wait_ge(self.sem_h[s], v)
            e.sem_inc(self.sem_h[sig], 1)

        self.ops[eng].append(closure)
        return tok

    def barrier(self, engs=ENGS):
        dma_toks = [(n, v) for n, v in self.cnt.items() if n.startswith("d")]
        toks = [self.join(e, dma_toks) for e in engs]
        for e in engs:
            nw, _ = self._prep(e, toks, None, 1)

            def closure(nw=nw, eng=e):
                en = getattr(self.nc, eng)
                for (s, v) in nw:
                    en.wait_ge(self.sem_h[s], v)

            self.ops[e].append(closure)

    def replay(self):
        with ExitStack() as st:
            for s in self.cnt:
                self.sem_h[s] = st.enter_context(self.nc.semaphore(s))
            for e in ENGS:
                for s, v in self.waited[e].items():
                    assert self.cnt.get(s, 0) >= v, (e, s, v, self.cnt.get(s))
            with self.nc.Block() as block:
                for e in ENGS:
                    getattr(block, e)(lambda _, e=e: [f() for f in self.ops[e]])


def build_program(level=99):
    nc = bass.Bass("TRN2", target_bir_lowering=False)
    P = Prog(nc)

    def din(name, shape):
        return nc.dram_tensor(name, shape, F32, kind="ExternalInput").ap()

    x0_d = din("x0", [T, D])
    Wq_d, Wk_d, Wv_d, Wo_d = (din(n, [D, D]) for n in ("Wq", "Wk", "Wv", "Wo"))
    bq_d, bk_d, bv_d = (din(n, [128, DC]) for n in ("bq", "bk", "bv"))
    gamma_d = din("gamma", [1, D])
    beta_d = din("beta", [1, D])
    W1_d = din("W1", [D, FF])
    b1_d = din("b1", [128, FC])
    W2_d = din("W2", [FF, D])
    b2_d = din("b2", [1, D])
    bo_row_d = din("bo_row", [1, D])
    clipx_d = din("clip_x", [1, 1])
    clipd_d = din("clip_down", [1, 1])
    wg_d = din("w_gate", [II, D])
    wu_d = din("w_up", [II, D])
    wd_d = din("w_down", [D, II])

    out_d = nc.dram_tensor("out", [T, D], F32, kind="ExternalOutput").ap()

    x1_scr = nc.dram_tensor("x1_scr", [T, D], F32, kind="Internal").ap()
    hact_scr = nc.dram_tensor("hact_scr", [II, T], BF16, kind="Internal").ap()
    w1_scr = nc.dram_tensor("w1_scr", [FC, 128, D], BF16, kind="Internal").ap()
    w2_scr = nc.dram_tensor("w2_scr", [FF, D], BF16, kind="Internal").ap()
    gmx_scr = nc.dram_tensor("gmx_scr", [128, 1], F32, kind="Internal").ap()
    dbg_scr = nc.dram_tensor(
        "dbg", [1024, D], F32,
        kind="ExternalOutput" if level <= 7 else "Internal").ap()
    cc1_in = nc.dram_tensor("cc1_in", [1, 1], F32, kind="Internal").ap()
    cc1_out = nc.dram_tensor("cc1_out", [1, 1], F32, kind="Internal",
                             addr_space="Shared").ap()
    cc2_in = nc.dram_tensor("cc2_in", [1, 1], F32, kind="Internal").ap()
    cc2_out = nc.dram_tensor("cc2_out", [1, 1], F32, kind="Internal",
                             addr_space="Shared").ap()

    ident_d = nc.inline_tensor(np.eye(128, dtype=ml_dtypes.bfloat16), "identc").ap()
    masks_d = nc.inline_tensor(
        np.ascontiguousarray(MASKS_NP.transpose(2, 0, 1, 3).reshape(128, 6 * 768)),
        "maskc").ap()
    ones_d = nc.inline_tensor(np.ones((1, 128), np.float32), "onesc").ap()

    st_perm = ExitStack()

    def sb(name, shape, dt):
        return st_perm.enter_context(
            nc.sbuf_tensor(name, shape, dt, side="left")).ap()

    ident = sb("ident", [128, 128], BF16)
    ones_r = sb("ones_r", [1, 128], F32)
    bq_s = sb("bq_s", [128, DC], F32)
    bk_s = sb("bk_s", [128, DC], F32)
    bv_s = sb("bv_s", [128, DC], F32)
    b1_s = sb("b1_s", [128, FC], F32)
    clip_s = sb("clip_s", [1, 2], F32)
    mu1 = sb("mu1", [128, NT], F32)
    r1 = sb("r1", [128, NT], F32)
    nmr1 = sb("nmr1", [128, NT], F32)
    mu2 = sb("mu2", [128, NT], F32)
    r2 = sb("r2", [128, NT], F32)
    nmr2 = sb("nmr2", [128, NT], F32)
    gmv = sb("gmv", [128, 1], F32)
    gmv2 = sb("gmv2", [128, 1], F32)
    stg1 = sb("stg1", [128, 20], F32)
    stg2 = sb("stg2", [128, 5], F32)
    lni = sb("lni", [128, 3], F32)
    gtmp = sb("gtmp", [128, 2], F32)
    srow = sb("srow", [1, 8], F32)
    qsc = sb("qsc", [128, 4], F32)
    scrap = sb("scrap", [128, 1024], BF16)
    sums_c = sb("sums_c", [128, 4], F32)
    recip_c = sb("recip_c", [128, 4], F32)
    rm_c = sb("rm_c", [128, 4], F32)
    grow = sb("grow", [1, 128], F32)
    qa = sb("qa", [128, NT], F32)
    qb = sb("qb", [128, NT], F32)
    rmx = sb("rmx", [128, 4], F32)
    rsc = sb("rsc", [128, 4], F32)
    ssc = sb("ssc", [128, 4], F32)

    st_ofm = ExitStack()
    o_fm = [st_ofm.enter_context(
        nc.sbuf_tensor(f"o_fm{h}", [128, T], BF16, side="left")).ap()
        for h in range(NH)]

    st_ps = ExitStack()

    def pst(name, shape, dt):
        return st_ps.enter_context(nc.psum_tensor(name, shape, dt)).ap()

    PB1 = pst("PB1", [128, 1024], F32)
    PB2 = pst("PB2", [128, 1024], F32)
    PB3 = pst("PB3", [128, 1024], F32)
    PT = pst("PTT", [128, 1024], BF16)
    PT2 = pst("PTT2", [128, 1024], BF16)
    PA = PB1[:, 0:512]
    PB = PB1[:, 512:1024]
    PC = PB2[:, 0:512]
    PD = PB2[:, 512:1024]
    PS_e = PB1[:, 0:768]
    PS_o = PB2[:, 0:768]
    PO2 = PB3[:, 0:256]

    for dst, src in ((ident, ident_d), (ones_r, ones_d),
                     (bq_s, bq_d), (bk_s, bk_d), (bv_s, bv_d), (b1_s, b1_d)):
        P.emit("sync", lambda e, d=dst, s=src: e.dma_start(out=d, in_=s),
               sig="dCst", inc=16)
    P.emit("sync", lambda e: e.dma_start(out=clip_s[:, 0:1], in_=clipx_d),
           sig="dCst", inc=16)
    tok_consts = P.emit("sync",
                        lambda e: e.dma_start(out=clip_s[:, 1:2], in_=clipd_d),
                        sig="dCst", inc=16)
    t_gm0 = P.emit("vector", lambda e: e.memset(gmv[:], 0.0), sig="vector")
    t_gm20 = P.emit("vector", lambda e: e.memset(gmv2[:], 0.0), sig="vector")

    st_setup = ExitStack()
    wtmp = [st_setup.enter_context(
        nc.sbuf_tensor(f"wtmp{i}", [128, D], BF16, side="right")).ap()
        for i in range(2)]
    wst = [None, None]
    for oc in range(FC):
        b = oc % 2
        tin = P.emit("gpsimd", lambda e, oc=oc, b=b: e.dma_start(
            out=wtmp[b].rearrange("p (c o) -> p c o", o=128),
            in_=W1_d.rearrange("(c p) o -> p c o", p=128)[:, :, oc * 128:(oc + 1) * 128],
            allow_slow_non_contiguous=True),
            waits=[wst[b]], sig=f"dWc{b}", inc=16)
        wst[b] = P.emit("gpsimd", lambda e, oc=oc, b=b: e.dma_start(
            out=w1_scr[oc], in_=wtmp[b][:]), waits=[tin], sig=f"dWs{b}", inc=16)
    for fc in range(FC):
        b = fc % 2
        tin = P.emit("gpsimd", lambda e, fc=fc, b=b: e.dma_start(
            out=wtmp[b][:], in_=W2_d[fc * 128:(fc + 1) * 128, :]),
            waits=[wst[b]], sig=f"dWc{b}", inc=16)
        wst[b] = P.emit("gpsimd", lambda e, fc=fc, b=b: e.dma_start(
            out=w2_scr[fc * 128:(fc + 1) * 128, :], in_=wtmp[b][:]),
            waits=[tin], sig=f"dWs{b}", inc=16)
    tok_w12 = P.join("sync", [wst[0], wst[1]])

    def ln_stats(xap, width, mu_ap, r_ap, nmr_ap, stg, wait_toks, war_tok):
        t1 = P.emit("vector", lambda e: e.tensor_reduce(
            stg[:, 0:1], xap, AX.X, ALU.add),
            waits=list(wait_toks) + [war_tok], sig="vector")
        t2 = P.emit("scalar", lambda e: e.activation(
            scrap[:, 0:width], xap, AF.Square, accum_out=stg[:, 1:2]),
            waits=list(wait_toks) + [war_tok], sig="scalar")
        t3 = P.emit("vector", lambda e: e.tensor_scalar(
            mu_ap, stg[:, 0:1], 1.0 / width, None, ALU.mult),
            waits=[t1], sig="vector")
        t4 = P.emit("vector", lambda e: e.tensor_scalar(
            stg[:, 2:3], stg[:, 1:2], 1.0 / width, None, ALU.mult),
            waits=[t2], sig="vector")
        t5 = P.emit("vector", lambda e: e.tensor_tensor(
            stg[:, 3:4], mu_ap, mu_ap, ALU.mult), waits=[t3], sig="vector")
        t6 = P.emit("vector", lambda e: e.tensor_tensor(
            stg[:, 2:3], stg[:, 2:3], stg[:, 3:4], ALU.subtract),
            waits=[t4, t5], sig="vector")
        t7 = P.emit("scalar", lambda e: e.activation(
            stg[:, 4:5], stg[:, 2:3], AF.Ln, bias=f32(EPS)),
            waits=[t6], sig="scalar")
        t8 = P.emit("scalar", lambda e: e.activation(
            r_ap, stg[:, 4:5], AF.Exp, scale=f32(-0.5)), waits=[t7],
            sig="scalar")
        t9 = P.emit("vector", lambda e: e.tensor_scalar(
            nmr_ap, mu_ap, r_ap, -1.0, ALU.mult, ALU.mult),
            waits=[t8, t3], sig="vector")
        return t9

    st_x1fm = ExitStack()
    xh1_fm = st_x1fm.enter_context(
        nc.sbuf_tensor("xh1_fm", [128, DC * T], BF16, side="right")).ap()
    xh1_3d = xh1_fm.rearrange("p (c t) -> p c t", t=T)
    st_p1a = ExitStack()
    x0b = [st_p1a.enter_context(
        nc.sbuf_tensor(f"x0b{i}", [128, D], F32, side="right")).ap()
        for i in range(2)]
    xhb = [st_p1a.enter_context(
        nc.sbuf_tensor(f"xhb{i}", [128, D], BF16, side="right")).ap()
        for i in range(2)]

    x0_ld = [None] * 2
    apply_tok = [None] * 2
    fm_evac = None
    war1 = None
    war1p = [None] * 2
    for tt in range(NT):
        b = tt % 2
        x0_ld[b] = P.emit("sync", lambda e, tt=tt, b=b: e.dma_start(
            out=x0b[b][:], in_=x0_d[tt * 128:(tt + 1) * 128, :]),
            waits=[apply_tok[b]], sig=f"dX{b}", inc=16)
        war1p[b] = ln_stats(x0b[b][:], D, mu1[:, tt:tt + 1], r1[:, tt:tt + 1],
                            nmr1[:, tt:tt + 1], stg1[:, b * 5:b * 5 + 5],
                            [x0_ld[b]], war1p[b])
        war1 = war1p[b]
        tap = P.emit("scalar", lambda e, tt=tt, b=b: e.activation(
            xhb[b][:], x0b[b][:], AF.Identity,
            bias=nmr1[:, tt:tt + 1], scale=r1[:, tt:tt + 1]),
            waits=[war1], sig="scalar")
        apply_tok[b] = tap
        tp = None
        for j in range(DC):
            tp = P.emit("tensor", lambda e, b=b, j=j: e.transpose(
                PT[:, j * 128:(j + 1) * 128], xhb[b][:, j * 128:(j + 1) * 128],
                ident), waits=[tap, fm_evac, tok_consts], sig="tensor")
        fm_evac = P.emit("scalar", lambda e, tt=tt: e.copy(
            xh1_3d[:, :, tt * 128:(tt + 1) * 128],
            PT[:].rearrange("p (c t) -> p c t", t=128)), waits=[tp], sig="scalar")
    tok_p1a = fm_evac

    if level <= 2:
        fins = []
        fins.append(P.emit("sync", lambda e: e.dma_start(
            out=out_d[0:128, 0:NT], in_=mu1[:]), waits=[tok_p1a], sig="dDmp", inc=16))
        fins.append(P.emit("sync", lambda e: e.dma_start(
            out=out_d[128:256, 0:NT], in_=r1[:]), waits=[tok_p1a], sig="dDmp", inc=16))
        fins.append(P.emit("sync", lambda e: e.dma_start(
            out=out_d[256:384, 0:NT], in_=nmr1[:]), waits=[tok_p1a], sig="dDmp", inc=16))
        fins.append(P.emit("gpsimd", lambda e: e.dma_start(
            out=out_d[384:896, :].rearrange("(p f) o -> p (f o)", p=128),
            in_=xh1_fm[:, 0:T]), waits=[tok_p1a], sig="dDmp", inc=16))
        P.join("sync", fins)
        P.replay()
        return nc

    st_p1b = ExitStack()

    def sbr(name, shape, dt):
        return st_p1b.enter_context(
            nc.sbuf_tensor(name, shape, dt, side="right")).ap()

    masks = sbr("masks", [128, 6 * 768], BF16)
    q_fm = sbr("q_fm", [128, T], BF16)
    k_fm = sbr("k_fm", [128, TK], BF16)
    v_fm = sbr("v_fm", [128, T], BF16)
    v_tm = sbr("v_tm", [128, TK], BF16)
    wsl = sbr("wsl", [128, 3 * D], BF16)
    p_sb = [sbr(f"p_sb{i}", [128, 2 * 768], BF16) for i in range(2)]
    pT_sb = sbr("pT_sb", [128, 2 * 768], BF16)

    P.emit("sync", lambda e: e.dma_start(out=masks, in_=masks_d), sig="dMsk", inc=16)
    P.emit("vector", lambda e: e.memset(k_fm[:, 0:KPAD], 0.0), sig="vector")
    P.emit("vector", lambda e: e.memset(k_fm[:, T + KPAD:TK], 0.0), sig="vector")
    P.emit("vector", lambda e: e.memset(v_tm[:, 0:KPAD], 0.0), sig="vector")
    P.emit("vector", lambda e: e.memset(pT_sb[:, 5 * 256:5 * 256 + 128], 0.0),
           sig="vector")
    P.emit("vector", lambda e: e.memset(pT_sb[:, 128:256], 0.0), sig="vector")
    tok_pad = P.emit("vector", lambda e: e.memset(v_tm[:, T + KPAD:TK], 0.0),
                     sig="vector")
    tok_mask = ("dMsk", P.cnt["dMsk"])

    wsl_free = None
    attn_prev = None
    ev = {}

    for h in range(NH if level >= 4 else 1):
        wtoks = []
        for wi, W_d in enumerate((Wq_d, Wk_d, Wv_d)):
            wtoks.append(P.emit("gpsimd", lambda e, wi=wi, W_d=W_d, h=h: e.dma_start(
                out=wsl.rearrange("p (w c o) -> p w c o", w=3, c=DC)[:, wi],
                in_=W_d.rearrange("(c p) o -> p c o", p=128)[
                    :, :, h * 128:(h + 1) * 128],
                allow_slow_non_contiguous=True),
                waits=[wsl_free], sig="dWsl", inc=16))
        wsl_ld = wtoks[-1]

        for wofs, dst, bias_s, base in ((0, q_fm, bq_s, 0),
                                        (D, k_fm, bk_s, KPAD),
                                        (2 * D, v_fm, bv_s, 0)):
            for g in range(NG):
                pn = ("PA", PA) if g % 2 == 0 else ("PB", PB)
                mm = None
                for fc in range(DC):
                    mm = P.emit("tensor", lambda e, fc=fc, g=g, pn=pn, wofs=wofs: e.matmul(
                        pn[1][:, :],
                        wsl[:, wofs + fc * 128:wofs + (fc + 1) * 128],
                        xh1_fm[:, fc * T + g * TG:fc * T + (g + 1) * TG],
                        start=(fc == 0), stop=(fc == DC - 1)),
                        waits=[wsl_ld, tok_p1a, ev.get(pn[0]), attn_prev],
                        sig="tensor")
                ev[pn[0]] = P.emit("scalar", lambda e, g=g, pn=pn, dst=dst,
                                   bias_s=bias_s, base=base, h=h: e.activation(
                    dst[:, base + g * TG:base + (g + 1) * TG], pn[1][:, :],
                    AF.Identity, bias=bias_s[:, h:h + 1]),
                    waits=[mm], sig="scalar")
        vev = (ev.get("PA"), ev.get("PB"))
        pt_prev2 = None
        for t8 in range(NT // 8):
            tp = None
            for j in range(8):
                tt = t8 * 8 + j
                tp = P.emit("tensor", lambda e, tt=tt, j=j: e.transpose(
                    PT[:, j * 128:(j + 1) * 128],
                    v_fm[:, tt * 128:(tt + 1) * 128], ident),
                    waits=[vev[0], vev[1], pt_prev2, tok_consts], sig="tensor")
            pt_prev2 = P.emit("scalar", lambda e, t8=t8: e.copy(
                v_tm[:, KPAD + t8 * 1024:KPAD + (t8 + 1) * 1024], PT[:]),
                waits=[tp], sig="scalar")
        qk_done = (ev.get("PA"), ev.get("PB"))
        v_done = (pt_prev2, pt_prev2)
        wsl_free = P.join("scalar", list(qk_done) + list(v_done))

        ps_free = {0: None, 1: None}
        pt_free = v_done[0]
        po_free = {0: None, 1: None}
        pend = None
        tails_done = []

        def emit_tail(ctx):
            nonlocal pt_free
            par, qbase, vblk, pn_toks = ctx
            ev_toks = []
            for qs in range(2):
                kc0 = qs
                tp = None
                for i5, kc in enumerate(range(kc0, kc0 + 5)):
                    tp = P.emit("tensor", lambda e, par=par, qs=qs, kc=kc,
                                i5=i5: e.transpose(
                        PT[:, i5 * 128:(i5 + 1) * 128],
                        p_sb[par][:, qs * 768 + kc * 128:qs * 768 + (kc + 1) * 128],
                        ident), waits=[pn_toks[qs], pt_free], sig="tensor")
                pt_free = P.emit("scalar", lambda e, qs=qs, kc0=kc0: e.copy(
                    pT_sb.rearrange("p (kc q) -> p kc q", q=256)[
                        :, kc0:kc0 + 5, qs * 128:(qs + 1) * 128],
                    PT[:, 0:640].rearrange("p (kc q) -> p kc q", q=128)),
                    waits=[tp], sig="scalar")
                ev_toks.append(pt_free)
            mm = None
            for kc in range(6):
                mm = P.emit("tensor", lambda e, kc=kc, vblk=vblk: e.matmul(
                    PO2[:, :],
                    v_tm[:, (vblk + kc) * 128:(vblk + kc + 1) * 128],
                    pT_sb[:, kc * 256:(kc + 1) * 256],
                    start=(kc == 0), stop=(kc == 5)),
                    waits=[ev_toks[0], ev_toks[1], po_free[0],
                           v_done[0], v_done[1]], sig="tensor")
            po_free[0] = P.emit("scalar", lambda e, h=h, qbase=qbase: e.copy(
                o_fm[h][:, qbase:qbase + 256], PO2[:, :]),
                waits=[mm], sig="scalar")
            tails_done.append(po_free[0])

        idx = 0
        for bb in range(BLOC):
            for c in range(NCH):
                var = 0 if c == 0 else (2 if c == NCH - 1 else 1)
                qbase = bb * SEQ + c * WIN
                wbase = bb * SEQ + c * WIN
                vblk = wbase // 128
                par = idx % 2
                exp_toks = []
                for qs in range(2):
                    psc = PS_e if qs == 0 else PS_o
                    nzc = MASK_NZ[(var, qs)]
                    P.emit("tensor", lambda e, qbase=qbase, qs=qs, wbase=wbase,
                           psc=psc: e.matmul(
                        psc[:, 0:512],
                        q_fm[:, qbase + qs * 128:qbase + (qs + 1) * 128],
                        k_fm[:, wbase:wbase + 512],
                        start=True, stop=False),
                        waits=[qk_done[0], qk_done[1], ps_free[qs], tok_pad,
                               tok_mask], sig="tensor")
                    mlast = P.emit("tensor", lambda e, qbase=qbase, qs=qs,
                                   wbase=wbase, psc=psc: e.matmul(
                        psc[:, 512:768],
                        q_fm[:, qbase + qs * 128:qbase + (qs + 1) * 128],
                        k_fm[:, wbase + 512:wbase + 768],
                        start=True, stop=False), sig="tensor")
                    for kc in nzc:
                        mlast = P.emit("tensor", lambda e, var=var, qs=qs,
                                       kc=kc, psc=psc: e.matmul(
                            psc[:, kc * 128:(kc + 1) * 128], ident,
                            masks[:, (var * 2 + qs) * 768 + kc * 128:
                                  (var * 2 + qs) * 768 + (kc + 1) * 128],
                            start=False, stop=True), sig="tensor")
                    col = par * 2 + qs
                    t1 = P.emit("vector", lambda e, col=col, psc=psc: e.tensor_reduce(
                        rm_c[:, col:col + 1], psc[:, :], AX.X, ALU.max),
                        waits=[mlast], sig="vector")
                    t2n = P.emit("vector", lambda e, col=col: e.tensor_scalar(
                        rm_c[:, col:col + 1], rm_c[:, col:col + 1], -SCALE,
                        None, ALU.mult), waits=[t1], sig="vector")
                    t3 = P.emit("scalar", lambda e, par=par, qs=qs, col=col,
                                psc=psc: e.activation(
                        p_sb[par][:, qs * 768:(qs + 1) * 768], psc[:, :],
                        AF.Exp, bias=rm_c[:, col:col + 1], scale=f32(SCALE),
                        accum_out=sums_c[:, col:col + 1]),
                        waits=[t2n], sig="scalar")
                    exp_toks.append(t3)
                    ps_free[qs] = t3
                tln = P.emit("scalar", lambda e, par=par: e.activation(
                    recip_c[:, par * 2:par * 2 + 2], sums_c[:, par * 2:par * 2 + 2],
                    AF.Ln), waits=exp_toks, sig="scalar")
                t5 = P.emit("scalar", lambda e, par=par: e.activation(
                    recip_c[:, par * 2:par * 2 + 2], recip_c[:, par * 2:par * 2 + 2],
                    AF.Exp, scale=f32(-1.0)), waits=[tln], sig="scalar")
                pn_toks = []
                for qs in range(2):
                    pn_toks.append(P.emit("vector", lambda e, par=par, qs=qs: e.tensor_scalar(
                        p_sb[par][:, qs * 768:(qs + 1) * 768],
                        p_sb[par][:, qs * 768:(qs + 1) * 768],
                        recip_c[:, par * 2 + qs:par * 2 + qs + 1], None,
                        ALU.mult), waits=[t5], sig="vector"))
                if pend is not None:
                    emit_tail(pend)
                pend = (par, qbase, vblk, pn_toks)
                idx += 1
        emit_tail(pend)
        pend = None
        attn_prev = P.join("scalar", tails_done[-2:] +
                           [ps_free[0], ps_free[1]])
    tok_attn = attn_prev

    if level <= 4:
        fins = []
        for h in range(NH if level >= 4 else 1):
            fins.append(P.emit("gpsimd", lambda e, h=h: e.dma_start(
                out=out_d[h * 512:(h + 1) * 512, :].rearrange(
                    "(p f) o -> p (f o)", p=128),
                in_=o_fm[h][:, :]), waits=[tok_attn], sig="dDmp", inc=16))
        P.join("sync", fins)
        P.replay()
        return nc

    st_p1b.close()
    st_p1a.close()
    st_x1fm.close()
    P.barrier()

    st_p1c = ExitStack()

    def sbc(name, shape, dt):
        return st_p1c.enter_context(
            nc.sbuf_tensor(name, shape, dt, side="right")).ap()

    gam_b = sbc("gam_b", [128, D], F32)
    bet_b = sbc("bet_b", [128, D], F32)
    b2_b = sbc("b2_b", [128, D], F32)
    bo_b = sbc("bo_b", [128, D], F32)
    Wo_s = sbc("Wo_s", [128, DC * D], BF16)
    w1b = [sbc(f"w1b{i}", [128, D], BF16) for i in range(2)]
    w2b = [sbc(f"w2b{i}", [128, 512], BF16) for i in range(2)]
    h_fm = [sbc(f"h_fm{i}", [128, (FC // 2) * TG], BF16) for i in range(2)]
    x0c = sbc("x0c", [128, 4 * D], F32)
    t1c = sbc("t1c", [128, D], F32)
    xhc = sbc("xhc", [128, D], F32)
    xhbf = sbc("xhbf", [128, D], BF16)
    t2tm = sbc("t2tm", [128, 4 * D], F32)
    t2fm = sbc("t2fm", [128, DC * TG], BF16)
    t2_3d = t2fm.rearrange("p (c t) -> p c t", t=TG)
    w2e = sbc("w2e", [128, 512], F32)

    bct = None
    rows_ld = []
    for i, src_d in enumerate((gamma_d, beta_d, b2_d, bo_row_d)):
        rows_ld.append(P.emit("sync", lambda e, i=i, src_d=src_d: e.dma_start(
            out=x0c[0:1, i * D:(i + 1) * D], in_=src_d), sig="dRow", inc=16))
    for i, dst in enumerate((gam_b, bet_b, b2_b, bo_b)):
        for half in range(2):
            m = P.emit("tensor", lambda e, i=i, half=half: e.matmul(
                PA[:, :], ones_r[:, :],
                x0c[0:1, i * D + half * 512:i * D + (half + 1) * 512],
                start=True, stop=True),
                waits=[tok_consts, rows_ld[i], bct], sig="tensor")
            bct = P.emit("vector", lambda e, dst=dst, half=half: e.tensor_copy(
                dst[:, half * 512:(half + 1) * 512], PA[:, :]),
                waits=[m], sig="vector")
    tok_bcast = bct

    wo_tok = None
    for ccc in range(DC):
        wo_tok = P.emit("gpsimd", lambda e, ccc=ccc: e.dma_start(
            out=Wo_s[:, ccc * D:(ccc + 1) * D],
            in_=Wo_d[ccc * 128:(ccc + 1) * 128, :]), sig="dWoL", inc=16)

    evt = {"PA": tok_bcast, "PB": None, "PC": None, "PD": None}
    x0_free = [None] * 4
    t1_free = None
    xh_free = None
    t2fm_evac = None
    h_free = None
    w1_free = [None, None]
    w2_free = [None, None]
    war_i = None
    war_2 = None
    x1_st = [None] * 4

    for g in range(NG):
        t2_done = []
        x0_ld2 = [None] * 4
        x0acc = [None] * 4
        for ti in range(4):
            tt = g * 4 + ti
            bx = ti % 2
            x0_ld2[ti] = P.emit("sync", lambda e, tt=tt, ti=ti: e.dma_start(
                out=x0c[:, ti * D:(ti + 1) * D],
                in_=x0_d[tt * 128:(tt + 1) * 128, :]),
                waits=[x0_free[ti], tok_bcast], sig=f"dxc{ti}", inc=16)
            for half, pname, pp in ((0, "PA", PA), (1, "PB", PB)):
                mm = None
                for hc in range(DC):
                    mm = P.emit("tensor", lambda e, tt=tt, hc=hc, half=half,
                                pp=pp: e.matmul(
                        pp[:, :],
                        o_fm[hc][:, tt * 128:(tt + 1) * 128],
                        Wo_s[:, hc * D + half * 512:hc * D + (half + 1) * 512],
                        start=(hc == 0), stop=(hc == DC - 1)),
                        waits=[wo_tok, tok_attn, evt[pname], tok_bcast],
                        sig="tensor")
                evt[pname] = P.emit("vector", lambda e, half=half,
                                    pp=pp: e.tensor_tensor(
                    t1c[:, half * 512:(half + 1) * 512], pp[:, :],
                    bo_b[:, half * 512:(half + 1) * 512], ALU.add),
                    waits=[mm, t1_free], sig="vector")
            ta = P.emit("scalar", lambda e, tt=tt, ti=ti: e.activation(
                xhc[:], x0c[:, ti * D:(ti + 1) * D], AF.Identity,
                bias=nmr1[:, tt:tt + 1], scale=r1[:, tt:tt + 1]),
                waits=[x0_ld2[ti], xh_free], sig="scalar")
            tb = P.emit("vector", lambda e: e.tensor_tensor(
                t1c[:], t1c[:], xhc[:], ALU.add),
                waits=[ta, evt["PA"], evt["PB"]], sig="vector")
            if level <= 5 and g == 7:
                P.emit("gpsimd", lambda e, ti=ti: e.dma_start(
                    out=dbg_scr[ti * 128:(ti + 1) * 128, :], in_=t1c[:]),
                    waits=[tb], sig="dDbg", inc=16)
            war_i = ln_stats(t1c[:], D, lni[:, 0:1], lni[:, 1:2],
                             lni[:, 2:3], stg1,
                             [tb, ("dDbg", P.cnt.get("dDbg", 0))], war_i)
            tc = P.emit("scalar", lambda e: e.activation(
                xhc[:], t1c[:], AF.Identity,
                bias=lni[:, 2:3], scale=lni[:, 1:2]), waits=[war_i],
                sig="scalar")
            t1_free = tc
            td = P.emit("vector", lambda e, ti=ti: e.tensor_tensor(
                t2tm[:, ti * D:(ti + 1) * D], xhc[:], gam_b[:], ALU.mult),
                waits=[tc], sig="vector")
            te = P.emit("vector", lambda e, ti=ti: e.tensor_tensor(
                t2tm[:, ti * D:(ti + 1) * D], t2tm[:, ti * D:(ti + 1) * D],
                bet_b[:], ALU.add), sig="vector")
            xh_free = td
            if level <= 5 and g == 7:
                P.emit("gpsimd", lambda e, ti=ti: e.dma_start(
                    out=dbg_scr[512 + ti * 128:512 + (ti + 1) * 128, :],
                    in_=t2tm[:, ti * D:(ti + 1) * D]), waits=[te],
                    sig="dDbg", inc=16)
            tf = P.emit("scalar", lambda e, ti=ti: e.copy(
                xhbf[:], t2tm[:, ti * D:(ti + 1) * D]),
                waits=[te], sig="scalar")
            tp = None
            for j in range(DC):
                tp = P.emit("tensor", lambda e, j=j: e.transpose(
                    PT[:, j * 128:(j + 1) * 128], xhbf[:, j * 128:(j + 1) * 128],
                    ident), waits=[tf, t2fm_evac], sig="tensor")
            t2fm_evac = P.emit("scalar", lambda e, ti=ti: e.copy(
                t2_3d[:, :, ti * 128:(ti + 1) * 128],
                PT[:].rearrange("p (c t) -> p c t", t=128)),
                waits=[tp], sig="scalar")
            t2_done.append(t2fm_evac)
            x0acc[ti] = P.emit("vector", lambda e, ti=ti: e.tensor_tensor(
                x0c[:, ti * D:(ti + 1) * D], x0c[:, ti * D:(ti + 1) * D],
                t2tm[:, ti * D:(ti + 1) * D], ALU.add),
                waits=[te, ta, x0_ld2[ti], ("dDbg", P.cnt.get("dDbg", 0))],
                sig="vector")

        for oc in range(FC):
            bw = oc % 2
            w1_ld = P.emit("sync", lambda e, oc=oc, bw=bw: e.dma_start(
                out=w1b[bw][:], in_=w1_scr[oc]),
                waits=[w1_free[bw], tok_w12], sig=f"dw1b{bw}", inc=16)
            pn = ("PC", PC) if oc % 2 == 0 else ("PD", PD)
            mm = None
            for fc in range(DC):
                mm = P.emit("tensor", lambda e, fc=fc, pn=pn, bw=bw: e.matmul(
                    pn[1][:, :], w1b[bw][:, fc * 128:(fc + 1) * 128],
                    t2fm[:, fc * TG:(fc + 1) * TG],
                    start=(fc == 0), stop=(fc == DC - 1)),
                    waits=[w1_ld, t2_done[-1], evt[pn[0]], h_free], sig="tensor")
            w1_free[bw] = mm
            evt[pn[0]] = P.emit("scalar", lambda e, oc=oc, pn=pn: e.activation(
                h_fm[oc // 16][:, (oc % 16) * TG:(oc % 16 + 1) * TG],
                pn[1][:, :], AF.Gelu, bias=b1_s[:, oc:oc + 1]),
                waits=[mm], sig="scalar")
        h_done = (evt["PC"], evt["PD"])

        PX = (("PC", PC), ("PD", PD), ("PA", PA), ("PB", PB))
        x0add = [None] * 4
        for oh in range(2):
            mm_t = [None] * 4
            for fc in range(FC):
                bw = fc % 2
                w2_ld = P.emit("sync", lambda e, fc=fc, oh=oh, bw=bw: e.dma_start(
                    out=w2b[bw][:],
                    in_=w2_scr[fc * 128:(fc + 1) * 128, oh * 512:(oh + 1) * 512]),
                    waits=[w2_free[bw], tok_w12], sig=f"dw2b{bw}", inc=16)
                for ti in range(4):
                    mm_t[ti] = P.emit("tensor", lambda e, fc=fc, ti=ti, oh=oh,
                                      bw=bw, PXi=PX[ti][1]: e.matmul(
                        PXi[:, :],
                        h_fm[fc // 16][:, (fc % 16) * TG + ti * 128:
                                       (fc % 16) * TG + (ti + 1) * 128],
                        w2b[bw][:],
                        start=(fc == 0), stop=(fc == FC - 1)),
                        waits=[w2_ld, h_done[0], h_done[1],
                               evt[PX[ti][0]] if fc == 0 else None],
                        sig="tensor")
                w2_free[bw] = mm_t[3]
            for ti in range(4):
                ew = P.emit("vector", lambda e, ti=ti, oh=oh, PXi=PX[ti][1]: e.tensor_tensor(
                    x0c[:, ti * D + oh * 512:ti * D + (oh + 1) * 512],
                    x0c[:, ti * D + oh * 512:ti * D + (oh + 1) * 512],
                    PXi[:, :], ALU.add), waits=[mm_t[ti], x0acc[ti]], sig="vector")
                evt[PX[ti][0]] = ew
                x0add[ti] = P.emit("vector", lambda e, ti=ti, oh=oh: e.tensor_tensor(
                    x0c[:, ti * D + oh * 512:ti * D + (oh + 1) * 512],
                    x0c[:, ti * D + oh * 512:ti * D + (oh + 1) * 512],
                    b2_b[:, oh * 512:(oh + 1) * 512], ALU.add),
                    waits=[ew], sig="vector")
        h_free = w2_free[1]
        for ti in range(4):
            tt = g * 4 + ti
            tg2 = x0add[ti]
            war_2 = ln_stats(x0c[:, ti * D:(ti + 1) * D], D, mu2[:, tt:tt + 1],
                             r2[:, tt:tt + 1], nmr2[:, tt:tt + 1], stg2,
                             [tg2], war_2)
            tmx = P.emit("vector", lambda e, ti=ti: e.tensor_reduce(
                gtmp[:, 0:1], x0c[:, ti * D:(ti + 1) * D], AX.X, ALU.max),
                waits=[tg2], sig="vector")
            tmn = P.emit("vector", lambda e, ti=ti: e.tensor_reduce(
                gtmp[:, 1:2], x0c[:, ti * D:(ti + 1) * D], AX.X, ALU.min),
                sig="vector")
            u1 = P.emit("vector", lambda e, tt=tt: e.tensor_tensor(
                gtmp[:, 0:1], gtmp[:, 0:1], mu2[:, tt:tt + 1], ALU.subtract),
                waits=[war_2, tmx], sig="vector")
            u2 = P.emit("vector", lambda e, tt=tt: e.tensor_tensor(
                gtmp[:, 1:2], mu2[:, tt:tt + 1], gtmp[:, 1:2], ALU.subtract),
                waits=[tmn], sig="vector")
            u3 = P.emit("vector", lambda e: e.tensor_tensor(
                gtmp[:, 0:1], gtmp[:, 0:1], gtmp[:, 1:2], ALU.max),
                waits=[u1, u2], sig="vector")
            u4 = P.emit("vector", lambda e, tt=tt: e.tensor_tensor(
                gtmp[:, 0:1], gtmp[:, 0:1], r2[:, tt:tt + 1], ALU.mult),
                waits=[u3], sig="vector")
            u5 = P.emit("vector", lambda e: e.tensor_tensor(
                gmv[:], gmv[:], gtmp[:, 0:1], ALU.max),
                waits=[t_gm0, u4], sig="vector")
            js = P.join("scalar", [u5, tg2])
            x1_st[ti] = P.emit("gpsimd", lambda e, tt=tt, ti=ti: e.dma_start(
                out=x1_scr[tt * 128:(tt + 1) * 128, :],
                in_=x0c[:, ti * D:(ti + 1) * D]),
                waits=[u5, tg2, js], sig=f"dx1s{ti}", inc=16)
            x0_free[ti] = x1_st[ti]
            if level <= 5 and g == 7:
                x0_free[ti] = P.emit("sync", lambda e, ti=ti: e.dma_start(
                    out=dbg_scr[ti * 128:(ti + 1) * 128, :],
                    in_=x0c[:, ti * D:(ti + 1) * D]),
                    waits=[x1_st[ti]], sig="dDbg2", inc=16)

    tok_x1 = P.join("sync", x1_st)

    if level <= 5:
        fin = P.emit("sync", lambda e: e.dma_start(
            out=out_d, in_=x1_scr), waits=[tok_x1], sig="dDmp", inc=16)
        P.join("sync", [fin])
        P.replay()
        return nc

    st_p1c.close()
    st_ofm.close()
    st_setup.close()
    P.barrier()

    st_wq = ExitStack()
    wdT = st_wq.enter_context(
        nc.sbuf_tensor("wdT", [128, IC * D], BF16, side="left")).ap()
    wgT = [st_wq.enter_context(
        nc.sbuf_tensor(f"wgT{i}", [128, II], BF16, side="left")).ap()
        for i in range(DC)]
    wuT = [st_wq.enter_context(
        nc.sbuf_tensor(f"wuT{i}", [128, II], BF16, side="left")).ap()
        for i in range(DC)]
    st_wql = ExitStack()
    wqld0 = st_wql.enter_context(
        nc.sbuf_tensor("wqld0", [128, II], F32, side="right")).ap()
    wqq0 = st_wql.enter_context(
        nc.sbuf_tensor("wqq0", [128, II], BF16, side="right")).ap()
    wqi0 = st_wql.enter_context(
        nc.sbuf_tensor("wqi0", [128, 1024], mybir.dt.int32, side="right")).ap()
    wq_ld = [wqld0, wqld0]
    wq_q = [wqq0, wqq0]
    wqi = [wqi0, wqi0]
    qw_ld_tok = [None] * 2
    qw_q_tok = [None] * 2

    def quant_weight(src_d, n_tiles, width, dstf, pt_prev, PTX=PT, slots=2):
        ld_tok = qw_ld_tok
        q_tok = qw_q_tok
        tp_done = pt_prev
        for t in range(n_tiles):
            b = 0
            ld_tok[b] = P.emit("sync", lambda e, t=t, b=b: e.dma_start(
                out=wq_ld[b][:, 0:width], in_=src_d[t * 128:(t + 1) * 128, :]),
                waits=[q_tok[b], tp_done], sig=f"dQl{b}", inc=16)
            t1 = P.emit("vector", lambda e, b=b, width=width: e.tensor_reduce(
                rmx[:, b:b + 1], wq_ld[b][:, 0:width], AX.X, ALU.max,
                apply_absolute_value=True), waits=[ld_tok[b]], sig="vector")
            t2 = P.emit("scalar", lambda e, b=b: e.activation(
                rsc[:, b:b + 1], rmx[:, b:b + 1], AF.Ln), waits=[t1],
                sig="scalar")
            t3 = P.emit("scalar", lambda e, b=b: e.activation(
                rsc[:, b:b + 1], rsc[:, b:b + 1], AF.Exp,
                scale=f32(-1.0), bias=f32(LN127)), waits=[t2], sig="scalar")
            t4 = P.emit("vector", lambda e, b=b: e.tensor_scalar(
                ssc[:, b:b + 1], rmx[:, b:b + 1], 1.0 / 127.0, None, ALU.mult),
                waits=[t1], sig="vector")
            P.emit("vector", lambda e, b=b, width=width: e.tensor_scalar(
                wq_ld[b][:, 0:width], wq_ld[b][:, 0:width],
                rsc[:, b:b + 1], -(QMAX + 1.0), ALU.mult, ALU.max),
                waits=[t3], sig="vector")
            for c0 in range(0, width, 1024):
                cw = min(1024, width - c0)
                P.emit("vector", lambda e, b=b, c0=c0, cw=cw: e.tensor_scalar(
                    wqi[b][:, 0:cw], wq_ld[b][:, c0:c0 + cw],
                    QMAX, None, ALU.min), sig="vector")
                q_tok[b] = P.emit("vector", lambda e, b=b, c0=c0,
                                  cw=cw: e.tensor_scalar(
                    wq_q[b][:, c0:c0 + cw], wqi[b][:, 0:cw],
                    ssc[:, b:b + 1], None, ALU.mult),
                    waits=[t4], sig="vector")
            nj = width // 128
            for j0 in range(0, nj, 8):
                jn = min(8, nj - j0)
                tp = None
                for j in range(jn):
                    tp = P.emit("tensor", lambda e, b=b, j0=j0, j=j,
                                PTX=PTX: e.transpose(
                        PTX[:, j * 128:(j + 1) * 128],
                        wq_q[b][:, (j0 + j) * 128:(j0 + j + 1) * 128], ident),
                        waits=[q_tok[b], tp_done], sig="tensor")
                for j in range(jn):
                    tp_done = P.emit("scalar", lambda e, t=t, j0=j0, j=j,
                                     dstf=dstf, PTX=PTX: e.copy(
                        dstf(j0 + j)[:, t * 128:(t + 1) * 128],
                        PTX[:, j * 128:(j + 1) * 128]),
                        waits=[tp], sig="scalar")
        return tp_done

    wd3 = wdT.rearrange("p (c i) -> p c i", i=D)
    tok_wg = quant_weight(wg_d, IC, D, lambda c: wgT[c], None)
    tok_wu = quant_weight(wu_d, IC, D, lambda c: wuT[c], tok_wg)

    t1 = P.emit("gpsimd", lambda e: e.dma_start(
        out=gmx_scr, in_=gmv[:]), waits=[tok_x1], sig="dGm1", inc=16)
    t2 = P.emit("sync", lambda e: e.dma_start(
        out=grow[:], in_=gmx_scr.rearrange("p one -> one p")),
        waits=[t1], sig="dGr1", inc=16)
    t3 = P.emit("vector", lambda e: e.tensor_reduce(
        srow[:, 0:1], grow[:], AX.X, ALU.max), waits=[t2], sig="vector")
    t4 = P.emit("gpsimd", lambda e: e.dma_start(out=cc1_in, in_=srow[:, 0:1]),
                waits=[t3], sig="dCi1", inc=16)
    t5 = P.emit("gpsimd", lambda e: e.collective_compute(
        "AllReduce", ALU.max, replica_groups=RG8, ins=[cc1_in], outs=[cc1_out]),
        waits=[t4], sig="pool")
    t6 = P.emit("sync", lambda e: e.dma_start(out=srow[:, 0:1], in_=cc1_out),
                waits=[t5], sig="dCo1", inc=16)
    t7 = P.emit("scalar", lambda e: e.activation(
        srow[:, 1:2], clip_s[0:1, 0:1], AF.Sigmoid),
        waits=[tok_consts, t6], sig="scalar")
    t8 = P.emit("vector", lambda e: e.tensor_tensor(
        srow[:, 2:3], srow[:, 0:1], srow[:, 1:2], ALU.mult),
        waits=[t7], sig="vector")
    t9 = P.emit("vector", lambda e: e.tensor_scalar(
        srow[:, 2:3], srow[:, 2:3], 1.0 / 127.0, None, ALU.mult),
        waits=[t8], sig="vector")
    t10 = P.emit("scalar", lambda e: e.activation(
        srow[:, 3:4], srow[:, 2:3], AF.Ln), waits=[t9], sig="scalar")
    t11 = P.emit("scalar", lambda e: e.activation(
        srow[:, 3:4], srow[:, 3:4], AF.Exp, scale=f32(-1.0)), waits=[t10],
        sig="scalar")
    t12 = P.emit("tensor", lambda e: e.matmul(
        PB3[:, 0:2], ones_r[:, :], srow[:, 2:4], start=True, stop=True),
        waits=[t11], sig="tensor")
    tok_s1 = P.emit("vector", lambda e: e.tensor_copy(qsc[:, 0:2], PB3[:, 0:2]),
                    waits=[t12], sig="vector")
    tqa = P.emit("vector", lambda e: e.tensor_scalar(
        qa[:], r2[:], qsc[:, 1:2], None, ALU.mult), waits=[tok_s1], sig="vector")
    tqb = P.emit("vector", lambda e: e.tensor_tensor(
        qb[:], mu2[:], qa[:], ALU.mult), waits=[tqa], sig="vector")
    tok_qc = P.emit("vector", lambda e: e.tensor_scalar(
        qb[:], qb[:], -1.0, None, ALU.mult), waits=[tqb], sig="vector")

    if level <= 6:
        fin = P.emit("sync", lambda e: e.dma_start(
            out=out_d[0:128, 0:4], in_=qsc[:]), waits=[tok_qc, tok_wu],
            sig="dDmp", inc=16)
        P.join("sync", [fin])
        P.replay()
        return nc

    st_p2 = ExitStack()

    def sb2(name, shape, dt):
        return st_p2.enter_context(
            nc.sbuf_tensor(name, shape, dt, side="right")).ap()

    x1b = [sb2(f"x1b{i}", [128, D], F32) for i in range(2)]
    xqb = [sb2(f"xqb{i}", [128, D], F32) for i in range(2)]
    xqbf = [sb2(f"xqbf{i}", [128, D], BF16) for i in range(2)]
    xqi = [sb2(f"xqi{i}", [128, D], mybir.dt.int32) for i in range(2)]
    xq_fm = sb2("xq_fm", [128, DC * TG], BF16)
    xq_3d = xq_fm.rearrange("p (c t) -> p c t", t=TG)
    sg_f = [sb2(f"sg_f{i}", [128, TG], F32) for i in range(2)]
    up_f = [sb2(f"up_f{i}", [128, TG], F32) for i in range(2)]
    ha_bf = [sb2(f"ha_bf{i}", [128, TG], BF16) for i in range(2)]

    x1_ld = [None, None]
    xq_free = None
    fm_ev = None
    xq4_free = [None, None]
    ev2 = {"PA": None, "PB": None, "PC": None, "PD": None}
    ha_free = [None, None]
    q_free = [None, None]

    for g in range(NG):
        fm_toks = []
        for ti in range(4):
            tt = g * 4 + ti
            bx = ti % 2
            x1_ld[bx] = P.emit("sync", lambda e, tt=tt, bx=bx: e.dma_start(
                out=x1b[bx][:], in_=x1_scr[tt * 128:(tt + 1) * 128, :]),
                waits=[q_free[bx], tok_x1], sig=f"dx1b{bx}", inc=16)
            q1 = P.emit("scalar", lambda e, tt=tt, bx=bx: e.activation(
                xqb[bx][:], x1b[bx][:], AF.Identity,
                bias=qb[:, tt:tt + 1], scale=qa[:, tt:tt + 1]),
                waits=[x1_ld[bx], tok_qc, q_free[bx]], sig="scalar")
            q_free[bx] = q1
            P.emit("vector", lambda e, bx=bx: e.tensor_scalar(
                xqb[bx][:], xqb[bx][:], -(QMAX + 1.0), QMAX, ALU.max, ALU.min),
                waits=[q1], sig="vector")
            P.emit("vector", lambda e, bx=bx: e.tensor_scalar(
                xqi[bx][:], xqb[bx][:], 0.0, None, ALU.add),
                waits=[xq4_free[bx]], sig="vector")
            q3c = P.join("vector", [])
            q4 = P.emit("scalar", lambda e, bx=bx: e.activation(
                xqbf[bx][:], xqi[bx][:], AF.Copy, scale=qsc[:, 0:1]),
                waits=[q3c, xq4_free[bx]], sig="scalar")
            xq4_free[bx] = q4
            tp = None
            for j in range(DC):
                tp = P.emit("tensor", lambda e, j=j, bx=bx: e.transpose(
                    PT[:, j * 128:(j + 1) * 128], xqbf[bx][:, j * 128:(j + 1) * 128],
                    ident), waits=[q4, xq_free, fm_ev], sig="tensor")
            fm_ev = P.emit("scalar", lambda e, ti=ti: e.copy(
                xq_3d[:, :, ti * 128:(ti + 1) * 128],
                PT[:].rearrange("p (c t) -> p c t", t=128)),
                waits=[tp], sig="scalar")
            fm_toks.append(fm_ev)
        tok_xq = fm_toks[-1]
        mmu = None
        for icx in range(IC):
            bh = icx % 2
            pg = ("PA", PA) if icx % 2 == 0 else ("PB", PB)
            pu = ("PC", PC) if icx % 2 == 0 else ("PD", PD)
            mmg = None
            for fc in range(DC):
                mmg = P.emit("tensor", lambda e, icx=icx, fc=fc, pg=pg: e.matmul(
                    pg[1][:, :], wgT[fc][:, icx * 128:(icx + 1) * 128],
                    xq_fm[:, fc * TG:(fc + 1) * TG],
                    start=(fc == 0), stop=(fc == DC - 1)),
                    waits=[tok_xq, tok_wg, ev2[pg[0]]], sig="tensor")
            for fc in range(DC):
                mmu = P.emit("tensor", lambda e, icx=icx, fc=fc, pu=pu: e.matmul(
                    pu[1][:, :], wuT[fc][:, icx * 128:(icx + 1) * 128],
                    xq_fm[:, fc * TG:(fc + 1) * TG],
                    start=(fc == 0), stop=(fc == DC - 1)),
                    waits=[tok_wu, ev2[pu[0]]], sig="tensor")
            e1 = P.emit("scalar", lambda e, bh=bh, pg=pg: e.activation(
                sg_f[bh][:], pg[1][:, :], AF.Silu),
                waits=[mmg, ha_free[bh]], sig="scalar")
            ev2[pg[0]] = e1
            e2 = P.emit("scalar", lambda e, bh=bh, pu=pu: e.activation(
                up_f[bh][:], pu[1][:, :], AF.Copy),
                waits=[mmu, ha_free[bh]], sig="scalar")
            ev2[pu[0]] = e2
            e3 = P.emit("vector", lambda e, bh=bh: e.tensor_tensor(
                ha_bf[bh][:], sg_f[bh][:], up_f[bh][:], ALU.mult),
                waits=[e1, e2], sig="vector")
            if level <= 7 and g == 0 and icx == 0:
                P.emit("gpsimd", lambda e, bh=bh: e.dma_start(
                    out=dbg_scr[512:640, :].rearrange(
                        "p o -> p o")[:, 0:TG], in_=ha_bf[bh][:]),
                    waits=[e3], sig="dDbg", inc=16)
                P.emit("gpsimd", lambda e, bh=bh: e.dma_start(
                    out=dbg_scr[640:768, 0:TG], in_=sg_f[bh][:]),
                    waits=[e3], sig="dDbg", inc=16)
                P.emit("gpsimd", lambda e, bh=bh: e.dma_start(
                    out=dbg_scr[768:896, 0:TG], in_=up_f[bh][:]),
                    waits=[e3], sig="dDbg", inc=16)
                P.emit("gpsimd", lambda e: e.dma_start(
                    out=dbg_scr[896:1024, :], in_=wgT[0][:, 0:D]),
                    waits=[tok_wg], sig="dDbg", inc=16)
            if level <= 7 and g == 0 and icx == 2:
                P.emit("gpsimd", lambda e: e.dma_start(
                    out=dbg_scr[0:512, :].rearrange("(a p) o -> p a o", p=128),
                    in_=xq_fm[:].rearrange("p (a o) -> p a o", o=1024)),
                    waits=[tok_xq], sig="dDbg", inc=16)
            e4 = P.emit("vector", lambda e, bh=bh: e.tensor_reduce(
                gtmp[:, 0:1], ha_bf[bh][:], AX.X, ALU.max,
                apply_absolute_value=True), waits=[e3], sig="vector")
            e5 = P.emit("vector", lambda e: e.tensor_tensor(
                gmv2[:], gmv2[:], gtmp[:, 0:1], ALU.max),
                waits=[t_gm20, e4], sig="vector")
            ha_free[bh] = P.emit("gpsimd", lambda e, icx=icx, g=g, bh=bh: e.dma_start(
                out=hact_scr[icx * 128:(icx + 1) * 128, g * TG:(g + 1) * TG],
                in_=ha_bf[bh][:]), waits=[e5], sig=f"dHa{bh}", inc=16)
        xq_free = mmu
    tok_wd = quant_weight(wd_d, DC, II, lambda c: wd3[:, c], tok_wu, PTX=PT2,
                          slots=1)
    tok_p2 = P.join("sync", [ha_free[0], ha_free[1]])

    st_p2.close()
    st_wql.close()
    P.barrier()

    t1 = P.emit("gpsimd", lambda e: e.dma_start(
        out=gmx_scr, in_=gmv2[:]), waits=[tok_p2], sig="dGm2", inc=16)
    t2 = P.emit("sync", lambda e: e.dma_start(
        out=grow[:], in_=gmx_scr.rearrange("p one -> one p")),
        waits=[t1], sig="dGr2", inc=16)
    t3 = P.emit("vector", lambda e: e.tensor_reduce(
        srow[:, 4:5], grow[:], AX.X, ALU.max), waits=[t2], sig="vector")
    t4 = P.emit("gpsimd", lambda e: e.dma_start(out=cc2_in, in_=srow[:, 4:5]),
                waits=[t3], sig="dCi2", inc=16)
    t5 = P.emit("gpsimd", lambda e: e.collective_compute(
        "AllReduce", ALU.max, replica_groups=RG8, ins=[cc2_in], outs=[cc2_out]),
        waits=[t4], sig="pool")
    t6 = P.emit("sync", lambda e: e.dma_start(out=srow[:, 4:5], in_=cc2_out),
                waits=[t5], sig="dCo2", inc=16)
    t7 = P.emit("scalar", lambda e: e.activation(
        srow[:, 5:6], clip_s[0:1, 1:2], AF.Sigmoid), waits=[t6], sig="scalar")
    t8 = P.emit("vector", lambda e: e.tensor_tensor(
        srow[:, 6:7], srow[:, 4:5], srow[:, 5:6], ALU.mult),
        waits=[t7], sig="vector")
    t9 = P.emit("vector", lambda e: e.tensor_scalar(
        srow[:, 6:7], srow[:, 6:7], 1.0 / 127.0, None, ALU.mult),
        waits=[t8], sig="vector")
    t10 = P.emit("scalar", lambda e: e.activation(
        srow[:, 7:8], srow[:, 6:7], AF.Ln), waits=[t9], sig="scalar")
    t11 = P.emit("scalar", lambda e: e.activation(
        srow[:, 7:8], srow[:, 7:8], AF.Exp, scale=f32(-1.0)), waits=[t10],
        sig="scalar")
    t12 = P.emit("tensor", lambda e: e.matmul(
        PB3[:, 0:2], ones_r[:, :], srow[:, 6:8], start=True, stop=True),
        waits=[t11], sig="tensor")
    tok_s2 = P.emit("vector", lambda e: e.tensor_copy(qsc[:, 2:4], PB3[:, 0:2]),
                    waits=[t12], sig="vector")

    if level <= 7:
        fin = P.emit("sync", lambda e: e.dma_start(
            out=out_d[0:128, 0:4], in_=qsc[:]), waits=[tok_s2],
            sig="dDmp", inc=16)
        P.join("sync", [fin])
        P.replay()
        return nc

    st_p3 = ExitStack()

    def sb3(name, shape, dt):
        return st_p3.enter_context(
            nc.sbuf_tensor(name, shape, dt, side="right")).ap()

    ha_ld = [sb3(f"ha_ld{i}", [128, TG], BF16) for i in range(2)]
    hq_fm = sb3("hq_fm", [128, IC * TG], BF16)
    x1p = [sb3(f"x1p{i}", [128, D], F32) for i in range(2)]
    otile = [sb3(f"otile{i}", [128, D], F32) for i in range(2)]
    qstg = sb3("qstg", [128, TG], F32)
    qsti = [sb3(f"qsti{i}", [128, TG], mybir.dt.int32) for i in range(2)]

    ha_tok = [None, None]
    ha_used = [None, None]
    hq3_free = [None, None]
    hq_free = None
    out_st = [None, None]
    ev3 = {"PA": None, "PB": None}
    x1_ld3 = [None, None]
    out_toks = []

    for g in range(NG):
        hq_done = []
        for icx in range(IC):
            bh = icx % 2
            ha_tok[bh] = P.emit("sync", lambda e, icx=icx, g=g, bh=bh: e.dma_start(
                out=ha_ld[bh][:],
                in_=hact_scr[icx * 128:(icx + 1) * 128, g * TG:(g + 1) * TG]),
                waits=[ha_used[bh], tok_p2], sig=f"dHl{bh}", inc=16)
            q1 = P.emit("vector", lambda e, bh=bh: e.tensor_scalar(
                qstg[:], ha_ld[bh][:], qsc[:, 3:4], -(QMAX + 1.0),
                ALU.mult, ALU.max),
                waits=[ha_tok[bh], tok_s2, hq_free, hq3_free[bh]], sig="vector")
            ha_used[bh] = q1
            q2v = P.emit("vector", lambda e, bh=bh: e.tensor_scalar(
                qsti[bh][:], qstg[:], QMAX, None, ALU.min), sig="vector")
            q3s = P.emit("scalar", lambda e, icx=icx, bh=bh: e.activation(
                hq_fm[:, icx * TG:(icx + 1) * TG], qsti[bh][:],
                AF.Copy, scale=qsc[:, 2:3]), waits=[q2v], sig="scalar")
            hq3_free[bh] = q3s
            hq_done.append(q3s)
        for ti in range(4):
            tt = g * 4 + ti
            bx = ti % 2
            x1_ld3[bx] = P.emit("sync", lambda e, tt=tt, bx=bx: e.dma_start(
                out=x1p[bx][:], in_=x1_scr[tt * 128:(tt + 1) * 128, :]),
                waits=[out_st[bx]], sig=f"dxp{bx}", inc=16)
            mm_l = None
            for oh, pname, pp in ((0, "PA", PA), (1, "PB", PB)):
                mm = None
                for icx in range(IC):
                    mm = P.emit("tensor", lambda e, icx=icx, ti=ti, oh=oh,
                                pp=pp: e.matmul(
                        pp[:, :],
                        hq_fm[:, icx * TG + ti * 128:icx * TG + (ti + 1) * 128],
                        wd3[:, icx, oh * 512:(oh + 1) * 512],
                        start=(icx == 0), stop=(icx == IC - 1)),
                        waits=[hq_done[icx], tok_wd, ev3[pname]], sig="tensor")
                mm_l = mm
                ev3[pname] = P.emit("scalar", lambda e, oh=oh, pp=pp,
                                    bx=bx: e.activation(
                    otile[bx][:, oh * 512:(oh + 1) * 512], pp[:, :], AF.Copy),
                    waits=[mm, out_st[bx]], sig="scalar")
            if ti == 3:
                hq_free = mm_l
            o1 = P.emit("vector", lambda e, bx=bx: e.tensor_tensor(
                otile[bx][:], otile[bx][:], x1p[bx][:], ALU.add),
                waits=[ev3["PA"], ev3["PB"], x1_ld3[bx]], sig="vector")
            out_st[bx] = P.emit("sync", lambda e, tt=tt, bx=bx: e.dma_start(
                out=out_d[tt * 128:(tt + 1) * 128, :], in_=otile[bx][:]),
                waits=[o1], sig=f"dOut{bx}", inc=16)
            out_toks.append(out_st[bx])

    P.join("sync", [out_toks[-1], out_toks[-2]])

    P.replay()
    st_wq.close()
    st_p3.close()
    st_ps.close()
    st_ofm_dummy = None
    st_perm.close()
    return nc


def _shard_inputs(inputs):
    x = np.asarray(inputs["L0_attn_x"])

    def lay(v, shape):
        return np.ascontiguousarray(np.asarray(v, np.float32).reshape(shape))

    def chunked(v, n):
        return np.ascontiguousarray(
            np.asarray(v, np.float32).reshape(n, 128).T)

    ins = {
        "Wq": lay(inputs["L0_attn_Wq"], (D, D)),
        "Wk": lay(inputs["L0_attn_Wk"], (D, D)),
        "Wv": lay(inputs["L0_attn_Wv"], (D, D)),
        "Wo": lay(inputs["L0_attn_Wo"], (D, D)),
        "bq": chunked(inputs["L0_attn_bq"], DC),
        "bk": chunked(inputs["L0_attn_bk"], DC),
        "bv": chunked(inputs["L0_attn_bv"], DC),
        "bo_row": lay(inputs["L0_attn_bo"], (1, D)),
        "gamma": lay(inputs["L0_attn_gamma"], (1, D)),
        "beta": lay(inputs["L0_attn_beta"], (1, D)),
        "W1": lay(inputs["L0_attn_W1"], (D, FF)),
        "b1": chunked(inputs["L0_attn_b1"], FC),
        "W2": lay(inputs["L0_attn_W2"], (FF, D)),
        "b2": lay(inputs["L0_attn_b2"], (1, D)),
        "clip_x": lay(inputs["L0_ffn_clip_x"], (1, 1)),
        "clip_down": lay(inputs["L0_ffn_clip_down"], (1, 1)),
        "w_gate": lay(inputs["L0_ffn_w_gate"], (II, D)),
        "w_up": lay(inputs["L0_ffn_w_up"], (II, D)),
        "w_down": lay(inputs["L0_ffn_w_down"], (D, II)),
    }
    in_maps = []
    for c in range(8):
        m = dict(ins)
        m["x0"] = np.ascontiguousarray(
            x[BLOC * c:BLOC * (c + 1)].reshape(T, D), np.float32)
        in_maps.append(m)
    return in_maps


def kernel(**inputs):
    nc = build_program()
    in_maps = _shard_inputs(inputs)
    res = run_bass_kernel_spmd(nc, in_maps, core_ids=list(range(8)))
    out = np.concatenate(
        [np.asarray(r["out"]).reshape(BLOC, SEQ, D) for r in res.results], axis=0)
    return out


if __name__ == "__main__":
    nc = build_program()
    print("built ok; instructions:", nc.n_instructions())


# version 19
# speedup vs baseline: 1.0038x; 1.0038x over previous
import numpy as np
import ml_dtypes
import concourse.bass as bass
import concourse.mybir as mybir
from concourse.bass_utils import run_bass_kernel_spmd
from contextlib import ExitStack

F32 = mybir.dt.float32
BF16 = mybir.dt.bfloat16
AF = mybir.ActivationFunctionType
ALU = mybir.AluOpType
AX = mybir.AxisListType
f32 = np.float32

D = 1024
FF = 4096
II = 2816
NH = 8
WIN = 256
SEQ = 2048
BLOC = 2
T = BLOC * SEQ
NT = T // 128
TG = 512
NG = T // TG
DC = D // 128
FC = FF // 128
IC = II // 128
NCH = SEQ // WIN
KPAD = WIN
TK = T + 2 * KPAD
SCALE = float(128.0) ** -0.5
EPS = 1e-5
QMAX = 127.0
MAGIC = float(f32(3 * 2 ** 22))
MASKV = -30000.0
LN127 = float(np.log(f32(127.0)))
RG8 = [[0, 1, 2, 3, 4, 5, 6, 7]]

ENGS = ("sync", "tensor", "vector", "scalar", "gpsimd")


def _build_masks():
    kr = np.arange(768)
    qi = np.arange(256)
    band = (kr[None, :] >= qi[:, None]) & (kr[None, :] <= qi[:, None] + 2 * WIN)
    masks = np.zeros((3, 2, 128, 768), np.float32)
    nonzero = {}
    for v in range(3):
        valid = band.copy()
        if v == 0:
            valid = valid & (kr[None, :] >= WIN)
        elif v == 2:
            valid = valid & (kr[None, :] < 2 * WIN)
        add = np.where(valid, 0.0, MASKV).astype(np.float32)
        for qs in range(2):
            masks[v, qs] = add[qs * 128:(qs + 1) * 128]
            nonzero[(v, qs)] = [
                kc for kc in range(6)
                if np.any(masks[v, qs][:, kc * 128:(kc + 1) * 128] != 0.0)]
    return masks.astype(ml_dtypes.bfloat16), nonzero


MASKS_NP, MASK_NZ = _build_masks()


class Prog:

    def __init__(self, nc):
        self.nc = nc
        self.ops = {e: [] for e in ENGS}
        self.sem_h = {}
        self.cnt = {}
        self.waited = {e: {} for e in ENGS}

    def _prep(self, eng, waits, sig, inc):
        nw = []
        for t in waits:
            if t is None:
                continue
            s, v = t
            if self.waited[eng].get(s, 0) >= v:
                continue
            nw.append((s, v))
            self.waited[eng][s] = v
        tok = None
        if sig is not None:
            self.cnt[sig] = self.cnt.get(sig, 0) + inc
            tok = (sig, self.cnt[sig])
        return tuple(nw), tok

    def emit(self, eng, fn, waits=(), sig=None, inc=1):
        nw, tok = self._prep(eng, waits, sig, inc)

        def closure(nw=nw, sig=sig, fn=fn, inc=inc, eng=eng):
            e = getattr(self.nc, eng)
            for (s, v) in nw:
                e.wait_ge(self.sem_h[s], v)
            inst = fn(e)
            if sig is not None:
                inst.then_inc(self.sem_h[sig], inc)

        self.ops[eng].append(closure)
        return tok

    def join(self, eng, waits, sig=None):
        if sig is None:
            sig = eng
        nw, tok = self._prep(eng, waits, sig, 1)

        def closure(nw=nw, sig=sig, eng=eng):
            e = getattr(self.nc, eng)
            for (s, v) in nw:
                e.wait_ge(self.sem_h[s], v)
            e.sem_inc(self.sem_h[sig], 1)

        self.ops[eng].append(closure)
        return tok

    def barrier(self, engs=ENGS):
        dma_toks = [(n, v) for n, v in self.cnt.items() if n.startswith("d")]
        toks = [self.join(e, dma_toks) for e in engs]
        for e in engs:
            nw, _ = self._prep(e, toks, None, 1)

            def closure(nw=nw, eng=e):
                en = getattr(self.nc, eng)
                for (s, v) in nw:
                    en.wait_ge(self.sem_h[s], v)

            self.ops[e].append(closure)

    def replay(self):
        with ExitStack() as st:
            for s in self.cnt:
                self.sem_h[s] = st.enter_context(self.nc.semaphore(s))
            for e in ENGS:
                for s, v in self.waited[e].items():
                    assert self.cnt.get(s, 0) >= v, (e, s, v, self.cnt.get(s))
            with self.nc.Block() as block:
                for e in ENGS:
                    getattr(block, e)(lambda _, e=e: [f() for f in self.ops[e]])


def build_program(level=99):
    nc = bass.Bass("TRN2", target_bir_lowering=False)
    P = Prog(nc)

    def din(name, shape):
        return nc.dram_tensor(name, shape, F32, kind="ExternalInput").ap()

    x0_d = din("x0", [T, D])
    Wq_d, Wk_d, Wv_d, Wo_d = (din(n, [D, D]) for n in ("Wq", "Wk", "Wv", "Wo"))
    bq_d, bk_d, bv_d = (din(n, [128, DC]) for n in ("bq", "bk", "bv"))
    gamma_d = din("gamma", [1, D])
    beta_d = din("beta", [1, D])
    W1_d = din("W1", [D, FF])
    b1_d = din("b1", [128, FC])
    W2_d = din("W2", [FF, D])
    b2_d = din("b2", [1, D])
    bo_row_d = din("bo_row", [1, D])
    clipx_d = din("clip_x", [1, 1])
    clipd_d = din("clip_down", [1, 1])
    wg_d = din("w_gate", [II, D])
    wu_d = din("w_up", [II, D])
    wd_d = din("w_down", [D, II])

    out_d = nc.dram_tensor("out", [T, D], F32, kind="ExternalOutput").ap()

    x1_scr = nc.dram_tensor("x1_scr", [T, D], F32, kind="Internal").ap()
    hact_scr = nc.dram_tensor("hact_scr", [II, T], BF16, kind="Internal").ap()
    w1_scr = nc.dram_tensor("w1_scr", [FC, 128, D], BF16, kind="Internal").ap()
    w2_scr = nc.dram_tensor("w2_scr", [FF, D], BF16, kind="Internal").ap()
    gmx_scr = nc.dram_tensor("gmx_scr", [128, 1], F32, kind="Internal").ap()
    dbg_scr = nc.dram_tensor(
        "dbg", [1024, D], F32,
        kind="ExternalOutput" if level <= 7 else "Internal").ap()
    cc1_in = nc.dram_tensor("cc1_in", [1, 1], F32, kind="Internal").ap()
    cc1_out = nc.dram_tensor("cc1_out", [1, 1], F32, kind="Internal",
                             addr_space="Shared").ap()
    cc2_in = nc.dram_tensor("cc2_in", [1, 1], F32, kind="Internal").ap()
    cc2_out = nc.dram_tensor("cc2_out", [1, 1], F32, kind="Internal",
                             addr_space="Shared").ap()

    ident_d = nc.inline_tensor(np.eye(128, dtype=ml_dtypes.bfloat16), "identc").ap()
    masks_d = nc.inline_tensor(
        np.ascontiguousarray(MASKS_NP.transpose(2, 0, 1, 3).reshape(128, 6 * 768)),
        "maskc").ap()
    ones_d = nc.inline_tensor(np.ones((1, 128), np.float32), "onesc").ap()

    st_perm = ExitStack()

    def sb(name, shape, dt):
        return st_perm.enter_context(
            nc.sbuf_tensor(name, shape, dt, side="left")).ap()

    ident = sb("ident", [128, 128], BF16)
    ones_r = sb("ones_r", [1, 128], F32)
    bq_s = sb("bq_s", [128, DC], F32)
    bk_s = sb("bk_s", [128, DC], F32)
    bv_s = sb("bv_s", [128, DC], F32)
    b1_s = sb("b1_s", [128, FC], F32)
    clip_s = sb("clip_s", [1, 2], F32)
    mu1 = sb("mu1", [128, NT], F32)
    r1 = sb("r1", [128, NT], F32)
    nmr1 = sb("nmr1", [128, NT], F32)
    mu2 = sb("mu2", [128, NT], F32)
    r2 = sb("r2", [128, NT], F32)
    nmr2 = sb("nmr2", [128, NT], F32)
    gmv = sb("gmv", [128, 1], F32)
    gmv2 = sb("gmv2", [128, 1], F32)
    stg1 = sb("stg1", [128, 20], F32)
    stg2 = sb("stg2", [128, 5], F32)
    lni = sb("lni", [128, 3], F32)
    gtmp = sb("gtmp", [128, 2], F32)
    srow = sb("srow", [1, 8], F32)
    qsc = sb("qsc", [128, 4], F32)
    scrap = sb("scrap", [128, 1024], BF16)
    sums_c = sb("sums_c", [128, 4], F32)
    recip_c = sb("recip_c", [128, 4], F32)
    rm_c = sb("rm_c", [128, 4], F32)
    grow = sb("grow", [1, 128], F32)
    qa = sb("qa", [128, NT], F32)
    qb = sb("qb", [128, NT], F32)
    rmx = sb("rmx", [128, 4], F32)
    rsc = sb("rsc", [128, 4], F32)
    ssc = sb("ssc", [128, 4], F32)

    st_ofm = ExitStack()
    o_fm = [st_ofm.enter_context(
        nc.sbuf_tensor(f"o_fm{h}", [128, T], BF16, side="left")).ap()
        for h in range(NH)]

    st_ps = ExitStack()

    def pst(name, shape, dt):
        return st_ps.enter_context(nc.psum_tensor(name, shape, dt)).ap()

    PB1 = pst("PB1", [128, 1024], F32)
    PB2 = pst("PB2", [128, 1024], F32)
    PB3 = pst("PB3", [128, 1024], F32)
    PT = pst("PTT", [128, 1024], BF16)
    PT2 = pst("PTT2", [128, 1024], BF16)
    PA = PB1[:, 0:512]
    PB = PB1[:, 512:1024]
    PC = PB2[:, 0:512]
    PD = PB2[:, 512:1024]
    PS_e = PB1[:, 0:768]
    PS_o = PB2[:, 0:768]
    PO2 = PB3[:, 0:256]

    for dst, src in ((ident, ident_d), (ones_r, ones_d),
                     (bq_s, bq_d), (bk_s, bk_d), (bv_s, bv_d), (b1_s, b1_d)):
        P.emit("sync", lambda e, d=dst, s=src: e.dma_start(out=d, in_=s),
               sig="dCst", inc=16)
    P.emit("sync", lambda e: e.dma_start(out=clip_s[:, 0:1], in_=clipx_d),
           sig="dCst", inc=16)
    tok_consts = P.emit("sync",
                        lambda e: e.dma_start(out=clip_s[:, 1:2], in_=clipd_d),
                        sig="dCst", inc=16)
    t_gm0 = P.emit("vector", lambda e: e.memset(gmv[:], 0.0), sig="vector")
    t_gm20 = P.emit("vector", lambda e: e.memset(gmv2[:], 0.0), sig="vector")

    st_setup = ExitStack()
    wtmp = [st_setup.enter_context(
        nc.sbuf_tensor(f"wtmp{i}", [128, D], BF16, side="right")).ap()
        for i in range(2)]
    wst = [None, None]
    for oc in range(FC):
        b = oc % 2
        tin = P.emit("gpsimd", lambda e, oc=oc, b=b: e.dma_start(
            out=wtmp[b].rearrange("p (c o) -> p c o", o=128),
            in_=W1_d.rearrange("(c p) o -> p c o", p=128)[:, :, oc * 128:(oc + 1) * 128],
            allow_slow_non_contiguous=True),
            waits=[wst[b]], sig=f"dWc{b}", inc=16)
        wst[b] = P.emit("gpsimd", lambda e, oc=oc, b=b: e.dma_start(
            out=w1_scr[oc], in_=wtmp[b][:]), waits=[tin], sig=f"dWs{b}", inc=16)
    for fc in range(FC):
        b = fc % 2
        tin = P.emit("gpsimd", lambda e, fc=fc, b=b: e.dma_start(
            out=wtmp[b][:], in_=W2_d[fc * 128:(fc + 1) * 128, :]),
            waits=[wst[b]], sig=f"dWc{b}", inc=16)
        wst[b] = P.emit("gpsimd", lambda e, fc=fc, b=b: e.dma_start(
            out=w2_scr[fc * 128:(fc + 1) * 128, :], in_=wtmp[b][:]),
            waits=[tin], sig=f"dWs{b}", inc=16)
    tok_w12 = P.join("sync", [wst[0], wst[1]])

    def ln_stats(xap, width, mu_ap, r_ap, nmr_ap, stg, wait_toks, war_tok):
        t1 = P.emit("vector", lambda e: e.tensor_reduce(
            stg[:, 0:1], xap, AX.X, ALU.add),
            waits=list(wait_toks) + [war_tok], sig="vector")
        t2 = P.emit("scalar", lambda e: e.activation(
            scrap[:, 0:width], xap, AF.Square, accum_out=stg[:, 1:2]),
            waits=list(wait_toks) + [war_tok], sig="scalar")
        t3 = P.emit("vector", lambda e: e.tensor_scalar(
            mu_ap, stg[:, 0:1], 1.0 / width, None, ALU.mult),
            waits=[t1], sig="vector")
        t4 = P.emit("vector", lambda e: e.tensor_scalar(
            stg[:, 2:3], stg[:, 1:2], 1.0 / width, None, ALU.mult),
            waits=[t2], sig="vector")
        t5 = P.emit("vector", lambda e: e.tensor_tensor(
            stg[:, 3:4], mu_ap, mu_ap, ALU.mult), waits=[t3], sig="vector")
        t6 = P.emit("vector", lambda e: e.tensor_tensor(
            stg[:, 2:3], stg[:, 2:3], stg[:, 3:4], ALU.subtract),
            waits=[t4, t5], sig="vector")
        t7 = P.emit("scalar", lambda e: e.activation(
            stg[:, 4:5], stg[:, 2:3], AF.Ln, bias=f32(EPS)),
            waits=[t6], sig="scalar")
        t8 = P.emit("scalar", lambda e: e.activation(
            r_ap, stg[:, 4:5], AF.Exp, scale=f32(-0.5)), waits=[t7],
            sig="scalar")
        t9 = P.emit("vector", lambda e: e.tensor_scalar(
            nmr_ap, mu_ap, r_ap, -1.0, ALU.mult, ALU.mult),
            waits=[t8, t3], sig="vector")
        return t9

    st_x1fm = ExitStack()
    xh1_fm = st_x1fm.enter_context(
        nc.sbuf_tensor("xh1_fm", [128, DC * T], BF16, side="right")).ap()
    xh1_3d = xh1_fm.rearrange("p (c t) -> p c t", t=T)
    st_p1a = ExitStack()
    x0b = [st_p1a.enter_context(
        nc.sbuf_tensor(f"x0b{i}", [128, D], F32, side="right")).ap()
        for i in range(2)]
    xhb = [st_p1a.enter_context(
        nc.sbuf_tensor(f"xhb{i}", [128, D], BF16, side="right")).ap()
        for i in range(2)]

    x0_ld = [None] * 2
    apply_tok = [None] * 2
    fm_evac = None
    war1 = None
    war1p = [None] * 2
    for tt in range(NT):
        b = tt % 2
        x0_ld[b] = P.emit("sync", lambda e, tt=tt, b=b: e.dma_start(
            out=x0b[b][:], in_=x0_d[tt * 128:(tt + 1) * 128, :]),
            waits=[apply_tok[b]], sig=f"dX{b}", inc=16)
        war1p[b] = ln_stats(x0b[b][:], D, mu1[:, tt:tt + 1], r1[:, tt:tt + 1],
                            nmr1[:, tt:tt + 1], stg1[:, b * 5:b * 5 + 5],
                            [x0_ld[b]], war1p[b])
        war1 = war1p[b]
        tap = P.emit("scalar", lambda e, tt=tt, b=b: e.activation(
            xhb[b][:], x0b[b][:], AF.Identity,
            bias=nmr1[:, tt:tt + 1], scale=r1[:, tt:tt + 1]),
            waits=[war1], sig="scalar")
        apply_tok[b] = tap
        tp = None
        for j in range(DC):
            tp = P.emit("tensor", lambda e, b=b, j=j: e.transpose(
                PT[:, j * 128:(j + 1) * 128], xhb[b][:, j * 128:(j + 1) * 128],
                ident), waits=[tap, fm_evac, tok_consts], sig="tensor")
        fm_evac = P.emit("scalar", lambda e, tt=tt: e.copy(
            xh1_3d[:, :, tt * 128:(tt + 1) * 128],
            PT[:].rearrange("p (c t) -> p c t", t=128)), waits=[tp], sig="scalar")
    tok_p1a = fm_evac

    if level <= 2:
        fins = []
        fins.append(P.emit("sync", lambda e: e.dma_start(
            out=out_d[0:128, 0:NT], in_=mu1[:]), waits=[tok_p1a], sig="dDmp", inc=16))
        fins.append(P.emit("sync", lambda e: e.dma_start(
            out=out_d[128:256, 0:NT], in_=r1[:]), waits=[tok_p1a], sig="dDmp", inc=16))
        fins.append(P.emit("sync", lambda e: e.dma_start(
            out=out_d[256:384, 0:NT], in_=nmr1[:]), waits=[tok_p1a], sig="dDmp", inc=16))
        fins.append(P.emit("gpsimd", lambda e: e.dma_start(
            out=out_d[384:896, :].rearrange("(p f) o -> p (f o)", p=128),
            in_=xh1_fm[:, 0:T]), waits=[tok_p1a], sig="dDmp", inc=16))
        P.join("sync", fins)
        P.replay()
        return nc

    st_p1b = ExitStack()

    def sbr(name, shape, dt):
        return st_p1b.enter_context(
            nc.sbuf_tensor(name, shape, dt, side="right")).ap()

    masks = sbr("masks", [128, 6 * 768], BF16)
    q_fm = sbr("q_fm", [128, T], BF16)
    k_fm = sbr("k_fm", [128, TK], BF16)
    v_fm = sbr("v_fm", [128, T], BF16)
    v_tm = sbr("v_tm", [128, TK], BF16)
    wsl = sbr("wsl", [128, 3 * D], BF16)
    p_sb = [sbr(f"p_sb{i}", [128, 2 * 768], BF16) for i in range(2)]
    pT_sb = sbr("pT_sb", [128, 2 * 768], BF16)

    P.emit("sync", lambda e: e.dma_start(out=masks, in_=masks_d), sig="dMsk", inc=16)
    P.emit("vector", lambda e: e.memset(k_fm[:, 0:KPAD], 0.0), sig="vector")
    P.emit("vector", lambda e: e.memset(k_fm[:, T + KPAD:TK], 0.0), sig="vector")
    P.emit("vector", lambda e: e.memset(v_tm[:, 0:KPAD], 0.0), sig="vector")
    P.emit("vector", lambda e: e.memset(pT_sb[:, 5 * 256:5 * 256 + 128], 0.0),
           sig="vector")
    P.emit("vector", lambda e: e.memset(pT_sb[:, 128:256], 0.0), sig="vector")
    tok_pad = P.emit("vector", lambda e: e.memset(v_tm[:, T + KPAD:TK], 0.0),
                     sig="vector")
    tok_mask = ("dMsk", P.cnt["dMsk"])

    wsl_free = None
    attn_prev = None
    ev = {}

    for h in range(NH if level >= 4 else 1):
        wtoks = []
        for wi, W_d in enumerate((Wq_d, Wk_d, Wv_d)):
            wtoks.append(P.emit("gpsimd", lambda e, wi=wi, W_d=W_d, h=h: e.dma_start(
                out=wsl.rearrange("p (w c o) -> p w c o", w=3, c=DC)[:, wi],
                in_=W_d.rearrange("(c p) o -> p c o", p=128)[
                    :, :, h * 128:(h + 1) * 128],
                allow_slow_non_contiguous=True),
                waits=[wsl_free], sig="dWsl", inc=16))
        wsl_ld = wtoks[-1]

        for wofs, dst, bias_s, base in ((0, q_fm, bq_s, 0),
                                        (D, k_fm, bk_s, KPAD),
                                        (2 * D, v_fm, bv_s, 0)):
            for g in range(NG):
                pn = ("PA", PA) if g % 2 == 0 else ("PB", PB)
                mm = None
                for fc in range(DC):
                    mm = P.emit("tensor", lambda e, fc=fc, g=g, pn=pn, wofs=wofs: e.matmul(
                        pn[1][:, :],
                        wsl[:, wofs + fc * 128:wofs + (fc + 1) * 128],
                        xh1_fm[:, fc * T + g * TG:fc * T + (g + 1) * TG],
                        start=(fc == 0), stop=(fc == DC - 1)),
                        waits=[wsl_ld, tok_p1a, ev.get(pn[0]), attn_prev],
                        sig="tensor")
                ev[pn[0]] = P.emit("scalar", lambda e, g=g, pn=pn, dst=dst,
                                   bias_s=bias_s, base=base, h=h: e.activation(
                    dst[:, base + g * TG:base + (g + 1) * TG], pn[1][:, :],
                    AF.Identity, bias=bias_s[:, h:h + 1]),
                    waits=[mm], sig="scalar")
        vev = (ev.get("PA"), ev.get("PB"))
        pt_prev2 = None
        for t8 in range(NT // 8):
            tp = None
            for j in range(8):
                tt = t8 * 8 + j
                tp = P.emit("tensor", lambda e, tt=tt, j=j: e.transpose(
                    PT[:, j * 128:(j + 1) * 128],
                    v_fm[:, tt * 128:(tt + 1) * 128], ident),
                    waits=[vev[0], vev[1], pt_prev2, tok_consts], sig="tensor")
            pt_prev2 = P.emit("scalar", lambda e, t8=t8: e.copy(
                v_tm[:, KPAD + t8 * 1024:KPAD + (t8 + 1) * 1024], PT[:]),
                waits=[tp], sig="scalar")
        qk_done = (ev.get("PA"), ev.get("PB"))
        v_done = (pt_prev2, pt_prev2)
        wsl_free = P.join("scalar", list(qk_done) + list(v_done))

        ps_free = {0: None, 1: None}
        pt_free = v_done[0]
        po_free = {0: None, 1: None}
        pend = None
        tails_done = []

        def emit_tail(ctx):
            nonlocal pt_free
            par, qbase, vblk, pn_toks = ctx
            ev_toks = []
            for qs in range(2):
                kc0 = qs
                tp = None
                for i5, kc in enumerate(range(kc0, kc0 + 5)):
                    tp = P.emit("tensor", lambda e, par=par, qs=qs, kc=kc,
                                i5=i5: e.transpose(
                        PT[:, i5 * 128:(i5 + 1) * 128],
                        p_sb[par][:, qs * 768 + kc * 128:qs * 768 + (kc + 1) * 128],
                        ident), waits=[pn_toks[qs], pt_free], sig="tensor")
                pt_free = P.emit("scalar", lambda e, qs=qs, kc0=kc0: e.copy(
                    pT_sb.rearrange("p (kc q) -> p kc q", q=256)[
                        :, kc0:kc0 + 5, qs * 128:(qs + 1) * 128],
                    PT[:, 0:640].rearrange("p (kc q) -> p kc q", q=128)),
                    waits=[tp], sig="scalar")
                ev_toks.append(pt_free)
            mm = None
            for kc in range(6):
                mm = P.emit("tensor", lambda e, kc=kc, vblk=vblk: e.matmul(
                    PO2[:, :],
                    v_tm[:, (vblk + kc) * 128:(vblk + kc + 1) * 128],
                    pT_sb[:, kc * 256:(kc + 1) * 256],
                    start=(kc == 0), stop=(kc == 5)),
                    waits=[ev_toks[0], ev_toks[1], po_free[0],
                           v_done[0], v_done[1]], sig="tensor")
            po_free[0] = P.emit("scalar", lambda e, h=h, qbase=qbase: e.copy(
                o_fm[h][:, qbase:qbase + 256], PO2[:, :]),
                waits=[mm], sig="scalar")
            tails_done.append(po_free[0])

        idx = 0
        for bb in range(BLOC):
            for c in range(NCH):
                var = 0 if c == 0 else (2 if c == NCH - 1 else 1)
                qbase = bb * SEQ + c * WIN
                wbase = bb * SEQ + c * WIN
                vblk = wbase // 128
                par = idx % 2
                exp_toks = []
                for qs in range(2):
                    psc = PS_e if qs == 0 else PS_o
                    nzc = MASK_NZ[(var, qs)]
                    P.emit("tensor", lambda e, qbase=qbase, qs=qs, wbase=wbase,
                           psc=psc: e.matmul(
                        psc[:, 0:512],
                        q_fm[:, qbase + qs * 128:qbase + (qs + 1) * 128],
                        k_fm[:, wbase:wbase + 512],
                        start=True, stop=False),
                        waits=[qk_done[0], qk_done[1], ps_free[qs], tok_pad,
                               tok_mask], sig="tensor")
                    mlast = P.emit("tensor", lambda e, qbase=qbase, qs=qs,
                                   wbase=wbase, psc=psc: e.matmul(
                        psc[:, 512:768],
                        q_fm[:, qbase + qs * 128:qbase + (qs + 1) * 128],
                        k_fm[:, wbase + 512:wbase + 768],
                        start=True, stop=False), sig="tensor")
                    for kc in nzc:
                        mlast = P.emit("tensor", lambda e, var=var, qs=qs,
                                       kc=kc, psc=psc: e.matmul(
                            psc[:, kc * 128:(kc + 1) * 128], ident,
                            masks[:, (var * 2 + qs) * 768 + kc * 128:
                                  (var * 2 + qs) * 768 + (kc + 1) * 128],
                            start=False, stop=True), sig="tensor")
                    col = par * 2 + qs
                    t1 = P.emit("vector", lambda e, col=col, psc=psc: e.tensor_reduce(
                        rm_c[:, col:col + 1], psc[:, :], AX.X, ALU.max),
                        waits=[mlast], sig="vector")
                    t2n = P.emit("vector", lambda e, col=col: e.tensor_scalar(
                        rm_c[:, col:col + 1], rm_c[:, col:col + 1], -SCALE,
                        None, ALU.mult), waits=[t1], sig="vector")
                    t3 = P.emit("scalar", lambda e, par=par, qs=qs, col=col,
                                psc=psc: e.activation(
                        p_sb[par][:, qs * 768:(qs + 1) * 768], psc[:, :],
                        AF.Exp, bias=rm_c[:, col:col + 1], scale=f32(SCALE),
                        accum_out=sums_c[:, col:col + 1]),
                        waits=[t2n], sig="scalar")
                    exp_toks.append(t3)
                    ps_free[qs] = t3
                tln = P.emit("scalar", lambda e, par=par: e.activation(
                    recip_c[:, par * 2:par * 2 + 2], sums_c[:, par * 2:par * 2 + 2],
                    AF.Ln), waits=exp_toks, sig="scalar")
                t5 = P.emit("scalar", lambda e, par=par: e.activation(
                    recip_c[:, par * 2:par * 2 + 2], recip_c[:, par * 2:par * 2 + 2],
                    AF.Exp, scale=f32(-1.0)), waits=[tln], sig="scalar")
                pn_toks = []
                for qs in range(2):
                    pn_toks.append(P.emit("vector", lambda e, par=par, qs=qs: e.tensor_scalar(
                        p_sb[par][:, qs * 768:(qs + 1) * 768],
                        p_sb[par][:, qs * 768:(qs + 1) * 768],
                        recip_c[:, par * 2 + qs:par * 2 + qs + 1], None,
                        ALU.mult), waits=[t5], sig="vector"))
                if pend is not None:
                    emit_tail(pend)
                pend = (par, qbase, vblk, pn_toks)
                idx += 1
        emit_tail(pend)
        pend = None
        attn_prev = P.join("scalar", tails_done[-2:] +
                           [ps_free[0], ps_free[1]])
    tok_attn = attn_prev

    if level <= 4:
        fins = []
        for h in range(NH if level >= 4 else 1):
            fins.append(P.emit("gpsimd", lambda e, h=h: e.dma_start(
                out=out_d[h * 512:(h + 1) * 512, :].rearrange(
                    "(p f) o -> p (f o)", p=128),
                in_=o_fm[h][:, :]), waits=[tok_attn], sig="dDmp", inc=16))
        P.join("sync", fins)
        P.replay()
        return nc

    st_p1b.close()
    st_p1a.close()
    st_x1fm.close()
    P.barrier()

    st_p1c = ExitStack()

    def sbc(name, shape, dt):
        return st_p1c.enter_context(
            nc.sbuf_tensor(name, shape, dt, side="right")).ap()

    gam_b = sbc("gam_b", [128, D], F32)
    bet_b = sbc("bet_b", [128, D], F32)
    b2_b = sbc("b2_b", [128, D], F32)
    bo_b = sbc("bo_b", [128, D], F32)
    Wo_s = sbc("Wo_s", [128, DC * D], BF16)
    w1b = [sbc(f"w1b{i}", [128, D], BF16) for i in range(2)]
    w2b = [sbc(f"w2b{i}", [128, 512], BF16) for i in range(2)]
    h_fm = [sbc(f"h_fm{i}", [128, (FC // 2) * TG], BF16) for i in range(2)]
    x0c = sbc("x0c", [128, 4 * D], F32)
    t1c = sbc("t1c", [128, D], F32)
    xhc = sbc("xhc", [128, D], F32)
    xhbf = sbc("xhbf", [128, D], BF16)
    t2tm = sbc("t2tm", [128, 4 * D], F32)
    t2fm = sbc("t2fm", [128, DC * TG], BF16)
    t2_3d = t2fm.rearrange("p (c t) -> p c t", t=TG)
    w2e = sbc("w2e", [128, 512], F32)

    bct = None
    rows_ld = []
    for i, src_d in enumerate((gamma_d, beta_d, b2_d, bo_row_d)):
        rows_ld.append(P.emit("sync", lambda e, i=i, src_d=src_d: e.dma_start(
            out=x0c[0:1, i * D:(i + 1) * D], in_=src_d), sig="dRow", inc=16))
    for i, dst in enumerate((gam_b, bet_b, b2_b, bo_b)):
        for half in range(2):
            m = P.emit("tensor", lambda e, i=i, half=half: e.matmul(
                PA[:, :], ones_r[:, :],
                x0c[0:1, i * D + half * 512:i * D + (half + 1) * 512],
                start=True, stop=True),
                waits=[tok_consts, rows_ld[i], bct], sig="tensor")
            bct = P.emit("vector", lambda e, dst=dst, half=half: e.tensor_copy(
                dst[:, half * 512:(half + 1) * 512], PA[:, :]),
                waits=[m], sig="vector")
    tok_bcast = bct

    wo_tok = None
    for ccc in range(DC):
        wo_tok = P.emit("gpsimd", lambda e, ccc=ccc: e.dma_start(
            out=Wo_s[:, ccc * D:(ccc + 1) * D],
            in_=Wo_d[ccc * 128:(ccc + 1) * 128, :]), sig="dWoL", inc=16)

    evt = {"PA": tok_bcast, "PB": None, "PC": None, "PD": None}
    x0_free = [None] * 4
    t1_free = None
    xh_free = None
    t2fm_evac = None
    h_free = None
    w1_free = [None, None]
    w2_free = [None, None]
    war_i = None
    war_2 = None
    x1_st = [None] * 4

    for g in range(NG):
        t2_done = []
        x0_ld2 = [None] * 4
        x0acc = [None] * 4
        for ti in range(4):
            tt = g * 4 + ti
            bx = ti % 2
            x0_ld2[ti] = P.emit("sync", lambda e, tt=tt, ti=ti: e.dma_start(
                out=x0c[:, ti * D:(ti + 1) * D],
                in_=x0_d[tt * 128:(tt + 1) * 128, :]),
                waits=[x0_free[ti], tok_bcast], sig=f"dxc{ti}", inc=16)
            for half, pname, pp in ((0, "PA", PA), (1, "PB", PB)):
                mm = None
                for hc in range(DC):
                    mm = P.emit("tensor", lambda e, tt=tt, hc=hc, half=half,
                                pp=pp: e.matmul(
                        pp[:, :],
                        o_fm[hc][:, tt * 128:(tt + 1) * 128],
                        Wo_s[:, hc * D + half * 512:hc * D + (half + 1) * 512],
                        start=(hc == 0), stop=(hc == DC - 1)),
                        waits=[wo_tok, tok_attn, evt[pname], tok_bcast],
                        sig="tensor")
                evt[pname] = P.emit("vector", lambda e, half=half,
                                    pp=pp: e.tensor_tensor(
                    t1c[:, half * 512:(half + 1) * 512], pp[:, :],
                    bo_b[:, half * 512:(half + 1) * 512], ALU.add),
                    waits=[mm, t1_free], sig="vector")
            ta = P.emit("scalar", lambda e, tt=tt, ti=ti: e.activation(
                xhc[:], x0c[:, ti * D:(ti + 1) * D], AF.Identity,
                bias=nmr1[:, tt:tt + 1], scale=r1[:, tt:tt + 1]),
                waits=[x0_ld2[ti], xh_free], sig="scalar")
            tb = P.emit("vector", lambda e: e.tensor_tensor(
                t1c[:], t1c[:], xhc[:], ALU.add),
                waits=[ta, evt["PA"], evt["PB"]], sig="vector")
            if level <= 5 and g == 7:
                P.emit("gpsimd", lambda e, ti=ti: e.dma_start(
                    out=dbg_scr[ti * 128:(ti + 1) * 128, :], in_=t1c[:]),
                    waits=[tb], sig="dDbg", inc=16)
            war_i = ln_stats(t1c[:], D, lni[:, 0:1], lni[:, 1:2],
                             lni[:, 2:3], stg1,
                             [tb, ("dDbg", P.cnt.get("dDbg", 0))], war_i)
            tc = P.emit("scalar", lambda e: e.activation(
                xhc[:], t1c[:], AF.Identity,
                bias=lni[:, 2:3], scale=lni[:, 1:2]), waits=[war_i],
                sig="scalar")
            t1_free = tc
            td = P.emit("vector", lambda e, ti=ti: e.tensor_tensor(
                t2tm[:, ti * D:(ti + 1) * D], xhc[:], gam_b[:], ALU.mult),
                waits=[tc], sig="vector")
            te = P.emit("vector", lambda e, ti=ti: e.tensor_tensor(
                t2tm[:, ti * D:(ti + 1) * D], t2tm[:, ti * D:(ti + 1) * D],
                bet_b[:], ALU.add), sig="vector")
            xh_free = td
            if level <= 5 and g == 7:
                P.emit("gpsimd", lambda e, ti=ti: e.dma_start(
                    out=dbg_scr[512 + ti * 128:512 + (ti + 1) * 128, :],
                    in_=t2tm[:, ti * D:(ti + 1) * D]), waits=[te],
                    sig="dDbg", inc=16)
            tf = P.emit("scalar", lambda e, ti=ti: e.copy(
                xhbf[:], t2tm[:, ti * D:(ti + 1) * D]),
                waits=[te], sig="scalar")
            tp = None
            for j in range(DC):
                tp = P.emit("tensor", lambda e, j=j: e.transpose(
                    PT[:, j * 128:(j + 1) * 128], xhbf[:, j * 128:(j + 1) * 128],
                    ident), waits=[tf, t2fm_evac], sig="tensor")
            t2fm_evac = P.emit("scalar", lambda e, ti=ti: e.copy(
                t2_3d[:, :, ti * 128:(ti + 1) * 128],
                PT[:].rearrange("p (c t) -> p c t", t=128)),
                waits=[tp], sig="scalar")
            t2_done.append(t2fm_evac)
            x0acc[ti] = P.emit("vector", lambda e, ti=ti: e.tensor_tensor(
                x0c[:, ti * D:(ti + 1) * D], x0c[:, ti * D:(ti + 1) * D],
                t2tm[:, ti * D:(ti + 1) * D], ALU.add),
                waits=[te, ta, x0_ld2[ti], ("dDbg", P.cnt.get("dDbg", 0))],
                sig="vector")

        for oc in range(FC):
            bw = oc % 2
            w1_ld = P.emit("sync", lambda e, oc=oc, bw=bw: e.dma_start(
                out=w1b[bw][:], in_=w1_scr[oc]),
                waits=[w1_free[bw], tok_w12], sig=f"dw1b{bw}", inc=16)
            pn = ("PC", PC) if oc % 2 == 0 else ("PD", PD)
            mm = None
            for fc in range(DC):
                mm = P.emit("tensor", lambda e, fc=fc, pn=pn, bw=bw: e.matmul(
                    pn[1][:, :], w1b[bw][:, fc * 128:(fc + 1) * 128],
                    t2fm[:, fc * TG:(fc + 1) * TG],
                    start=(fc == 0), stop=(fc == DC - 1)),
                    waits=[w1_ld, t2_done[-1], evt[pn[0]], h_free], sig="tensor")
            w1_free[bw] = mm
            evt[pn[0]] = P.emit("scalar", lambda e, oc=oc, pn=pn: e.activation(
                h_fm[oc // 16][:, (oc % 16) * TG:(oc % 16 + 1) * TG],
                pn[1][:, :], AF.Gelu, bias=b1_s[:, oc:oc + 1]),
                waits=[mm], sig="scalar")
        h_done = (evt["PC"], evt["PD"])

        PX = (("PC", PC), ("PD", PD), ("PA", PA), ("PB", PB))
        x0add = [None] * 4
        for oh in range(2):
            mm_t = [None] * 4
            for fc in range(FC):
                bw = fc % 2
                w2_ld = P.emit("sync", lambda e, fc=fc, oh=oh, bw=bw: e.dma_start(
                    out=w2b[bw][:],
                    in_=w2_scr[fc * 128:(fc + 1) * 128, oh * 512:(oh + 1) * 512]),
                    waits=[w2_free[bw], tok_w12], sig=f"dw2b{bw}", inc=16)
                for ti in range(4):
                    mm_t[ti] = P.emit("tensor", lambda e, fc=fc, ti=ti, oh=oh,
                                      bw=bw, PXi=PX[ti][1]: e.matmul(
                        PXi[:, :],
                        h_fm[fc // 16][:, (fc % 16) * TG + ti * 128:
                                       (fc % 16) * TG + (ti + 1) * 128],
                        w2b[bw][:],
                        start=(fc == 0), stop=(fc == FC - 1)),
                        waits=[w2_ld, h_done[0], h_done[1],
                               evt[PX[ti][0]] if fc == 0 else None],
                        sig="tensor")
                w2_free[bw] = mm_t[3]
            for ti in range(4):
                ew = P.emit("vector", lambda e, ti=ti, oh=oh, PXi=PX[ti][1]: e.tensor_tensor(
                    x0c[:, ti * D + oh * 512:ti * D + (oh + 1) * 512],
                    x0c[:, ti * D + oh * 512:ti * D + (oh + 1) * 512],
                    PXi[:, :], ALU.add), waits=[mm_t[ti], x0acc[ti]], sig="vector")
                evt[PX[ti][0]] = ew
                x0add[ti] = P.emit("vector", lambda e, ti=ti, oh=oh: e.tensor_tensor(
                    x0c[:, ti * D + oh * 512:ti * D + (oh + 1) * 512],
                    x0c[:, ti * D + oh * 512:ti * D + (oh + 1) * 512],
                    b2_b[:, oh * 512:(oh + 1) * 512], ALU.add),
                    waits=[ew], sig="vector")
        h_free = w2_free[1]
        for ti in range(4):
            tt = g * 4 + ti
            tg2 = x0add[ti]
            war_2 = ln_stats(x0c[:, ti * D:(ti + 1) * D], D, mu2[:, tt:tt + 1],
                             r2[:, tt:tt + 1], nmr2[:, tt:tt + 1], stg2,
                             [tg2], war_2)
            tmx = P.emit("vector", lambda e, ti=ti: e.tensor_reduce(
                gtmp[:, 0:1], x0c[:, ti * D:(ti + 1) * D], AX.X, ALU.max),
                waits=[tg2], sig="vector")
            tmn = P.emit("vector", lambda e, ti=ti: e.tensor_reduce(
                gtmp[:, 1:2], x0c[:, ti * D:(ti + 1) * D], AX.X, ALU.min),
                sig="vector")
            u1 = P.emit("vector", lambda e, tt=tt: e.tensor_tensor(
                gtmp[:, 0:1], gtmp[:, 0:1], mu2[:, tt:tt + 1], ALU.subtract),
                waits=[war_2, tmx], sig="vector")
            u2 = P.emit("vector", lambda e, tt=tt: e.tensor_tensor(
                gtmp[:, 1:2], mu2[:, tt:tt + 1], gtmp[:, 1:2], ALU.subtract),
                waits=[tmn], sig="vector")
            u3 = P.emit("vector", lambda e: e.tensor_tensor(
                gtmp[:, 0:1], gtmp[:, 0:1], gtmp[:, 1:2], ALU.max),
                waits=[u1, u2], sig="vector")
            u4 = P.emit("vector", lambda e, tt=tt: e.tensor_tensor(
                gtmp[:, 0:1], gtmp[:, 0:1], r2[:, tt:tt + 1], ALU.mult),
                waits=[u3], sig="vector")
            u5 = P.emit("vector", lambda e: e.tensor_tensor(
                gmv[:], gmv[:], gtmp[:, 0:1], ALU.max),
                waits=[t_gm0, u4], sig="vector")
            js = P.join("scalar", [u5, tg2])
            x1_st[ti] = P.emit("gpsimd", lambda e, tt=tt, ti=ti: e.dma_start(
                out=x1_scr[tt * 128:(tt + 1) * 128, :],
                in_=x0c[:, ti * D:(ti + 1) * D]),
                waits=[u5, tg2, js], sig=f"dx1s{ti}", inc=16)
            x0_free[ti] = x1_st[ti]
            if level <= 5 and g == 7:
                x0_free[ti] = P.emit("sync", lambda e, ti=ti: e.dma_start(
                    out=dbg_scr[ti * 128:(ti + 1) * 128, :],
                    in_=x0c[:, ti * D:(ti + 1) * D]),
                    waits=[x1_st[ti]], sig="dDbg2", inc=16)

    tok_x1 = P.join("sync", x1_st)

    if level <= 5:
        fin = P.emit("sync", lambda e: e.dma_start(
            out=out_d, in_=x1_scr), waits=[tok_x1], sig="dDmp", inc=16)
        P.join("sync", [fin])
        P.replay()
        return nc

    st_p1c.close()
    st_ofm.close()
    st_setup.close()
    P.barrier()

    st_wq = ExitStack()
    wdT = st_wq.enter_context(
        nc.sbuf_tensor("wdT", [128, IC * D], BF16, side="left")).ap()
    wgT = [st_wq.enter_context(
        nc.sbuf_tensor(f"wgT{i}", [128, II], BF16, side="left")).ap()
        for i in range(DC)]
    wuT = [st_wq.enter_context(
        nc.sbuf_tensor(f"wuT{i}", [128, II], BF16, side="left")).ap()
        for i in range(DC)]
    st_wql = ExitStack()
    wqld0 = st_wql.enter_context(
        nc.sbuf_tensor("wqld0", [128, II], F32, side="right")).ap()
    wqld1 = st_wql.enter_context(
        nc.sbuf_tensor("wqld1", [128, 1024], F32, side="right")).ap()
    wqq0 = st_wql.enter_context(
        nc.sbuf_tensor("wqq0", [128, II], BF16, side="right")).ap()
    wqq1 = st_wql.enter_context(
        nc.sbuf_tensor("wqq1", [128, 1024], BF16, side="right")).ap()
    wqi0 = st_wql.enter_context(
        nc.sbuf_tensor("wqi0", [128, 1024], mybir.dt.int32, side="right")).ap()
    wq_ld = [wqld0, wqld1]
    wq_q = [wqq0, wqq1]
    wqi = [wqi0, wqi0]
    qw_ld_tok = [None] * 2
    qw_q_tok = [None] * 2

    def quant_weight(src_d, n_tiles, width, dstf, pt_prev, PTX=PT, slots=2):
        ld_tok = qw_ld_tok
        q_tok = qw_q_tok
        tp_done = pt_prev
        for t in range(n_tiles):
            b = t % slots
            ld_tok[b] = P.emit("sync", lambda e, t=t, b=b: e.dma_start(
                out=wq_ld[b][:, 0:width], in_=src_d[t * 128:(t + 1) * 128, :]),
                waits=[q_tok[b], tp_done], sig=f"dQl{b}", inc=16)
            t1 = P.emit("vector", lambda e, b=b, width=width: e.tensor_reduce(
                rmx[:, b:b + 1], wq_ld[b][:, 0:width], AX.X, ALU.max,
                apply_absolute_value=True), waits=[ld_tok[b]], sig="vector")
            t2 = P.emit("scalar", lambda e, b=b: e.activation(
                rsc[:, b:b + 1], rmx[:, b:b + 1], AF.Ln), waits=[t1],
                sig="scalar")
            t3 = P.emit("scalar", lambda e, b=b: e.activation(
                rsc[:, b:b + 1], rsc[:, b:b + 1], AF.Exp,
                scale=f32(-1.0), bias=f32(LN127)), waits=[t2], sig="scalar")
            t4 = P.emit("vector", lambda e, b=b: e.tensor_scalar(
                ssc[:, b:b + 1], rmx[:, b:b + 1], 1.0 / 127.0, None, ALU.mult),
                waits=[t1], sig="vector")
            P.emit("vector", lambda e, b=b, width=width: e.tensor_scalar(
                wq_ld[b][:, 0:width], wq_ld[b][:, 0:width],
                rsc[:, b:b + 1], -(QMAX + 1.0), ALU.mult, ALU.max),
                waits=[t3], sig="vector")
            for c0 in range(0, width, 1024):
                cw = min(1024, width - c0)
                P.emit("vector", lambda e, b=b, c0=c0, cw=cw: e.tensor_scalar(
                    wqi[b][:, 0:cw], wq_ld[b][:, c0:c0 + cw],
                    QMAX, None, ALU.min), sig="vector")
                q_tok[b] = P.emit("vector", lambda e, b=b, c0=c0,
                                  cw=cw: e.tensor_scalar(
                    wq_q[b][:, c0:c0 + cw], wqi[b][:, 0:cw],
                    ssc[:, b:b + 1], None, ALU.mult),
                    waits=[t4], sig="vector")
            nj = width // 128
            for j0 in range(0, nj, 8):
                jn = min(8, nj - j0)
                tp = None
                for j in range(jn):
                    tp = P.emit("tensor", lambda e, b=b, j0=j0, j=j,
                                PTX=PTX: e.transpose(
                        PTX[:, j * 128:(j + 1) * 128],
                        wq_q[b][:, (j0 + j) * 128:(j0 + j + 1) * 128], ident),
                        waits=[q_tok[b], tp_done], sig="tensor")
                for j in range(jn):
                    tp_done = P.emit("scalar", lambda e, t=t, j0=j0, j=j,
                                     dstf=dstf, PTX=PTX: e.copy(
                        dstf(j0 + j)[:, t * 128:(t + 1) * 128],
                        PTX[:, j * 128:(j + 1) * 128]),
                        waits=[tp], sig="scalar")
        return tp_done

    wd3 = wdT.rearrange("p (c i) -> p c i", i=D)
    tok_wg = quant_weight(wg_d, IC, D, lambda c: wgT[c], None)
    tok_wu = quant_weight(wu_d, IC, D, lambda c: wuT[c], tok_wg)

    t1 = P.emit("gpsimd", lambda e: e.dma_start(
        out=gmx_scr, in_=gmv[:]), waits=[tok_x1], sig="dGm1", inc=16)
    t2 = P.emit("sync", lambda e: e.dma_start(
        out=grow[:], in_=gmx_scr.rearrange("p one -> one p")),
        waits=[t1], sig="dGr1", inc=16)
    t3 = P.emit("vector", lambda e: e.tensor_reduce(
        srow[:, 0:1], grow[:], AX.X, ALU.max), waits=[t2], sig="vector")
    t4 = P.emit("gpsimd", lambda e: e.dma_start(out=cc1_in, in_=srow[:, 0:1]),
                waits=[t3], sig="dCi1", inc=16)
    t5 = P.emit("gpsimd", lambda e: e.collective_compute(
        "AllReduce", ALU.max, replica_groups=RG8, ins=[cc1_in], outs=[cc1_out]),
        waits=[t4], sig="pool")
    t6 = P.emit("sync", lambda e: e.dma_start(out=srow[:, 0:1], in_=cc1_out),
                waits=[t5], sig="dCo1", inc=16)
    t7 = P.emit("scalar", lambda e: e.activation(
        srow[:, 1:2], clip_s[0:1, 0:1], AF.Sigmoid),
        waits=[tok_consts, t6], sig="scalar")
    t8 = P.emit("vector", lambda e: e.tensor_tensor(
        srow[:, 2:3], srow[:, 0:1], srow[:, 1:2], ALU.mult),
        waits=[t7], sig="vector")
    t9 = P.emit("vector", lambda e: e.tensor_scalar(
        srow[:, 2:3], srow[:, 2:3], 1.0 / 127.0, None, ALU.mult),
        waits=[t8], sig="vector")
    t10 = P.emit("scalar", lambda e: e.activation(
        srow[:, 3:4], srow[:, 2:3], AF.Ln), waits=[t9], sig="scalar")
    t11 = P.emit("scalar", lambda e: e.activation(
        srow[:, 3:4], srow[:, 3:4], AF.Exp, scale=f32(-1.0)), waits=[t10],
        sig="scalar")
    t12 = P.emit("tensor", lambda e: e.matmul(
        PB3[:, 0:2], ones_r[:, :], srow[:, 2:4], start=True, stop=True),
        waits=[t11], sig="tensor")
    tok_s1 = P.emit("vector", lambda e: e.tensor_copy(qsc[:, 0:2], PB3[:, 0:2]),
                    waits=[t12], sig="vector")
    tqa = P.emit("vector", lambda e: e.tensor_scalar(
        qa[:], r2[:], qsc[:, 1:2], None, ALU.mult), waits=[tok_s1], sig="vector")
    tqb = P.emit("vector", lambda e: e.tensor_tensor(
        qb[:], mu2[:], qa[:], ALU.mult), waits=[tqa], sig="vector")
    tok_qc = P.emit("vector", lambda e: e.tensor_scalar(
        qb[:], qb[:], -1.0, None, ALU.mult), waits=[tqb], sig="vector")

    if level <= 6:
        fin = P.emit("sync", lambda e: e.dma_start(
            out=out_d[0:128, 0:4], in_=qsc[:]), waits=[tok_qc, tok_wu],
            sig="dDmp", inc=16)
        P.join("sync", [fin])
        P.replay()
        return nc

    st_p2 = ExitStack()

    def sb2(name, shape, dt):
        return st_p2.enter_context(
            nc.sbuf_tensor(name, shape, dt, side="right")).ap()

    x1b = [sb2(f"x1b{i}", [128, D], F32) for i in range(2)]
    xqb = [sb2(f"xqb{i}", [128, D], F32) for i in range(2)]
    xqbf0 = sb2("xqbf0", [128, D], BF16)
    xqbf = [xqbf0, xqbf0]
    xqi = [sb2(f"xqi{i}", [128, D], mybir.dt.int32) for i in range(2)]
    xq_fm = sb2("xq_fm", [128, DC * TG], BF16)
    xq_3d = xq_fm.rearrange("p (c t) -> p c t", t=TG)
    sg_f = [sb2(f"sg_f{i}", [128, TG], F32) for i in range(2)]
    up_f = [sb2(f"up_f{i}", [128, TG], F32) for i in range(2)]
    ha_bf = [sb2(f"ha_bf{i}", [128, TG], BF16) for i in range(2)]

    x1_ld = [None, None]
    xq_free = None
    fm_ev = None
    xq4_free = [None, None]
    ev2 = {"PA": None, "PB": None, "PC": None, "PD": None}
    ha_free = [None, None]
    q_free = [None, None]

    for g in range(NG):
        fm_toks = []
        for ti in range(4):
            tt = g * 4 + ti
            bx = ti % 2
            x1_ld[bx] = P.emit("sync", lambda e, tt=tt, bx=bx: e.dma_start(
                out=x1b[bx][:], in_=x1_scr[tt * 128:(tt + 1) * 128, :]),
                waits=[q_free[bx], tok_x1], sig=f"dx1b{bx}", inc=16)
            q1 = P.emit("scalar", lambda e, tt=tt, bx=bx: e.activation(
                xqb[bx][:], x1b[bx][:], AF.Identity,
                bias=qb[:, tt:tt + 1], scale=qa[:, tt:tt + 1]),
                waits=[x1_ld[bx], tok_qc, q_free[bx]], sig="scalar")
            q_free[bx] = q1
            P.emit("vector", lambda e, bx=bx: e.tensor_scalar(
                xqb[bx][:], xqb[bx][:], -(QMAX + 1.0), QMAX, ALU.max, ALU.min),
                waits=[q1], sig="vector")
            P.emit("vector", lambda e, bx=bx: e.tensor_scalar(
                xqi[bx][:], xqb[bx][:], 0.0, None, ALU.add),
                waits=[xq4_free[bx]], sig="vector")
            q3c = P.join("vector", [])
            q4 = P.emit("scalar", lambda e, bx=bx: e.activation(
                xqbf[bx][:], xqi[bx][:], AF.Copy, scale=qsc[:, 0:1]),
                waits=[q3c, xq4_free[bx]], sig="scalar")
            xq4_free[bx] = q4
            tp = None
            for j in range(DC):
                tp = P.emit("tensor", lambda e, j=j, bx=bx: e.transpose(
                    PT[:, j * 128:(j + 1) * 128], xqbf[bx][:, j * 128:(j + 1) * 128],
                    ident), waits=[q4, xq_free, fm_ev], sig="tensor")
            fm_ev = P.emit("scalar", lambda e, ti=ti: e.copy(
                xq_3d[:, :, ti * 128:(ti + 1) * 128],
                PT[:].rearrange("p (c t) -> p c t", t=128)),
                waits=[tp], sig="scalar")
            fm_toks.append(fm_ev)
        tok_xq = fm_toks[-1]
        mmu = None
        for icx in range(IC):
            bh = icx % 2
            pg = ("PA", PA) if icx % 2 == 0 else ("PB", PB)
            pu = ("PC", PC) if icx % 2 == 0 else ("PD", PD)
            mmg = None
            for fc in range(DC):
                mmg = P.emit("tensor", lambda e, icx=icx, fc=fc, pg=pg: e.matmul(
                    pg[1][:, :], wgT[fc][:, icx * 128:(icx + 1) * 128],
                    xq_fm[:, fc * TG:(fc + 1) * TG],
                    start=(fc == 0), stop=(fc == DC - 1)),
                    waits=[tok_xq, tok_wg, ev2[pg[0]]], sig="tensor")
            for fc in range(DC):
                mmu = P.emit("tensor", lambda e, icx=icx, fc=fc, pu=pu: e.matmul(
                    pu[1][:, :], wuT[fc][:, icx * 128:(icx + 1) * 128],
                    xq_fm[:, fc * TG:(fc + 1) * TG],
                    start=(fc == 0), stop=(fc == DC - 1)),
                    waits=[tok_wu, ev2[pu[0]]], sig="tensor")
            e1 = P.emit("scalar", lambda e, bh=bh, pg=pg: e.activation(
                sg_f[bh][:], pg[1][:, :], AF.Silu),
                waits=[mmg, ha_free[bh]], sig="scalar")
            ev2[pg[0]] = e1
            e2 = P.emit("scalar", lambda e, bh=bh, pu=pu: e.activation(
                up_f[bh][:], pu[1][:, :], AF.Copy),
                waits=[mmu, ha_free[bh]], sig="scalar")
            ev2[pu[0]] = e2
            e3 = P.emit("vector", lambda e, bh=bh: e.tensor_tensor(
                ha_bf[bh][:], sg_f[bh][:], up_f[bh][:], ALU.mult),
                waits=[e1, e2], sig="vector")
            if level <= 7 and g == 0 and icx == 0:
                P.emit("gpsimd", lambda e, bh=bh: e.dma_start(
                    out=dbg_scr[512:640, :].rearrange(
                        "p o -> p o")[:, 0:TG], in_=ha_bf[bh][:]),
                    waits=[e3], sig="dDbg", inc=16)
                P.emit("gpsimd", lambda e, bh=bh: e.dma_start(
                    out=dbg_scr[640:768, 0:TG], in_=sg_f[bh][:]),
                    waits=[e3], sig="dDbg", inc=16)
                P.emit("gpsimd", lambda e, bh=bh: e.dma_start(
                    out=dbg_scr[768:896, 0:TG], in_=up_f[bh][:]),
                    waits=[e3], sig="dDbg", inc=16)
                P.emit("gpsimd", lambda e: e.dma_start(
                    out=dbg_scr[896:1024, :], in_=wgT[0][:, 0:D]),
                    waits=[tok_wg], sig="dDbg", inc=16)
            if level <= 7 and g == 0 and icx == 2:
                P.emit("gpsimd", lambda e: e.dma_start(
                    out=dbg_scr[0:512, :].rearrange("(a p) o -> p a o", p=128),
                    in_=xq_fm[:].rearrange("p (a o) -> p a o", o=1024)),
                    waits=[tok_xq], sig="dDbg", inc=16)
            e4 = P.emit("vector", lambda e, bh=bh: e.tensor_reduce(
                gtmp[:, 0:1], ha_bf[bh][:], AX.X, ALU.max,
                apply_absolute_value=True), waits=[e3], sig="vector")
            e5 = P.emit("vector", lambda e: e.tensor_tensor(
                gmv2[:], gmv2[:], gtmp[:, 0:1], ALU.max),
                waits=[t_gm20, e4], sig="vector")
            ha_free[bh] = P.emit("gpsimd", lambda e, icx=icx, g=g, bh=bh: e.dma_start(
                out=hact_scr[icx * 128:(icx + 1) * 128, g * TG:(g + 1) * TG],
                in_=ha_bf[bh][:]), waits=[e5], sig=f"dHa{bh}", inc=16)
        xq_free = mmu
    tok_wd = quant_weight(wd_d, DC, II, lambda c: wd3[:, c], tok_wu, PTX=PT2,
                          slots=1)
    tok_p2 = P.join("sync", [ha_free[0], ha_free[1]])

    st_p2.close()
    st_wql.close()
    P.barrier()

    t1 = P.emit("gpsimd", lambda e: e.dma_start(
        out=gmx_scr, in_=gmv2[:]), waits=[tok_p2], sig="dGm2", inc=16)
    t2 = P.emit("sync", lambda e: e.dma_start(
        out=grow[:], in_=gmx_scr.rearrange("p one -> one p")),
        waits=[t1], sig="dGr2", inc=16)
    t3 = P.emit("vector", lambda e: e.tensor_reduce(
        srow[:, 4:5], grow[:], AX.X, ALU.max), waits=[t2], sig="vector")
    t4 = P.emit("gpsimd", lambda e: e.dma_start(out=cc2_in, in_=srow[:, 4:5]),
                waits=[t3], sig="dCi2", inc=16)
    t5 = P.emit("gpsimd", lambda e: e.collective_compute(
        "AllReduce", ALU.max, replica_groups=RG8, ins=[cc2_in], outs=[cc2_out]),
        waits=[t4], sig="pool")
    t6 = P.emit("sync", lambda e: e.dma_start(out=srow[:, 4:5], in_=cc2_out),
                waits=[t5], sig="dCo2", inc=16)
    t7 = P.emit("scalar", lambda e: e.activation(
        srow[:, 5:6], clip_s[0:1, 1:2], AF.Sigmoid), waits=[t6], sig="scalar")
    t8 = P.emit("vector", lambda e: e.tensor_tensor(
        srow[:, 6:7], srow[:, 4:5], srow[:, 5:6], ALU.mult),
        waits=[t7], sig="vector")
    t9 = P.emit("vector", lambda e: e.tensor_scalar(
        srow[:, 6:7], srow[:, 6:7], 1.0 / 127.0, None, ALU.mult),
        waits=[t8], sig="vector")
    t10 = P.emit("scalar", lambda e: e.activation(
        srow[:, 7:8], srow[:, 6:7], AF.Ln), waits=[t9], sig="scalar")
    t11 = P.emit("scalar", lambda e: e.activation(
        srow[:, 7:8], srow[:, 7:8], AF.Exp, scale=f32(-1.0)), waits=[t10],
        sig="scalar")
    t12 = P.emit("tensor", lambda e: e.matmul(
        PB3[:, 0:2], ones_r[:, :], srow[:, 6:8], start=True, stop=True),
        waits=[t11], sig="tensor")
    tok_s2 = P.emit("vector", lambda e: e.tensor_copy(qsc[:, 2:4], PB3[:, 0:2]),
                    waits=[t12], sig="vector")

    if level <= 7:
        fin = P.emit("sync", lambda e: e.dma_start(
            out=out_d[0:128, 0:4], in_=qsc[:]), waits=[tok_s2],
            sig="dDmp", inc=16)
        P.join("sync", [fin])
        P.replay()
        return nc

    st_p3 = ExitStack()

    def sb3(name, shape, dt):
        return st_p3.enter_context(
            nc.sbuf_tensor(name, shape, dt, side="right")).ap()

    ha_ld = [sb3(f"ha_ld{i}", [128, TG], BF16) for i in range(2)]
    hq_fm = sb3("hq_fm", [128, IC * TG], BF16)
    x1p = [sb3(f"x1p{i}", [128, D], F32) for i in range(2)]
    otile = [sb3(f"otile{i}", [128, D], F32) for i in range(2)]
    qstg = sb3("qstg", [128, TG], F32)
    qsti = [sb3(f"qsti{i}", [128, TG], mybir.dt.int32) for i in range(2)]

    ha_tok = [None, None]
    ha_used = [None, None]
    hq3_free = [None, None]
    hq_free = None
    out_st = [None, None]
    ev3 = {"PA": None, "PB": None}
    x1_ld3 = [None, None]
    out_toks = []

    for g in range(NG):
        hq_done = []
        for icx in range(IC):
            bh = icx % 2
            ha_tok[bh] = P.emit("sync", lambda e, icx=icx, g=g, bh=bh: e.dma_start(
                out=ha_ld[bh][:],
                in_=hact_scr[icx * 128:(icx + 1) * 128, g * TG:(g + 1) * TG]),
                waits=[ha_used[bh], tok_p2], sig=f"dHl{bh}", inc=16)
            q1 = P.emit("vector", lambda e, bh=bh: e.tensor_scalar(
                qstg[:], ha_ld[bh][:], qsc[:, 3:4], -(QMAX + 1.0),
                ALU.mult, ALU.max),
                waits=[ha_tok[bh], tok_s2, hq_free, hq3_free[bh]], sig="vector")
            ha_used[bh] = q1
            q2v = P.emit("vector", lambda e, bh=bh: e.tensor_scalar(
                qsti[bh][:], qstg[:], QMAX, None, ALU.min), sig="vector")
            q3s = P.emit("scalar", lambda e, icx=icx, bh=bh: e.activation(
                hq_fm[:, icx * TG:(icx + 1) * TG], qsti[bh][:],
                AF.Copy, scale=qsc[:, 2:3]), waits=[q2v], sig="scalar")
            hq3_free[bh] = q3s
            hq_done.append(q3s)
        for ti in range(4):
            tt = g * 4 + ti
            bx = ti % 2
            x1_ld3[bx] = P.emit("sync", lambda e, tt=tt, bx=bx: e.dma_start(
                out=x1p[bx][:], in_=x1_scr[tt * 128:(tt + 1) * 128, :]),
                waits=[out_st[bx]], sig=f"dxp{bx}", inc=16)
            mm_l = None
            for oh, pname, pp in ((0, "PA", PA), (1, "PB", PB)):
                mm = None
                for icx in range(IC):
                    mm = P.emit("tensor", lambda e, icx=icx, ti=ti, oh=oh,
                                pp=pp: e.matmul(
                        pp[:, :],
                        hq_fm[:, icx * TG + ti * 128:icx * TG + (ti + 1) * 128],
                        wd3[:, icx, oh * 512:(oh + 1) * 512],
                        start=(icx == 0), stop=(icx == IC - 1)),
                        waits=[hq_done[icx], tok_wd, ev3[pname]], sig="tensor")
                mm_l = mm
                ev3[pname] = P.emit("scalar", lambda e, oh=oh, pp=pp,
                                    bx=bx: e.activation(
                    otile[bx][:, oh * 512:(oh + 1) * 512], pp[:, :], AF.Copy),
                    waits=[mm, out_st[bx]], sig="scalar")
            if ti == 3:
                hq_free = mm_l
            o1 = P.emit("vector", lambda e, bx=bx: e.tensor_tensor(
                otile[bx][:], otile[bx][:], x1p[bx][:], ALU.add),
                waits=[ev3["PA"], ev3["PB"], x1_ld3[bx]], sig="vector")
            out_st[bx] = P.emit("sync", lambda e, tt=tt, bx=bx: e.dma_start(
                out=out_d[tt * 128:(tt + 1) * 128, :], in_=otile[bx][:]),
                waits=[o1], sig=f"dOut{bx}", inc=16)
            out_toks.append(out_st[bx])

    P.join("sync", [out_toks[-1], out_toks[-2]])

    P.replay()
    st_wq.close()
    st_p3.close()
    st_ps.close()
    st_ofm_dummy = None
    st_perm.close()
    return nc


def _shard_inputs(inputs):
    x = np.asarray(inputs["L0_attn_x"])

    def lay(v, shape):
        return np.ascontiguousarray(np.asarray(v, np.float32).reshape(shape))

    def chunked(v, n):
        return np.ascontiguousarray(
            np.asarray(v, np.float32).reshape(n, 128).T)

    ins = {
        "Wq": lay(inputs["L0_attn_Wq"], (D, D)),
        "Wk": lay(inputs["L0_attn_Wk"], (D, D)),
        "Wv": lay(inputs["L0_attn_Wv"], (D, D)),
        "Wo": lay(inputs["L0_attn_Wo"], (D, D)),
        "bq": chunked(inputs["L0_attn_bq"], DC),
        "bk": chunked(inputs["L0_attn_bk"], DC),
        "bv": chunked(inputs["L0_attn_bv"], DC),
        "bo_row": lay(inputs["L0_attn_bo"], (1, D)),
        "gamma": lay(inputs["L0_attn_gamma"], (1, D)),
        "beta": lay(inputs["L0_attn_beta"], (1, D)),
        "W1": lay(inputs["L0_attn_W1"], (D, FF)),
        "b1": chunked(inputs["L0_attn_b1"], FC),
        "W2": lay(inputs["L0_attn_W2"], (FF, D)),
        "b2": lay(inputs["L0_attn_b2"], (1, D)),
        "clip_x": lay(inputs["L0_ffn_clip_x"], (1, 1)),
        "clip_down": lay(inputs["L0_ffn_clip_down"], (1, 1)),
        "w_gate": lay(inputs["L0_ffn_w_gate"], (II, D)),
        "w_up": lay(inputs["L0_ffn_w_up"], (II, D)),
        "w_down": lay(inputs["L0_ffn_w_down"], (D, II)),
    }
    in_maps = []
    for c in range(8):
        m = dict(ins)
        m["x0"] = np.ascontiguousarray(
            x[BLOC * c:BLOC * (c + 1)].reshape(T, D), np.float32)
        in_maps.append(m)
    return in_maps


def kernel(**inputs):
    nc = build_program()
    in_maps = _shard_inputs(inputs)
    res = run_bass_kernel_spmd(nc, in_maps, core_ids=list(range(8)))
    out = np.concatenate(
        [np.asarray(r["out"]).reshape(BLOC, SEQ, D) for r in res.results], axis=0)
    return out


if __name__ == "__main__":
    nc = build_program()
    print("built ok; instructions:", nc.n_instructions())


# version 20
# speedup vs baseline: 1.0295x; 1.0256x over previous
import numpy as np
import ml_dtypes
import concourse.bass as bass
import concourse.mybir as mybir
from concourse.bass_utils import run_bass_kernel_spmd
from contextlib import ExitStack

F32 = mybir.dt.float32
BF16 = mybir.dt.bfloat16
AF = mybir.ActivationFunctionType
ALU = mybir.AluOpType
AX = mybir.AxisListType
f32 = np.float32

D = 1024
FF = 4096
II = 2816
NH = 8
WIN = 256
SEQ = 2048
BLOC = 2
T = BLOC * SEQ
NT = T // 128
TG = 512
NG = T // TG
DC = D // 128
FC = FF // 128
IC = II // 128
NCH = SEQ // WIN
KPAD = WIN
TK = T + 2 * KPAD
SCALE = float(128.0) ** -0.5
EPS = 1e-5
QMAX = 127.0
MAGIC = float(f32(3 * 2 ** 22))
MASKV = -30000.0
LN127 = float(np.log(f32(127.0)))
RG8 = [[0, 1, 2, 3, 4, 5, 6, 7]]

ENGS = ("sync", "tensor", "vector", "scalar", "gpsimd")


def _build_masks():
    kr = np.arange(768)
    qi = np.arange(256)
    band = (kr[None, :] >= qi[:, None]) & (kr[None, :] <= qi[:, None] + 2 * WIN)
    masks = np.zeros((3, 2, 128, 768), np.float32)
    nonzero = {}
    for v in range(3):
        valid = band.copy()
        if v == 0:
            valid = valid & (kr[None, :] >= WIN)
        elif v == 2:
            valid = valid & (kr[None, :] < 2 * WIN)
        add = np.where(valid, 0.0, MASKV).astype(np.float32)
        for qs in range(2):
            masks[v, qs] = add[qs * 128:(qs + 1) * 128]
            nonzero[(v, qs)] = [
                kc for kc in range(6)
                if np.any(masks[v, qs][:, kc * 128:(kc + 1) * 128] != 0.0)]
    return masks.astype(ml_dtypes.bfloat16), nonzero


MASKS_NP, MASK_NZ = _build_masks()


class Prog:

    def __init__(self, nc):
        self.nc = nc
        self.ops = {e: [] for e in ENGS}
        self.sem_h = {}
        self.cnt = {}
        self.waited = {e: {} for e in ENGS}

    def _prep(self, eng, waits, sig, inc):
        nw = []
        for t in waits:
            if t is None:
                continue
            s, v = t
            if self.waited[eng].get(s, 0) >= v:
                continue
            nw.append((s, v))
            self.waited[eng][s] = v
        tok = None
        if sig is not None:
            self.cnt[sig] = self.cnt.get(sig, 0) + inc
            tok = (sig, self.cnt[sig])
        return tuple(nw), tok

    def emit(self, eng, fn, waits=(), sig=None, inc=1):
        nw, tok = self._prep(eng, waits, sig, inc)

        def closure(nw=nw, sig=sig, fn=fn, inc=inc, eng=eng):
            e = getattr(self.nc, eng)
            for (s, v) in nw:
                e.wait_ge(self.sem_h[s], v)
            inst = fn(e)
            if sig is not None:
                inst.then_inc(self.sem_h[sig], inc)

        self.ops[eng].append(closure)
        return tok

    def join(self, eng, waits, sig=None):
        if sig is None:
            sig = eng
        nw, tok = self._prep(eng, waits, sig, 1)

        def closure(nw=nw, sig=sig, eng=eng):
            e = getattr(self.nc, eng)
            for (s, v) in nw:
                e.wait_ge(self.sem_h[s], v)
            e.sem_inc(self.sem_h[sig], 1)

        self.ops[eng].append(closure)
        return tok

    def barrier(self, engs=ENGS):
        dma_toks = [(n, v) for n, v in self.cnt.items() if n.startswith("d")]
        toks = [self.join(e, dma_toks) for e in engs]
        for e in engs:
            nw, _ = self._prep(e, toks, None, 1)

            def closure(nw=nw, eng=e):
                en = getattr(self.nc, eng)
                for (s, v) in nw:
                    en.wait_ge(self.sem_h[s], v)

            self.ops[e].append(closure)

    def replay(self):
        with ExitStack() as st:
            for s in self.cnt:
                self.sem_h[s] = st.enter_context(self.nc.semaphore(s))
            for e in ENGS:
                for s, v in self.waited[e].items():
                    assert self.cnt.get(s, 0) >= v, (e, s, v, self.cnt.get(s))
            with self.nc.Block() as block:
                for e in ENGS:
                    getattr(block, e)(lambda _, e=e: [f() for f in self.ops[e]])


def build_program(level=99):
    nc = bass.Bass("TRN2", target_bir_lowering=False)
    P = Prog(nc)

    def din(name, shape):
        return nc.dram_tensor(name, shape, F32, kind="ExternalInput").ap()

    x0_d = din("x0", [T, D])
    Wq_d, Wk_d, Wv_d, Wo_d = (din(n, [D, D]) for n in ("Wq", "Wk", "Wv", "Wo"))
    bq_d, bk_d, bv_d = (din(n, [128, DC]) for n in ("bq", "bk", "bv"))
    gamma_d = din("gamma", [1, D])
    beta_d = din("beta", [1, D])
    W1_d = din("W1", [D, FF])
    b1_d = din("b1", [128, FC])
    W2_d = din("W2", [FF, D])
    b2_d = din("b2", [1, D])
    bo_row_d = din("bo_row", [1, D])
    clipx_d = din("clip_x", [1, 1])
    clipd_d = din("clip_down", [1, 1])
    wg_d = din("w_gate", [II, D])
    wu_d = din("w_up", [II, D])
    wd_d = din("w_down", [D, II])

    out_d = nc.dram_tensor("out", [T, D], F32, kind="ExternalOutput").ap()

    x1_scr = nc.dram_tensor("x1_scr", [T, D], F32, kind="Internal").ap()
    hact_scr = nc.dram_tensor("hact_scr", [II, T], BF16, kind="Internal").ap()
    w1_scr = nc.dram_tensor("w1_scr", [FC, 128, D], BF16, kind="Internal").ap()
    w2_scr = nc.dram_tensor("w2_scr", [FF, D], BF16, kind="Internal").ap()
    gmx_scr = nc.dram_tensor("gmx_scr", [128, 1], F32, kind="Internal").ap()
    dbg_scr = nc.dram_tensor(
        "dbg", [1024, D], F32,
        kind="ExternalOutput" if level <= 7 else "Internal").ap()
    cc1_in = nc.dram_tensor("cc1_in", [1, 1], F32, kind="Internal").ap()
    cc1_out = nc.dram_tensor("cc1_out", [1, 1], F32, kind="Internal",
                             addr_space="Shared").ap()
    cc2_in = nc.dram_tensor("cc2_in", [1, 1], F32, kind="Internal").ap()
    cc2_out = nc.dram_tensor("cc2_out", [1, 1], F32, kind="Internal",
                             addr_space="Shared").ap()

    ident_d = nc.inline_tensor(np.eye(128, dtype=ml_dtypes.bfloat16), "identc").ap()
    masks_d = nc.inline_tensor(
        np.ascontiguousarray(MASKS_NP.transpose(2, 0, 1, 3).reshape(128, 6 * 768)),
        "maskc").ap()
    ones_d = nc.inline_tensor(np.ones((1, 128), np.float32), "onesc").ap()

    st_perm = ExitStack()

    def sb(name, shape, dt):
        return st_perm.enter_context(
            nc.sbuf_tensor(name, shape, dt, side="left")).ap()

    ident = sb("ident", [128, 128], BF16)
    ones_r = sb("ones_r", [1, 128], F32)
    bq_s = sb("bq_s", [128, DC], F32)
    bk_s = sb("bk_s", [128, DC], F32)
    bv_s = sb("bv_s", [128, DC], F32)
    b1_s = sb("b1_s", [128, FC], F32)
    clip_s = sb("clip_s", [1, 2], F32)
    mu1 = sb("mu1", [128, NT], F32)
    r1 = sb("r1", [128, NT], F32)
    nmr1 = sb("nmr1", [128, NT], F32)
    mu2 = sb("mu2", [128, NT], F32)
    r2 = sb("r2", [128, NT], F32)
    nmr2 = sb("nmr2", [128, NT], F32)
    gmv = sb("gmv", [128, 1], F32)
    gmv2 = sb("gmv2", [128, 1], F32)
    stg1 = sb("stg1", [128, 20], F32)
    stg2 = sb("stg2", [128, 5], F32)
    lni = sb("lni", [128, 3], F32)
    gtmp = sb("gtmp", [128, 2], F32)
    srow = sb("srow", [1, 8], F32)
    qsc = sb("qsc", [128, 4], F32)
    scrap = sb("scrap", [128, 1024], BF16)
    sums_c = sb("sums_c", [128, 4], F32)
    recip_c = sb("recip_c", [128, 4], F32)
    rm_c = sb("rm_c", [128, 4], F32)
    grow = sb("grow", [1, 128], F32)
    qa = sb("qa", [128, NT], F32)
    qb = sb("qb", [128, NT], F32)
    rmx = sb("rmx", [128, 4], F32)
    rsc = sb("rsc", [128, 4], F32)
    ssc = sb("ssc", [128, 4], F32)

    st_ofm = ExitStack()
    o_fm = [st_ofm.enter_context(
        nc.sbuf_tensor(f"o_fm{h}", [128, T], BF16, side="left")).ap()
        for h in range(NH)]

    st_ps = ExitStack()

    def pst(name, shape, dt):
        return st_ps.enter_context(nc.psum_tensor(name, shape, dt)).ap()

    PB1 = pst("PB1", [128, 1024], F32)
    PB2 = pst("PB2", [128, 1024], F32)
    PB3 = pst("PB3", [128, 1024], F32)
    PT = pst("PTT", [128, 1024], BF16)
    PT2 = pst("PTT2", [128, 1024], BF16)
    PA = PB1[:, 0:512]
    PB = PB1[:, 512:1024]
    PC = PB2[:, 0:512]
    PD = PB2[:, 512:1024]
    PS_e = PB1[:, 0:768]
    PS_o = PB2[:, 0:768]
    PO2 = PB3[:, 0:256]

    for dst, src in ((ident, ident_d), (ones_r, ones_d),
                     (bq_s, bq_d), (bk_s, bk_d), (bv_s, bv_d), (b1_s, b1_d)):
        P.emit("sync", lambda e, d=dst, s=src: e.dma_start(out=d, in_=s),
               sig="dCst", inc=16)
    P.emit("sync", lambda e: e.dma_start(out=clip_s[:, 0:1], in_=clipx_d),
           sig="dCst", inc=16)
    tok_consts = P.emit("sync",
                        lambda e: e.dma_start(out=clip_s[:, 1:2], in_=clipd_d),
                        sig="dCst", inc=16)
    t_gm0 = P.emit("vector", lambda e: e.memset(gmv[:], 0.0), sig="vector")
    t_gm20 = P.emit("vector", lambda e: e.memset(gmv2[:], 0.0), sig="vector")

    st_setup = ExitStack()
    wtmp = [st_setup.enter_context(
        nc.sbuf_tensor(f"wtmp{i}", [128, D], BF16, side="right")).ap()
        for i in range(2)]
    wst = [None, None]
    for oc in range(FC):
        b = oc % 2
        tin = P.emit("gpsimd", lambda e, oc=oc, b=b: e.dma_start(
            out=wtmp[b].rearrange("p (c o) -> p c o", o=128),
            in_=W1_d.rearrange("(c p) o -> p c o", p=128)[:, :, oc * 128:(oc + 1) * 128],
            allow_slow_non_contiguous=True),
            waits=[wst[b]], sig=f"dWc{b}", inc=16)
        wst[b] = P.emit("gpsimd", lambda e, oc=oc, b=b: e.dma_start(
            out=w1_scr[oc], in_=wtmp[b][:]), waits=[tin], sig=f"dWs{b}", inc=16)
    for fc in range(FC):
        b = fc % 2
        tin = P.emit("gpsimd", lambda e, fc=fc, b=b: e.dma_start(
            out=wtmp[b][:], in_=W2_d[fc * 128:(fc + 1) * 128, :]),
            waits=[wst[b]], sig=f"dWc{b}", inc=16)
        wst[b] = P.emit("gpsimd", lambda e, fc=fc, b=b: e.dma_start(
            out=w2_scr[fc * 128:(fc + 1) * 128, :], in_=wtmp[b][:]),
            waits=[tin], sig=f"dWs{b}", inc=16)
    tok_w12 = P.join("sync", [wst[0], wst[1]])

    def ln_stats(xap, width, mu_ap, r_ap, nmr_ap, stg, wait_toks, war_tok):
        t1 = P.emit("vector", lambda e: e.tensor_reduce(
            stg[:, 0:1], xap, AX.X, ALU.add),
            waits=list(wait_toks) + [war_tok], sig="vector")
        t2 = P.emit("scalar", lambda e: e.activation(
            scrap[:, 0:width], xap, AF.Square, accum_out=stg[:, 1:2]),
            waits=list(wait_toks) + [war_tok], sig="scalar")
        t3 = P.emit("vector", lambda e: e.tensor_scalar(
            mu_ap, stg[:, 0:1], 1.0 / width, None, ALU.mult),
            waits=[t1], sig="vector")
        t4 = P.emit("vector", lambda e: e.tensor_scalar(
            stg[:, 2:3], stg[:, 1:2], 1.0 / width, None, ALU.mult),
            waits=[t2], sig="vector")
        t5 = P.emit("vector", lambda e: e.tensor_tensor(
            stg[:, 3:4], mu_ap, mu_ap, ALU.mult), waits=[t3], sig="vector")
        t6 = P.emit("vector", lambda e: e.tensor_tensor(
            stg[:, 2:3], stg[:, 2:3], stg[:, 3:4], ALU.subtract),
            waits=[t4, t5], sig="vector")
        t7 = P.emit("scalar", lambda e: e.activation(
            stg[:, 4:5], stg[:, 2:3], AF.Ln, bias=f32(EPS)),
            waits=[t6], sig="scalar")
        t8 = P.emit("scalar", lambda e: e.activation(
            r_ap, stg[:, 4:5], AF.Exp, scale=f32(-0.5)), waits=[t7],
            sig="scalar")
        t9 = P.emit("vector", lambda e: e.tensor_scalar(
            nmr_ap, mu_ap, r_ap, -1.0, ALU.mult, ALU.mult),
            waits=[t8, t3], sig="vector")
        return t9

    st_x1fm = ExitStack()
    xh1_fm = st_x1fm.enter_context(
        nc.sbuf_tensor("xh1_fm", [128, DC * T], BF16, side="right")).ap()
    xh1_3d = xh1_fm.rearrange("p (c t) -> p c t", t=T)
    st_p1a = ExitStack()
    x0b = [st_p1a.enter_context(
        nc.sbuf_tensor(f"x0b{i}", [128, D], F32, side="right")).ap()
        for i in range(2)]
    xhb = [st_p1a.enter_context(
        nc.sbuf_tensor(f"xhb{i}", [128, D], BF16, side="right")).ap()
        for i in range(2)]

    x0_ld = [None] * 2
    apply_tok = [None] * 2
    fm_evac = None
    war1 = None
    war1p = [None] * 2
    for tt in range(NT):
        b = tt % 2
        x0_ld[b] = P.emit("sync", lambda e, tt=tt, b=b: e.dma_start(
            out=x0b[b][:], in_=x0_d[tt * 128:(tt + 1) * 128, :]),
            waits=[apply_tok[b]], sig=f"dX{b}", inc=16)
        war1p[b] = ln_stats(x0b[b][:], D, mu1[:, tt:tt + 1], r1[:, tt:tt + 1],
                            nmr1[:, tt:tt + 1], stg1[:, b * 5:b * 5 + 5],
                            [x0_ld[b]], war1p[b])
        war1 = war1p[b]
        tap = P.emit("scalar", lambda e, tt=tt, b=b: e.activation(
            xhb[b][:], x0b[b][:], AF.Identity,
            bias=nmr1[:, tt:tt + 1], scale=r1[:, tt:tt + 1]),
            waits=[war1], sig="scalar")
        apply_tok[b] = tap
        tp = None
        for j in range(DC):
            tp = P.emit("tensor", lambda e, b=b, j=j: e.transpose(
                PT[:, j * 128:(j + 1) * 128], xhb[b][:, j * 128:(j + 1) * 128],
                ident), waits=[tap, fm_evac, tok_consts], sig="tensor")
        fm_evac = P.emit("scalar", lambda e, tt=tt: e.copy(
            xh1_3d[:, :, tt * 128:(tt + 1) * 128],
            PT[:].rearrange("p (c t) -> p c t", t=128)), waits=[tp], sig="scalar")
    tok_p1a = fm_evac

    if level <= 2:
        fins = []
        fins.append(P.emit("sync", lambda e: e.dma_start(
            out=out_d[0:128, 0:NT], in_=mu1[:]), waits=[tok_p1a], sig="dDmp", inc=16))
        fins.append(P.emit("sync", lambda e: e.dma_start(
            out=out_d[128:256, 0:NT], in_=r1[:]), waits=[tok_p1a], sig="dDmp", inc=16))
        fins.append(P.emit("sync", lambda e: e.dma_start(
            out=out_d[256:384, 0:NT], in_=nmr1[:]), waits=[tok_p1a], sig="dDmp", inc=16))
        fins.append(P.emit("gpsimd", lambda e: e.dma_start(
            out=out_d[384:896, :].rearrange("(p f) o -> p (f o)", p=128),
            in_=xh1_fm[:, 0:T]), waits=[tok_p1a], sig="dDmp", inc=16))
        P.join("sync", fins)
        P.replay()
        return nc

    st_p1b = ExitStack()

    def sbr(name, shape, dt):
        return st_p1b.enter_context(
            nc.sbuf_tensor(name, shape, dt, side="right")).ap()

    masks = sbr("masks", [128, 6 * 768], BF16)
    q_fm = sbr("q_fm", [128, T], BF16)
    k_fm = sbr("k_fm", [128, TK], BF16)
    v_fm = sbr("v_fm", [128, T], BF16)
    v_tm = sbr("v_tm", [128, TK], BF16)
    wsl = sbr("wsl", [128, 3 * D], BF16)
    p_sb = [sbr(f"p_sb{i}", [128, 2 * 768], BF16) for i in range(2)]
    pT_sb = sbr("pT_sb", [128, 2 * 768], BF16)

    P.emit("sync", lambda e: e.dma_start(out=masks, in_=masks_d), sig="dMsk", inc=16)
    P.emit("vector", lambda e: e.memset(k_fm[:, 0:KPAD], 0.0), sig="vector")
    P.emit("vector", lambda e: e.memset(k_fm[:, T + KPAD:TK], 0.0), sig="vector")
    P.emit("vector", lambda e: e.memset(v_tm[:, 0:KPAD], 0.0), sig="vector")
    P.emit("vector", lambda e: e.memset(pT_sb[:, 5 * 256:5 * 256 + 128], 0.0),
           sig="vector")
    P.emit("vector", lambda e: e.memset(pT_sb[:, 128:256], 0.0), sig="vector")
    tok_pad = P.emit("vector", lambda e: e.memset(v_tm[:, T + KPAD:TK], 0.0),
                     sig="vector")
    tok_mask = ("dMsk", P.cnt["dMsk"])

    wsl_free = None
    attn_prev = None
    ev = {}

    for h in range(NH if level >= 4 else 1):
        wtoks = []
        for wi, W_d in enumerate((Wq_d, Wk_d, Wv_d)):
            wtoks.append(P.emit("gpsimd", lambda e, wi=wi, W_d=W_d, h=h: e.dma_start(
                out=wsl.rearrange("p (w c o) -> p w c o", w=3, c=DC)[:, wi],
                in_=W_d.rearrange("(c p) o -> p c o", p=128)[
                    :, :, h * 128:(h + 1) * 128],
                allow_slow_non_contiguous=True),
                waits=[wsl_free], sig="dWsl", inc=16))
        wsl_ld = wtoks[-1]

        for wofs, dst, bias_s, base in ((0, q_fm, bq_s, 0),
                                        (D, k_fm, bk_s, KPAD),
                                        (2 * D, v_fm, bv_s, 0)):
            for g in range(NG):
                pn = ("PA", PA) if g % 2 == 0 else ("PB", PB)
                mm = None
                for fc in range(DC):
                    mm = P.emit("tensor", lambda e, fc=fc, g=g, pn=pn, wofs=wofs: e.matmul(
                        pn[1][:, :],
                        wsl[:, wofs + fc * 128:wofs + (fc + 1) * 128],
                        xh1_fm[:, fc * T + g * TG:fc * T + (g + 1) * TG],
                        start=(fc == 0), stop=(fc == DC - 1)),
                        waits=[wsl_ld, tok_p1a, ev.get(pn[0]), attn_prev],
                        sig="tensor")
                ev[pn[0]] = P.emit("scalar", lambda e, g=g, pn=pn, dst=dst,
                                   bias_s=bias_s, base=base, h=h: e.activation(
                    dst[:, base + g * TG:base + (g + 1) * TG], pn[1][:, :],
                    AF.Identity, bias=bias_s[:, h:h + 1]),
                    waits=[mm], sig="scalar")
        vev = (ev.get("PA"), ev.get("PB"))
        pt_prev2 = None
        for t8 in range(NT // 8):
            tp = None
            for j in range(8):
                tt = t8 * 8 + j
                tp = P.emit("tensor", lambda e, tt=tt, j=j: e.transpose(
                    PT[:, j * 128:(j + 1) * 128],
                    v_fm[:, tt * 128:(tt + 1) * 128], ident),
                    waits=[vev[0], vev[1], pt_prev2, tok_consts], sig="tensor")
            pt_prev2 = P.emit("scalar", lambda e, t8=t8: e.copy(
                v_tm[:, KPAD + t8 * 1024:KPAD + (t8 + 1) * 1024], PT[:]),
                waits=[tp], sig="scalar")
        qk_done = (ev.get("PA"), ev.get("PB"))
        v_done = (pt_prev2, pt_prev2)
        wsl_free = P.join("scalar", list(qk_done) + list(v_done))

        ps_free = {0: None, 1: None}
        pt_free = v_done[0]
        po_free = {0: None, 1: None}
        pend = None
        tails_done = []

        def emit_tail(ctx):
            nonlocal pt_free
            par, qbase, vblk, pn_toks = ctx
            ev_toks = []
            for qs in range(2):
                kc0 = qs
                tp = None
                for i5, kc in enumerate(range(kc0, kc0 + 5)):
                    tp = P.emit("tensor", lambda e, par=par, qs=qs, kc=kc,
                                i5=i5: e.transpose(
                        PT[:, i5 * 128:(i5 + 1) * 128],
                        p_sb[par][:, qs * 768 + kc * 128:qs * 768 + (kc + 1) * 128],
                        ident), waits=[pn_toks[qs], pt_free], sig="tensor")
                pt_free = P.emit("scalar", lambda e, qs=qs, kc0=kc0: e.copy(
                    pT_sb.rearrange("p (kc q) -> p kc q", q=256)[
                        :, kc0:kc0 + 5, qs * 128:(qs + 1) * 128],
                    PT[:, 0:640].rearrange("p (kc q) -> p kc q", q=128)),
                    waits=[tp], sig="scalar")
                ev_toks.append(pt_free)
            mm = None
            for kc in range(6):
                mm = P.emit("tensor", lambda e, kc=kc, vblk=vblk: e.matmul(
                    PO2[:, :],
                    v_tm[:, (vblk + kc) * 128:(vblk + kc + 1) * 128],
                    pT_sb[:, kc * 256:(kc + 1) * 256],
                    start=(kc == 0), stop=(kc == 5)),
                    waits=[ev_toks[0], ev_toks[1], po_free[0],
                           v_done[0], v_done[1]], sig="tensor")
            po_free[0] = P.emit("scalar", lambda e, h=h, qbase=qbase: e.copy(
                o_fm[h][:, qbase:qbase + 256], PO2[:, :]),
                waits=[mm], sig="scalar")
            tails_done.append(po_free[0])

        idx = 0
        for bb in range(BLOC):
            for c in range(NCH):
                var = 0 if c == 0 else (2 if c == NCH - 1 else 1)
                qbase = bb * SEQ + c * WIN
                wbase = bb * SEQ + c * WIN
                vblk = wbase // 128
                par = idx % 2
                exp_toks = []
                for qs in range(2):
                    psc = PS_e if qs == 0 else PS_o
                    nzc = MASK_NZ[(var, qs)]
                    P.emit("tensor", lambda e, qbase=qbase, qs=qs, wbase=wbase,
                           psc=psc: e.matmul(
                        psc[:, 0:512],
                        q_fm[:, qbase + qs * 128:qbase + (qs + 1) * 128],
                        k_fm[:, wbase:wbase + 512],
                        start=True, stop=False),
                        waits=[qk_done[0], qk_done[1], ps_free[qs], tok_pad,
                               tok_mask], sig="tensor")
                    mlast = P.emit("tensor", lambda e, qbase=qbase, qs=qs,
                                   wbase=wbase, psc=psc: e.matmul(
                        psc[:, 512:768],
                        q_fm[:, qbase + qs * 128:qbase + (qs + 1) * 128],
                        k_fm[:, wbase + 512:wbase + 768],
                        start=True, stop=False), sig="tensor")
                    for kc in nzc:
                        mlast = P.emit("tensor", lambda e, var=var, qs=qs,
                                       kc=kc, psc=psc: e.matmul(
                            psc[:, kc * 128:(kc + 1) * 128], ident,
                            masks[:, (var * 2 + qs) * 768 + kc * 128:
                                  (var * 2 + qs) * 768 + (kc + 1) * 128],
                            start=False, stop=True), sig="tensor")
                    col = par * 2 + qs
                    t1 = P.emit("vector", lambda e, col=col, psc=psc: e.tensor_reduce(
                        rm_c[:, col:col + 1], psc[:, :], AX.X, ALU.max),
                        waits=[mlast], sig="vector")
                    t2n = P.emit("vector", lambda e, col=col: e.tensor_scalar(
                        rm_c[:, col:col + 1], rm_c[:, col:col + 1], -SCALE,
                        None, ALU.mult), waits=[t1], sig="vector")
                    t3 = P.emit("scalar", lambda e, par=par, qs=qs, col=col,
                                psc=psc: e.activation(
                        p_sb[par][:, qs * 768:(qs + 1) * 768], psc[:, :],
                        AF.Exp, bias=rm_c[:, col:col + 1], scale=f32(SCALE),
                        accum_out=sums_c[:, col:col + 1]),
                        waits=[t2n], sig="scalar")
                    exp_toks.append(t3)
                    ps_free[qs] = t3
                tln = P.emit("scalar", lambda e, par=par: e.activation(
                    recip_c[:, par * 2:par * 2 + 2], sums_c[:, par * 2:par * 2 + 2],
                    AF.Ln), waits=exp_toks, sig="scalar")
                t5 = P.emit("scalar", lambda e, par=par: e.activation(
                    recip_c[:, par * 2:par * 2 + 2], recip_c[:, par * 2:par * 2 + 2],
                    AF.Exp, scale=f32(-1.0)), waits=[tln], sig="scalar")
                pn_toks = []
                for qs in range(2):
                    pn_toks.append(P.emit("vector", lambda e, par=par, qs=qs: e.tensor_scalar(
                        p_sb[par][:, qs * 768:(qs + 1) * 768],
                        p_sb[par][:, qs * 768:(qs + 1) * 768],
                        recip_c[:, par * 2 + qs:par * 2 + qs + 1], None,
                        ALU.mult), waits=[t5], sig="vector"))
                if pend is not None:
                    emit_tail(pend)
                pend = (par, qbase, vblk, pn_toks)
                idx += 1
        emit_tail(pend)
        pend = None
        attn_prev = P.join("scalar", tails_done[-2:] +
                           [ps_free[0], ps_free[1]])
    tok_attn = attn_prev

    if level <= 4:
        fins = []
        for h in range(NH if level >= 4 else 1):
            fins.append(P.emit("gpsimd", lambda e, h=h: e.dma_start(
                out=out_d[h * 512:(h + 1) * 512, :].rearrange(
                    "(p f) o -> p (f o)", p=128),
                in_=o_fm[h][:, :]), waits=[tok_attn], sig="dDmp", inc=16))
        P.join("sync", fins)
        P.replay()
        return nc

    st_p1b.close()
    st_p1a.close()
    st_x1fm.close()
    P.barrier()

    st_p1c = ExitStack()

    def sbc(name, shape, dt):
        return st_p1c.enter_context(
            nc.sbuf_tensor(name, shape, dt, side="right")).ap()

    gam_b = sbc("gam_b", [128, D], F32)
    bet_b = sbc("bet_b", [128, D], F32)
    b2_b = sbc("b2_b", [128, D], F32)
    bo_b = sbc("bo_b", [128, D], F32)
    Wo_s = sbc("Wo_s", [128, DC * D], BF16)
    w1b = [sbc(f"w1b{i}", [128, D], BF16) for i in range(2)]
    w2b = [sbc(f"w2b{i}", [128, 512], BF16) for i in range(2)]
    h_fm = [sbc(f"h_fm{i}", [128, (FC // 2) * TG], BF16) for i in range(2)]
    x0c = sbc("x0c", [128, 4 * D], F32)
    t1c = sbc("t1c", [128, D], F32)
    xhc = sbc("xhc", [128, D], F32)
    xhbf = sbc("xhbf", [128, D], BF16)
    t2tm = sbc("t2tm", [128, 4 * D], F32)
    t2fm = sbc("t2fm", [128, DC * TG], BF16)
    t2_3d = t2fm.rearrange("p (c t) -> p c t", t=TG)
    w2e = sbc("w2e", [128, 512], F32)

    bct = None
    rows_ld = []
    for i, src_d in enumerate((gamma_d, beta_d, b2_d, bo_row_d)):
        rows_ld.append(P.emit("sync", lambda e, i=i, src_d=src_d: e.dma_start(
            out=x0c[0:1, i * D:(i + 1) * D], in_=src_d), sig="dRow", inc=16))
    for i, dst in enumerate((gam_b, bet_b, b2_b, bo_b)):
        for half in range(2):
            m = P.emit("tensor", lambda e, i=i, half=half: e.matmul(
                PA[:, :], ones_r[:, :],
                x0c[0:1, i * D + half * 512:i * D + (half + 1) * 512],
                start=True, stop=True),
                waits=[tok_consts, rows_ld[i], bct], sig="tensor")
            bct = P.emit("vector", lambda e, dst=dst, half=half: e.tensor_copy(
                dst[:, half * 512:(half + 1) * 512], PA[:, :]),
                waits=[m], sig="vector")
    tok_bcast = bct

    wo_tok = None
    for ccc in range(DC):
        wo_tok = P.emit("gpsimd", lambda e, ccc=ccc: e.dma_start(
            out=Wo_s[:, ccc * D:(ccc + 1) * D],
            in_=Wo_d[ccc * 128:(ccc + 1) * 128, :]), sig="dWoL", inc=16)

    evt = {"PA": tok_bcast, "PB": None, "PC": None, "PD": None}
    x0_free = [None] * 4
    t1_free = None
    xh_free = None
    t2fm_evac = None
    h_free = None
    w1_free = [None, None]
    w2_free = [None, None]
    war_i = None
    war_2 = None
    x1_st = [None] * 4

    for g in range(NG):
        t2_done = []
        x0_ld2 = [None] * 4
        x0acc = [None] * 4
        for ti in range(4):
            tt = g * 4 + ti
            bx = ti % 2
            x0_ld2[ti] = P.emit("sync", lambda e, tt=tt, ti=ti: e.dma_start(
                out=x0c[:, ti * D:(ti + 1) * D],
                in_=x0_d[tt * 128:(tt + 1) * 128, :]),
                waits=[x0_free[ti], tok_bcast], sig=f"dxc{ti}", inc=16)
            for half, pname, pp in ((0, "PA", PA), (1, "PB", PB)):
                mm = None
                for hc in range(DC):
                    mm = P.emit("tensor", lambda e, tt=tt, hc=hc, half=half,
                                pp=pp: e.matmul(
                        pp[:, :],
                        o_fm[hc][:, tt * 128:(tt + 1) * 128],
                        Wo_s[:, hc * D + half * 512:hc * D + (half + 1) * 512],
                        start=(hc == 0), stop=(hc == DC - 1)),
                        waits=[wo_tok, tok_attn, evt[pname], tok_bcast],
                        sig="tensor")
                evt[pname] = P.emit("vector", lambda e, half=half,
                                    pp=pp: e.tensor_tensor(
                    t1c[:, half * 512:(half + 1) * 512], pp[:, :],
                    bo_b[:, half * 512:(half + 1) * 512], ALU.add),
                    waits=[mm, t1_free], sig="vector")
            ta = P.emit("scalar", lambda e, tt=tt, ti=ti: e.activation(
                xhc[:], x0c[:, ti * D:(ti + 1) * D], AF.Identity,
                bias=nmr1[:, tt:tt + 1], scale=r1[:, tt:tt + 1]),
                waits=[x0_ld2[ti], xh_free], sig="scalar")
            tb = P.emit("vector", lambda e: e.tensor_tensor(
                t1c[:], t1c[:], xhc[:], ALU.add),
                waits=[ta, evt["PA"], evt["PB"]], sig="vector")
            if level <= 5 and g == 7:
                P.emit("gpsimd", lambda e, ti=ti: e.dma_start(
                    out=dbg_scr[ti * 128:(ti + 1) * 128, :], in_=t1c[:]),
                    waits=[tb], sig="dDbg", inc=16)
            war_i = ln_stats(t1c[:], D, lni[:, 0:1], lni[:, 1:2],
                             lni[:, 2:3], stg1,
                             [tb, ("dDbg", P.cnt.get("dDbg", 0))], war_i)
            tc = P.emit("scalar", lambda e: e.activation(
                xhc[:], t1c[:], AF.Identity,
                bias=lni[:, 2:3], scale=lni[:, 1:2]), waits=[war_i],
                sig="scalar")
            t1_free = tc
            td = P.emit("vector", lambda e, ti=ti: e.tensor_tensor(
                t2tm[:, ti * D:(ti + 1) * D], xhc[:], gam_b[:], ALU.mult),
                waits=[tc], sig="vector")
            te = P.emit("vector", lambda e, ti=ti: e.tensor_tensor(
                t2tm[:, ti * D:(ti + 1) * D], t2tm[:, ti * D:(ti + 1) * D],
                bet_b[:], ALU.add), sig="vector")
            xh_free = td
            if level <= 5 and g == 7:
                P.emit("gpsimd", lambda e, ti=ti: e.dma_start(
                    out=dbg_scr[512 + ti * 128:512 + (ti + 1) * 128, :],
                    in_=t2tm[:, ti * D:(ti + 1) * D]), waits=[te],
                    sig="dDbg", inc=16)
            tf = P.emit("scalar", lambda e, ti=ti: e.copy(
                xhbf[:], t2tm[:, ti * D:(ti + 1) * D]),
                waits=[te], sig="scalar")
            tp = None
            for j in range(DC):
                tp = P.emit("tensor", lambda e, j=j: e.transpose(
                    PT[:, j * 128:(j + 1) * 128], xhbf[:, j * 128:(j + 1) * 128],
                    ident), waits=[tf, t2fm_evac], sig="tensor")
            t2fm_evac = P.emit("scalar", lambda e, ti=ti: e.copy(
                t2_3d[:, :, ti * 128:(ti + 1) * 128],
                PT[:].rearrange("p (c t) -> p c t", t=128)),
                waits=[tp], sig="scalar")
            t2_done.append(t2fm_evac)
            x0acc[ti] = P.emit("vector", lambda e, ti=ti: e.tensor_tensor(
                x0c[:, ti * D:(ti + 1) * D], x0c[:, ti * D:(ti + 1) * D],
                t2tm[:, ti * D:(ti + 1) * D], ALU.add),
                waits=[te, ta, x0_ld2[ti], ("dDbg", P.cnt.get("dDbg", 0))],
                sig="vector")

        for oc in range(FC):
            bw = oc % 2
            w1_ld = P.emit("sync", lambda e, oc=oc, bw=bw: e.dma_start(
                out=w1b[bw][:], in_=w1_scr[oc]),
                waits=[w1_free[bw], tok_w12], sig=f"dw1b{bw}", inc=16)
            pn = ("PC", PC) if oc % 2 == 0 else ("PD", PD)
            mm = None
            for fc in range(DC):
                mm = P.emit("tensor", lambda e, fc=fc, pn=pn, bw=bw: e.matmul(
                    pn[1][:, :], w1b[bw][:, fc * 128:(fc + 1) * 128],
                    t2fm[:, fc * TG:(fc + 1) * TG],
                    start=(fc == 0), stop=(fc == DC - 1)),
                    waits=[w1_ld, t2_done[-1], evt[pn[0]], h_free], sig="tensor")
            w1_free[bw] = mm
            evt[pn[0]] = P.emit("scalar", lambda e, oc=oc, pn=pn: e.activation(
                h_fm[oc // 16][:, (oc % 16) * TG:(oc % 16 + 1) * TG],
                pn[1][:, :], AF.Gelu, bias=b1_s[:, oc:oc + 1]),
                waits=[mm], sig="scalar")
        h_done = (evt["PC"], evt["PD"])

        PX = (("PC", PC), ("PD", PD), ("PA", PA), ("PB", PB))
        x0add = [None] * 4
        for oh in range(2):
            mm_t = [None] * 4
            for fc in range(FC):
                bw = fc % 2
                w2_ld = P.emit("sync", lambda e, fc=fc, oh=oh, bw=bw: e.dma_start(
                    out=w2b[bw][:],
                    in_=w2_scr[fc * 128:(fc + 1) * 128, oh * 512:(oh + 1) * 512]),
                    waits=[w2_free[bw], tok_w12], sig=f"dw2b{bw}", inc=16)
                for ti in range(4):
                    mm_t[ti] = P.emit("tensor", lambda e, fc=fc, ti=ti, oh=oh,
                                      bw=bw, PXi=PX[ti][1]: e.matmul(
                        PXi[:, :],
                        h_fm[fc // 16][:, (fc % 16) * TG + ti * 128:
                                       (fc % 16) * TG + (ti + 1) * 128],
                        w2b[bw][:],
                        start=(fc == 0), stop=(fc == FC - 1)),
                        waits=[w2_ld, h_done[0], h_done[1],
                               evt[PX[ti][0]] if fc == 0 else None],
                        sig="tensor")
                w2_free[bw] = mm_t[3]
            for ti in range(4):
                ew = P.emit("vector", lambda e, ti=ti, oh=oh, PXi=PX[ti][1]: e.tensor_tensor(
                    x0c[:, ti * D + oh * 512:ti * D + (oh + 1) * 512],
                    x0c[:, ti * D + oh * 512:ti * D + (oh + 1) * 512],
                    PXi[:, :], ALU.add), waits=[mm_t[ti], x0acc[ti]], sig="vector")
                evt[PX[ti][0]] = ew
                x0add[ti] = P.emit("vector", lambda e, ti=ti, oh=oh: e.tensor_tensor(
                    x0c[:, ti * D + oh * 512:ti * D + (oh + 1) * 512],
                    x0c[:, ti * D + oh * 512:ti * D + (oh + 1) * 512],
                    b2_b[:, oh * 512:(oh + 1) * 512], ALU.add),
                    waits=[ew], sig="vector")
        h_free = w2_free[1]
        for ti in range(4):
            tt = g * 4 + ti
            tg2 = x0add[ti]
            war_2 = ln_stats(x0c[:, ti * D:(ti + 1) * D], D, mu2[:, tt:tt + 1],
                             r2[:, tt:tt + 1], nmr2[:, tt:tt + 1], stg2,
                             [tg2], war_2)
            tmx = P.emit("vector", lambda e, ti=ti: e.tensor_reduce(
                gtmp[:, 0:1], x0c[:, ti * D:(ti + 1) * D], AX.X, ALU.max),
                waits=[tg2], sig="vector")
            tmn = P.emit("vector", lambda e, ti=ti: e.tensor_reduce(
                gtmp[:, 1:2], x0c[:, ti * D:(ti + 1) * D], AX.X, ALU.min),
                sig="vector")
            u1 = P.emit("vector", lambda e, tt=tt: e.tensor_tensor(
                gtmp[:, 0:1], gtmp[:, 0:1], mu2[:, tt:tt + 1], ALU.subtract),
                waits=[war_2, tmx], sig="vector")
            u2 = P.emit("vector", lambda e, tt=tt: e.tensor_tensor(
                gtmp[:, 1:2], mu2[:, tt:tt + 1], gtmp[:, 1:2], ALU.subtract),
                waits=[tmn], sig="vector")
            u3 = P.emit("vector", lambda e: e.tensor_tensor(
                gtmp[:, 0:1], gtmp[:, 0:1], gtmp[:, 1:2], ALU.max),
                waits=[u1, u2], sig="vector")
            u4 = P.emit("vector", lambda e, tt=tt: e.tensor_tensor(
                gtmp[:, 0:1], gtmp[:, 0:1], r2[:, tt:tt + 1], ALU.mult),
                waits=[u3], sig="vector")
            u5 = P.emit("vector", lambda e: e.tensor_tensor(
                gmv[:], gmv[:], gtmp[:, 0:1], ALU.max),
                waits=[t_gm0, u4], sig="vector")
            js = P.join("scalar", [u5, tg2])
            x1_st[ti] = P.emit("gpsimd", lambda e, tt=tt, ti=ti: e.dma_start(
                out=x1_scr[tt * 128:(tt + 1) * 128, :],
                in_=x0c[:, ti * D:(ti + 1) * D]),
                waits=[u5, tg2, js], sig=f"dx1s{ti}", inc=16)
            x0_free[ti] = x1_st[ti]
            if level <= 5 and g == 7:
                x0_free[ti] = P.emit("sync", lambda e, ti=ti: e.dma_start(
                    out=dbg_scr[ti * 128:(ti + 1) * 128, :],
                    in_=x0c[:, ti * D:(ti + 1) * D]),
                    waits=[x1_st[ti]], sig="dDbg2", inc=16)

    tok_x1 = P.join("sync", x1_st)

    if level <= 5:
        fin = P.emit("sync", lambda e: e.dma_start(
            out=out_d, in_=x1_scr), waits=[tok_x1], sig="dDmp", inc=16)
        P.join("sync", [fin])
        P.replay()
        return nc

    st_p1c.close()
    st_ofm.close()
    st_setup.close()
    P.barrier()

    st_wq = ExitStack()
    wdT = st_wq.enter_context(
        nc.sbuf_tensor("wdT", [128, IC * D], BF16, side="left")).ap()
    wgT = [st_wq.enter_context(
        nc.sbuf_tensor(f"wgT{i}", [128, II], BF16, side="left")).ap()
        for i in range(DC)]
    wuT = [st_wq.enter_context(
        nc.sbuf_tensor(f"wuT{i}", [128, II], BF16, side="left")).ap()
        for i in range(DC)]
    st_wql = ExitStack()
    wqld0 = st_wql.enter_context(
        nc.sbuf_tensor("wqld0", [128, II], F32, side="right")).ap()
    wqq0 = st_wql.enter_context(
        nc.sbuf_tensor("wqq0", [128, II], BF16, side="right")).ap()
    wqi0 = st_wql.enter_context(
        nc.sbuf_tensor("wqi0", [128, 1024], mybir.dt.int32, side="right")).ap()
    wq_ld = [wqld0, wqld0]
    wq_q = [wqq0, wqq0]
    wqi = [wqi0, wqi0]
    qw_ld_tok = [None] * 2
    qw_q_tok = [None] * 2

    def quant_weight(src_d, n_tiles, width, dstf, pt_prev, PTX=PT, slots=2,
                     alt=False):
        ld_tok = qw_ld_tok
        q_tok = qw_q_tok
        tp_done = pt_prev
        tp_last = pt_prev
        ptx_done = {0: pt_prev, 1: pt_prev}
        for t in range(n_tiles):
            b = 0
            k = t % 2 if alt else 0
            PTXt = (PT, PT2)[k] if alt else PTX
            ld_tok[b] = P.emit("sync", lambda e, t=t, b=b: e.dma_start(
                out=wq_ld[b][:, 0:width], in_=src_d[t * 128:(t + 1) * 128, :]),
                waits=[q_tok[b], tp_last if alt else tp_done],
                sig=f"dQl{b}", inc=16)
            t1 = P.emit("vector", lambda e, b=b, width=width: e.tensor_reduce(
                rmx[:, b:b + 1], wq_ld[b][:, 0:width], AX.X, ALU.max,
                apply_absolute_value=True), waits=[ld_tok[b]], sig="vector")
            t2 = P.emit("scalar", lambda e, b=b: e.activation(
                rsc[:, b:b + 1], rmx[:, b:b + 1], AF.Ln), waits=[t1],
                sig="scalar")
            t3 = P.emit("scalar", lambda e, b=b: e.activation(
                rsc[:, b:b + 1], rsc[:, b:b + 1], AF.Exp,
                scale=f32(-1.0), bias=f32(LN127)), waits=[t2], sig="scalar")
            t4 = P.emit("vector", lambda e, b=b: e.tensor_scalar(
                ssc[:, b:b + 1], rmx[:, b:b + 1], 1.0 / 127.0, None, ALU.mult),
                waits=[t1], sig="vector")
            P.emit("vector", lambda e, b=b, width=width: e.tensor_scalar(
                wq_ld[b][:, 0:width], wq_ld[b][:, 0:width],
                rsc[:, b:b + 1], -(QMAX + 1.0), ALU.mult, ALU.max),
                waits=[t3], sig="vector")
            for c0 in range(0, width, 1024):
                cw = min(1024, width - c0)
                P.emit("vector", lambda e, b=b, c0=c0, cw=cw: e.tensor_scalar(
                    wqi[b][:, 0:cw], wq_ld[b][:, c0:c0 + cw],
                    QMAX, None, ALU.min), sig="vector")
                q_tok[b] = P.emit("vector", lambda e, b=b, c0=c0,
                                  cw=cw: e.tensor_scalar(
                    wq_q[b][:, c0:c0 + cw], wqi[b][:, 0:cw],
                    ssc[:, b:b + 1], None, ALU.mult),
                    waits=[t4], sig="vector")
            nj = width // 128
            for j0 in range(0, nj, 8):
                jn = min(8, nj - j0)
                tp = None
                for j in range(jn):
                    tp = P.emit("tensor", lambda e, b=b, j0=j0, j=j,
                                PTX=PTXt: e.transpose(
                        PTX[:, j * 128:(j + 1) * 128],
                        wq_q[b][:, (j0 + j) * 128:(j0 + j + 1) * 128], ident),
                        waits=[q_tok[b],
                               ptx_done[k] if alt else tp_done], sig="tensor")
                tp_last = tp
                for j in range(jn):
                    tp_done = P.emit("scalar", lambda e, t=t, j0=j0, j=j,
                                     dstf=dstf, PTX=PTXt: e.copy(
                        dstf(j0 + j)[:, t * 128:(t + 1) * 128],
                        PTX[:, j * 128:(j + 1) * 128]),
                        waits=[tp], sig="scalar")
                ptx_done[k] = tp_done
        return tp_done

    wd3 = wdT.rearrange("p (c i) -> p c i", i=D)
    tok_wg = quant_weight(wg_d, IC, D, lambda c: wgT[c], None,
                          alt=True)
    tok_wu = quant_weight(wu_d, IC, D, lambda c: wuT[c], tok_wg,
                          alt=True)

    t1 = P.emit("gpsimd", lambda e: e.dma_start(
        out=gmx_scr, in_=gmv[:]), waits=[tok_x1], sig="dGm1", inc=16)
    t2 = P.emit("sync", lambda e: e.dma_start(
        out=grow[:], in_=gmx_scr.rearrange("p one -> one p")),
        waits=[t1], sig="dGr1", inc=16)
    t3 = P.emit("vector", lambda e: e.tensor_reduce(
        srow[:, 0:1], grow[:], AX.X, ALU.max), waits=[t2], sig="vector")
    t4 = P.emit("gpsimd", lambda e: e.dma_start(out=cc1_in, in_=srow[:, 0:1]),
                waits=[t3], sig="dCi1", inc=16)
    t5 = P.emit("gpsimd", lambda e: e.collective_compute(
        "AllReduce", ALU.max, replica_groups=RG8, ins=[cc1_in], outs=[cc1_out]),
        waits=[t4], sig="pool")
    t6 = P.emit("sync", lambda e: e.dma_start(out=srow[:, 0:1], in_=cc1_out),
                waits=[t5], sig="dCo1", inc=16)
    t7 = P.emit("scalar", lambda e: e.activation(
        srow[:, 1:2], clip_s[0:1, 0:1], AF.Sigmoid),
        waits=[tok_consts, t6], sig="scalar")
    t8 = P.emit("vector", lambda e: e.tensor_tensor(
        srow[:, 2:3], srow[:, 0:1], srow[:, 1:2], ALU.mult),
        waits=[t7], sig="vector")
    t9 = P.emit("vector", lambda e: e.tensor_scalar(
        srow[:, 2:3], srow[:, 2:3], 1.0 / 127.0, None, ALU.mult),
        waits=[t8], sig="vector")
    t10 = P.emit("scalar", lambda e: e.activation(
        srow[:, 3:4], srow[:, 2:3], AF.Ln), waits=[t9], sig="scalar")
    t11 = P.emit("scalar", lambda e: e.activation(
        srow[:, 3:4], srow[:, 3:4], AF.Exp, scale=f32(-1.0)), waits=[t10],
        sig="scalar")
    t12 = P.emit("tensor", lambda e: e.matmul(
        PB3[:, 0:2], ones_r[:, :], srow[:, 2:4], start=True, stop=True),
        waits=[t11], sig="tensor")
    tok_s1 = P.emit("vector", lambda e: e.tensor_copy(qsc[:, 0:2], PB3[:, 0:2]),
                    waits=[t12], sig="vector")
    tqa = P.emit("vector", lambda e: e.tensor_scalar(
        qa[:], r2[:], qsc[:, 1:2], None, ALU.mult), waits=[tok_s1], sig="vector")
    tqb = P.emit("vector", lambda e: e.tensor_tensor(
        qb[:], mu2[:], qa[:], ALU.mult), waits=[tqa], sig="vector")
    tok_qc = P.emit("vector", lambda e: e.tensor_scalar(
        qb[:], qb[:], -1.0, None, ALU.mult), waits=[tqb], sig="vector")

    if level <= 6:
        fin = P.emit("sync", lambda e: e.dma_start(
            out=out_d[0:128, 0:4], in_=qsc[:]), waits=[tok_qc, tok_wu],
            sig="dDmp", inc=16)
        P.join("sync", [fin])
        P.replay()
        return nc

    st_p2 = ExitStack()

    def sb2(name, shape, dt):
        return st_p2.enter_context(
            nc.sbuf_tensor(name, shape, dt, side="right")).ap()

    x1b = [sb2(f"x1b{i}", [128, D], F32) for i in range(2)]
    xqb = [sb2(f"xqb{i}", [128, D], F32) for i in range(2)]
    xqbf = [sb2(f"xqbf{i}", [128, D], BF16) for i in range(2)]
    xqi = [sb2(f"xqi{i}", [128, D], mybir.dt.int32) for i in range(2)]
    xq_fm = sb2("xq_fm", [128, DC * TG], BF16)
    xq_3d = xq_fm.rearrange("p (c t) -> p c t", t=TG)
    sg_f = [sb2(f"sg_f{i}", [128, TG], F32) for i in range(2)]
    up_f = [sb2(f"up_f{i}", [128, TG], F32) for i in range(2)]
    ha_bf = [sb2(f"ha_bf{i}", [128, TG], BF16) for i in range(2)]

    x1_ld = [None, None]
    xq_free = None
    fm_ev = None
    xq4_free = [None, None]
    ev2 = {"PA": None, "PB": None, "PC": None, "PD": None}
    ha_free = [None, None]
    q_free = [None, None]

    for g in range(NG):
        fm_toks = []
        for ti in range(4):
            tt = g * 4 + ti
            bx = ti % 2
            x1_ld[bx] = P.emit("sync", lambda e, tt=tt, bx=bx: e.dma_start(
                out=x1b[bx][:], in_=x1_scr[tt * 128:(tt + 1) * 128, :]),
                waits=[q_free[bx], tok_x1], sig=f"dx1b{bx}", inc=16)
            q1 = P.emit("scalar", lambda e, tt=tt, bx=bx: e.activation(
                xqb[bx][:], x1b[bx][:], AF.Identity,
                bias=qb[:, tt:tt + 1], scale=qa[:, tt:tt + 1]),
                waits=[x1_ld[bx], tok_qc, q_free[bx]], sig="scalar")
            q_free[bx] = q1
            P.emit("vector", lambda e, bx=bx: e.tensor_scalar(
                xqb[bx][:], xqb[bx][:], -(QMAX + 1.0), QMAX, ALU.max, ALU.min),
                waits=[q1], sig="vector")
            P.emit("vector", lambda e, bx=bx: e.tensor_scalar(
                xqi[bx][:], xqb[bx][:], 0.0, None, ALU.add),
                waits=[xq4_free[bx]], sig="vector")
            q3c = P.join("vector", [])
            q4 = P.emit("scalar", lambda e, bx=bx: e.activation(
                xqbf[bx][:], xqi[bx][:], AF.Copy, scale=qsc[:, 0:1]),
                waits=[q3c, xq4_free[bx]], sig="scalar")
            xq4_free[bx] = q4
            tp = None
            for j in range(DC):
                tp = P.emit("tensor", lambda e, j=j, bx=bx: e.transpose(
                    PT[:, j * 128:(j + 1) * 128], xqbf[bx][:, j * 128:(j + 1) * 128],
                    ident), waits=[q4, xq_free, fm_ev], sig="tensor")
            fm_ev = P.emit("scalar", lambda e, ti=ti: e.copy(
                xq_3d[:, :, ti * 128:(ti + 1) * 128],
                PT[:].rearrange("p (c t) -> p c t", t=128)),
                waits=[tp], sig="scalar")
            fm_toks.append(fm_ev)
        tok_xq = fm_toks[-1]
        mmu = None
        for icx in range(IC):
            bh = icx % 2
            pg = ("PA", PA) if icx % 2 == 0 else ("PB", PB)
            pu = ("PC", PC) if icx % 2 == 0 else ("PD", PD)
            mmg = None
            for fc in range(DC):
                mmg = P.emit("tensor", lambda e, icx=icx, fc=fc, pg=pg: e.matmul(
                    pg[1][:, :], wgT[fc][:, icx * 128:(icx + 1) * 128],
                    xq_fm[:, fc * TG:(fc + 1) * TG],
                    start=(fc == 0), stop=(fc == DC - 1)),
                    waits=[tok_xq, tok_wg, ev2[pg[0]]], sig="tensor")
            for fc in range(DC):
                mmu = P.emit("tensor", lambda e, icx=icx, fc=fc, pu=pu: e.matmul(
                    pu[1][:, :], wuT[fc][:, icx * 128:(icx + 1) * 128],
                    xq_fm[:, fc * TG:(fc + 1) * TG],
                    start=(fc == 0), stop=(fc == DC - 1)),
                    waits=[tok_wu, ev2[pu[0]]], sig="tensor")
            e1 = P.emit("scalar", lambda e, bh=bh, pg=pg: e.activation(
                sg_f[bh][:], pg[1][:, :], AF.Silu),
                waits=[mmg, ha_free[bh]], sig="scalar")
            ev2[pg[0]] = e1
            e2 = P.emit("scalar", lambda e, bh=bh, pu=pu: e.activation(
                up_f[bh][:], pu[1][:, :], AF.Copy),
                waits=[mmu, ha_free[bh]], sig="scalar")
            ev2[pu[0]] = e2
            e3 = P.emit("vector", lambda e, bh=bh: e.tensor_tensor(
                ha_bf[bh][:], sg_f[bh][:], up_f[bh][:], ALU.mult),
                waits=[e1, e2], sig="vector")
            if level <= 7 and g == 0 and icx == 0:
                P.emit("gpsimd", lambda e, bh=bh: e.dma_start(
                    out=dbg_scr[512:640, :].rearrange(
                        "p o -> p o")[:, 0:TG], in_=ha_bf[bh][:]),
                    waits=[e3], sig="dDbg", inc=16)
                P.emit("gpsimd", lambda e, bh=bh: e.dma_start(
                    out=dbg_scr[640:768, 0:TG], in_=sg_f[bh][:]),
                    waits=[e3], sig="dDbg", inc=16)
                P.emit("gpsimd", lambda e, bh=bh: e.dma_start(
                    out=dbg_scr[768:896, 0:TG], in_=up_f[bh][:]),
                    waits=[e3], sig="dDbg", inc=16)
                P.emit("gpsimd", lambda e: e.dma_start(
                    out=dbg_scr[896:1024, :], in_=wgT[0][:, 0:D]),
                    waits=[tok_wg], sig="dDbg", inc=16)
            if level <= 7 and g == 0 and icx == 2:
                P.emit("gpsimd", lambda e: e.dma_start(
                    out=dbg_scr[0:512, :].rearrange("(a p) o -> p a o", p=128),
                    in_=xq_fm[:].rearrange("p (a o) -> p a o", o=1024)),
                    waits=[tok_xq], sig="dDbg", inc=16)
            e4 = P.emit("vector", lambda e, bh=bh: e.tensor_reduce(
                gtmp[:, 0:1], ha_bf[bh][:], AX.X, ALU.max,
                apply_absolute_value=True), waits=[e3], sig="vector")
            e5 = P.emit("vector", lambda e: e.tensor_tensor(
                gmv2[:], gmv2[:], gtmp[:, 0:1], ALU.max),
                waits=[t_gm20, e4], sig="vector")
            ha_free[bh] = P.emit("gpsimd", lambda e, icx=icx, g=g, bh=bh: e.dma_start(
                out=hact_scr[icx * 128:(icx + 1) * 128, g * TG:(g + 1) * TG],
                in_=ha_bf[bh][:]), waits=[e5], sig=f"dHa{bh}", inc=16)
        xq_free = mmu
    tok_wd = quant_weight(wd_d, DC, II, lambda c: wd3[:, c], tok_wu, PTX=PT2,
                          slots=1)
    tok_p2 = P.join("sync", [ha_free[0], ha_free[1]])

    st_p2.close()
    st_wql.close()
    P.barrier()

    t1 = P.emit("gpsimd", lambda e: e.dma_start(
        out=gmx_scr, in_=gmv2[:]), waits=[tok_p2], sig="dGm2", inc=16)
    t2 = P.emit("sync", lambda e: e.dma_start(
        out=grow[:], in_=gmx_scr.rearrange("p one -> one p")),
        waits=[t1], sig="dGr2", inc=16)
    t3 = P.emit("vector", lambda e: e.tensor_reduce(
        srow[:, 4:5], grow[:], AX.X, ALU.max), waits=[t2], sig="vector")
    t4 = P.emit("gpsimd", lambda e: e.dma_start(out=cc2_in, in_=srow[:, 4:5]),
                waits=[t3], sig="dCi2", inc=16)
    t5 = P.emit("gpsimd", lambda e: e.collective_compute(
        "AllReduce", ALU.max, replica_groups=RG8, ins=[cc2_in], outs=[cc2_out]),
        waits=[t4], sig="pool")
    t6 = P.emit("sync", lambda e: e.dma_start(out=srow[:, 4:5], in_=cc2_out),
                waits=[t5], sig="dCo2", inc=16)
    t7 = P.emit("scalar", lambda e: e.activation(
        srow[:, 5:6], clip_s[0:1, 1:2], AF.Sigmoid), waits=[t6], sig="scalar")
    t8 = P.emit("vector", lambda e: e.tensor_tensor(
        srow[:, 6:7], srow[:, 4:5], srow[:, 5:6], ALU.mult),
        waits=[t7], sig="vector")
    t9 = P.emit("vector", lambda e: e.tensor_scalar(
        srow[:, 6:7], srow[:, 6:7], 1.0 / 127.0, None, ALU.mult),
        waits=[t8], sig="vector")
    t10 = P.emit("scalar", lambda e: e.activation(
        srow[:, 7:8], srow[:, 6:7], AF.Ln), waits=[t9], sig="scalar")
    t11 = P.emit("scalar", lambda e: e.activation(
        srow[:, 7:8], srow[:, 7:8], AF.Exp, scale=f32(-1.0)), waits=[t10],
        sig="scalar")
    t12 = P.emit("tensor", lambda e: e.matmul(
        PB3[:, 0:2], ones_r[:, :], srow[:, 6:8], start=True, stop=True),
        waits=[t11], sig="tensor")
    tok_s2 = P.emit("vector", lambda e: e.tensor_copy(qsc[:, 2:4], PB3[:, 0:2]),
                    waits=[t12], sig="vector")

    if level <= 7:
        fin = P.emit("sync", lambda e: e.dma_start(
            out=out_d[0:128, 0:4], in_=qsc[:]), waits=[tok_s2],
            sig="dDmp", inc=16)
        P.join("sync", [fin])
        P.replay()
        return nc

    st_p3 = ExitStack()

    def sb3(name, shape, dt):
        return st_p3.enter_context(
            nc.sbuf_tensor(name, shape, dt, side="right")).ap()

    ha_ld = [sb3(f"ha_ld{i}", [128, TG], BF16) for i in range(2)]
    hq_fm = sb3("hq_fm", [128, IC * TG], BF16)
    x1p = [sb3(f"x1p{i}", [128, D], F32) for i in range(2)]
    otile = [sb3(f"otile{i}", [128, D], F32) for i in range(2)]
    qstg = sb3("qstg", [128, TG], F32)
    qsti = [sb3(f"qsti{i}", [128, TG], mybir.dt.int32) for i in range(2)]

    ha_tok = [None, None]
    ha_used = [None, None]
    hq3_free = [None, None]
    hq_free = None
    out_st = [None, None]
    ev3 = {"PA": None, "PB": None}
    x1_ld3 = [None, None]
    out_toks = []

    for g in range(NG):
        hq_done = []
        for icx in range(IC):
            bh = icx % 2
            ha_tok[bh] = P.emit("sync", lambda e, icx=icx, g=g, bh=bh: e.dma_start(
                out=ha_ld[bh][:],
                in_=hact_scr[icx * 128:(icx + 1) * 128, g * TG:(g + 1) * TG]),
                waits=[ha_used[bh], tok_p2], sig=f"dHl{bh}", inc=16)
            q1 = P.emit("vector", lambda e, bh=bh: e.tensor_scalar(
                qstg[:], ha_ld[bh][:], qsc[:, 3:4], -(QMAX + 1.0),
                ALU.mult, ALU.max),
                waits=[ha_tok[bh], tok_s2, hq_free, hq3_free[bh]], sig="vector")
            ha_used[bh] = q1
            q2v = P.emit("vector", lambda e, bh=bh: e.tensor_scalar(
                qsti[bh][:], qstg[:], QMAX, None, ALU.min), sig="vector")
            q3s = P.emit("scalar", lambda e, icx=icx, bh=bh: e.activation(
                hq_fm[:, icx * TG:(icx + 1) * TG], qsti[bh][:],
                AF.Copy, scale=qsc[:, 2:3]), waits=[q2v], sig="scalar")
            hq3_free[bh] = q3s
            hq_done.append(q3s)
        for ti in range(4):
            tt = g * 4 + ti
            bx = ti % 2
            x1_ld3[bx] = P.emit("sync", lambda e, tt=tt, bx=bx: e.dma_start(
                out=x1p[bx][:], in_=x1_scr[tt * 128:(tt + 1) * 128, :]),
                waits=[out_st[bx]], sig=f"dxp{bx}", inc=16)
            mm_l = None
            for oh, pname, pp in ((0, "PA", PA), (1, "PB", PB)):
                mm = None
                for icx in range(IC):
                    mm = P.emit("tensor", lambda e, icx=icx, ti=ti, oh=oh,
                                pp=pp: e.matmul(
                        pp[:, :],
                        hq_fm[:, icx * TG + ti * 128:icx * TG + (ti + 1) * 128],
                        wd3[:, icx, oh * 512:(oh + 1) * 512],
                        start=(icx == 0), stop=(icx == IC - 1)),
                        waits=[hq_done[icx], tok_wd, ev3[pname]], sig="tensor")
                mm_l = mm
                ev3[pname] = P.emit("scalar", lambda e, oh=oh, pp=pp,
                                    bx=bx: e.activation(
                    otile[bx][:, oh * 512:(oh + 1) * 512], pp[:, :], AF.Copy),
                    waits=[mm, out_st[bx]], sig="scalar")
            if ti == 3:
                hq_free = mm_l
            o1 = P.emit("vector", lambda e, bx=bx: e.tensor_tensor(
                otile[bx][:], otile[bx][:], x1p[bx][:], ALU.add),
                waits=[ev3["PA"], ev3["PB"], x1_ld3[bx]], sig="vector")
            out_st[bx] = P.emit("sync", lambda e, tt=tt, bx=bx: e.dma_start(
                out=out_d[tt * 128:(tt + 1) * 128, :], in_=otile[bx][:]),
                waits=[o1], sig=f"dOut{bx}", inc=16)
            out_toks.append(out_st[bx])

    P.join("sync", [out_toks[-1], out_toks[-2]])

    P.replay()
    st_wq.close()
    st_p3.close()
    st_ps.close()
    st_ofm_dummy = None
    st_perm.close()
    return nc


def _shard_inputs(inputs):
    x = np.asarray(inputs["L0_attn_x"])

    def lay(v, shape):
        return np.ascontiguousarray(np.asarray(v, np.float32).reshape(shape))

    def chunked(v, n):
        return np.ascontiguousarray(
            np.asarray(v, np.float32).reshape(n, 128).T)

    ins = {
        "Wq": lay(inputs["L0_attn_Wq"], (D, D)),
        "Wk": lay(inputs["L0_attn_Wk"], (D, D)),
        "Wv": lay(inputs["L0_attn_Wv"], (D, D)),
        "Wo": lay(inputs["L0_attn_Wo"], (D, D)),
        "bq": chunked(inputs["L0_attn_bq"], DC),
        "bk": chunked(inputs["L0_attn_bk"], DC),
        "bv": chunked(inputs["L0_attn_bv"], DC),
        "bo_row": lay(inputs["L0_attn_bo"], (1, D)),
        "gamma": lay(inputs["L0_attn_gamma"], (1, D)),
        "beta": lay(inputs["L0_attn_beta"], (1, D)),
        "W1": lay(inputs["L0_attn_W1"], (D, FF)),
        "b1": chunked(inputs["L0_attn_b1"], FC),
        "W2": lay(inputs["L0_attn_W2"], (FF, D)),
        "b2": lay(inputs["L0_attn_b2"], (1, D)),
        "clip_x": lay(inputs["L0_ffn_clip_x"], (1, 1)),
        "clip_down": lay(inputs["L0_ffn_clip_down"], (1, 1)),
        "w_gate": lay(inputs["L0_ffn_w_gate"], (II, D)),
        "w_up": lay(inputs["L0_ffn_w_up"], (II, D)),
        "w_down": lay(inputs["L0_ffn_w_down"], (D, II)),
    }
    in_maps = []
    for c in range(8):
        m = dict(ins)
        m["x0"] = np.ascontiguousarray(
            x[BLOC * c:BLOC * (c + 1)].reshape(T, D), np.float32)
        in_maps.append(m)
    return in_maps


def kernel(**inputs):
    nc = build_program()
    in_maps = _shard_inputs(inputs)
    res = run_bass_kernel_spmd(nc, in_maps, core_ids=list(range(8)))
    out = np.concatenate(
        [np.asarray(r["out"]).reshape(BLOC, SEQ, D) for r in res.results], axis=0)
    return out


if __name__ == "__main__":
    nc = build_program()
    print("built ok; instructions:", nc.n_instructions())


# version 21 (final)
# speedup vs baseline: 1.0327x; 1.0031x over previous
import numpy as np
import ml_dtypes
import concourse.bass as bass
import concourse.mybir as mybir
from concourse.bass_utils import run_bass_kernel_spmd
from contextlib import ExitStack

F32 = mybir.dt.float32
BF16 = mybir.dt.bfloat16
AF = mybir.ActivationFunctionType
ALU = mybir.AluOpType
AX = mybir.AxisListType
f32 = np.float32

D = 1024
FF = 4096
II = 2816
NH = 8
WIN = 256
SEQ = 2048
BLOC = 2
T = BLOC * SEQ
NT = T // 128
TG = 512
NG = T // TG
DC = D // 128
FC = FF // 128
IC = II // 128
NCH = SEQ // WIN
KPAD = WIN
TK = T + 2 * KPAD
SCALE = float(128.0) ** -0.5
EPS = 1e-5
QMAX = 127.0
MAGIC = float(f32(3 * 2 ** 22))
MASKV = -30000.0
LN127 = float(np.log(f32(127.0)))
RG8 = [[0, 1, 2, 3, 4, 5, 6, 7]]

ENGS = ("sync", "tensor", "vector", "scalar", "gpsimd")


def _build_masks():
    kr = np.arange(768)
    qi = np.arange(256)
    band = (kr[None, :] >= qi[:, None]) & (kr[None, :] <= qi[:, None] + 2 * WIN)
    masks = np.zeros((3, 2, 128, 768), np.float32)
    nonzero = {}
    for v in range(3):
        valid = band.copy()
        if v == 0:
            valid = valid & (kr[None, :] >= WIN)
        elif v == 2:
            valid = valid & (kr[None, :] < 2 * WIN)
        add = np.where(valid, 0.0, MASKV).astype(np.float32)
        for qs in range(2):
            masks[v, qs] = add[qs * 128:(qs + 1) * 128]
            nonzero[(v, qs)] = [
                kc for kc in range(6)
                if np.any(masks[v, qs][:, kc * 128:(kc + 1) * 128] != 0.0)]
    return masks.astype(ml_dtypes.bfloat16), nonzero


MASKS_NP, MASK_NZ = _build_masks()


class Prog:

    def __init__(self, nc):
        self.nc = nc
        self.ops = {e: [] for e in ENGS}
        self.sem_h = {}
        self.cnt = {}
        self.waited = {e: {} for e in ENGS}

    def _prep(self, eng, waits, sig, inc):
        nw = []
        for t in waits:
            if t is None:
                continue
            s, v = t
            if self.waited[eng].get(s, 0) >= v:
                continue
            nw.append((s, v))
            self.waited[eng][s] = v
        tok = None
        if sig is not None:
            self.cnt[sig] = self.cnt.get(sig, 0) + inc
            tok = (sig, self.cnt[sig])
        return tuple(nw), tok

    def emit(self, eng, fn, waits=(), sig=None, inc=1):
        nw, tok = self._prep(eng, waits, sig, inc)

        def closure(nw=nw, sig=sig, fn=fn, inc=inc, eng=eng):
            e = getattr(self.nc, eng)
            for (s, v) in nw:
                e.wait_ge(self.sem_h[s], v)
            inst = fn(e)
            if sig is not None:
                inst.then_inc(self.sem_h[sig], inc)

        self.ops[eng].append(closure)
        return tok

    def join(self, eng, waits, sig=None):
        if sig is None:
            sig = eng
        nw, tok = self._prep(eng, waits, sig, 1)

        def closure(nw=nw, sig=sig, eng=eng):
            e = getattr(self.nc, eng)
            for (s, v) in nw:
                e.wait_ge(self.sem_h[s], v)
            e.sem_inc(self.sem_h[sig], 1)

        self.ops[eng].append(closure)
        return tok

    def barrier(self, engs=ENGS):
        dma_toks = [(n, v) for n, v in self.cnt.items() if n.startswith("d")]
        toks = [self.join(e, dma_toks) for e in engs]
        for e in engs:
            nw, _ = self._prep(e, toks, None, 1)

            def closure(nw=nw, eng=e):
                en = getattr(self.nc, eng)
                for (s, v) in nw:
                    en.wait_ge(self.sem_h[s], v)

            self.ops[e].append(closure)

    def replay(self):
        with ExitStack() as st:
            for s in self.cnt:
                self.sem_h[s] = st.enter_context(self.nc.semaphore(s))
            for e in ENGS:
                for s, v in self.waited[e].items():
                    assert self.cnt.get(s, 0) >= v, (e, s, v, self.cnt.get(s))
            with self.nc.Block() as block:
                for e in ENGS:
                    getattr(block, e)(lambda _, e=e: [f() for f in self.ops[e]])


def build_program(level=99):
    nc = bass.Bass("TRN2", target_bir_lowering=False)
    P = Prog(nc)

    def din(name, shape):
        return nc.dram_tensor(name, shape, F32, kind="ExternalInput").ap()

    x0_d = din("x0", [T, D])
    Wq_d, Wk_d, Wv_d, Wo_d = (din(n, [D, D]) for n in ("Wq", "Wk", "Wv", "Wo"))
    bq_d, bk_d, bv_d = (din(n, [128, DC]) for n in ("bq", "bk", "bv"))
    gamma_d = din("gamma", [1, D])
    beta_d = din("beta", [1, D])
    W1_d = din("W1", [D, FF])
    b1_d = din("b1", [128, FC])
    W2_d = din("W2", [FF, D])
    b2_d = din("b2", [1, D])
    bo_row_d = din("bo_row", [1, D])
    clipx_d = din("clip_x", [1, 1])
    clipd_d = din("clip_down", [1, 1])
    wg_d = din("w_gate", [II, D])
    wu_d = din("w_up", [II, D])
    wd_d = din("w_down", [D, II])

    out_d = nc.dram_tensor("out", [T, D], F32, kind="ExternalOutput").ap()

    x1_scr = nc.dram_tensor("x1_scr", [T, D], F32, kind="Internal").ap()
    hact_scr = nc.dram_tensor("hact_scr", [II, T], BF16, kind="Internal").ap()
    w1_scr = nc.dram_tensor("w1_scr", [FC, 128, D], BF16, kind="Internal").ap()
    w2_scr = nc.dram_tensor("w2_scr", [FF, D], BF16, kind="Internal").ap()
    gmx_scr = nc.dram_tensor("gmx_scr", [128, 1], F32, kind="Internal").ap()
    dbg_scr = nc.dram_tensor(
        "dbg", [1024, D], F32,
        kind="ExternalOutput" if level <= 7 else "Internal").ap()
    cc1_in = nc.dram_tensor("cc1_in", [1, 1], F32, kind="Internal").ap()
    cc1_out = nc.dram_tensor("cc1_out", [1, 1], F32, kind="Internal",
                             addr_space="Shared").ap()
    cc2_in = nc.dram_tensor("cc2_in", [1, 1], F32, kind="Internal").ap()
    cc2_out = nc.dram_tensor("cc2_out", [1, 1], F32, kind="Internal",
                             addr_space="Shared").ap()

    ident_d = nc.inline_tensor(np.eye(128, dtype=ml_dtypes.bfloat16), "identc").ap()
    masks_d = nc.inline_tensor(
        np.ascontiguousarray(MASKS_NP.transpose(2, 0, 1, 3).reshape(128, 6 * 768)),
        "maskc").ap()
    ones_d = nc.inline_tensor(np.ones((1, 128), np.float32), "onesc").ap()

    st_perm = ExitStack()

    def sb(name, shape, dt):
        return st_perm.enter_context(
            nc.sbuf_tensor(name, shape, dt, side="left")).ap()

    ident = sb("ident", [128, 128], BF16)
    ones_r = sb("ones_r", [1, 128], F32)
    bq_s = sb("bq_s", [128, DC], F32)
    bk_s = sb("bk_s", [128, DC], F32)
    bv_s = sb("bv_s", [128, DC], F32)
    b1_s = sb("b1_s", [128, FC], F32)
    clip_s = sb("clip_s", [1, 2], F32)
    mu1 = sb("mu1", [128, NT], F32)
    r1 = sb("r1", [128, NT], F32)
    nmr1 = sb("nmr1", [128, NT], F32)
    mu2 = sb("mu2", [128, NT], F32)
    r2 = sb("r2", [128, NT], F32)
    nmr2 = sb("nmr2", [128, NT], F32)
    gmv = sb("gmv", [128, 1], F32)
    gmv2 = sb("gmv2", [128, 1], F32)
    stg1 = sb("stg1", [128, 20], F32)
    stg2 = sb("stg2", [128, 5], F32)
    lni = sb("lni", [128, 3], F32)
    gtmp = sb("gtmp", [128, 2], F32)
    srow = sb("srow", [1, 8], F32)
    qsc = sb("qsc", [128, 4], F32)
    scrap = sb("scrap", [128, 1024], BF16)
    sums_c = sb("sums_c", [128, 4], F32)
    recip_c = sb("recip_c", [128, 4], F32)
    rm_c = sb("rm_c", [128, 4], F32)
    grow = sb("grow", [1, 128], F32)
    qa = sb("qa", [128, NT], F32)
    qb = sb("qb", [128, NT], F32)
    rmx = sb("rmx", [128, 4], F32)
    rsc = sb("rsc", [128, 4], F32)
    ssc = sb("ssc", [128, 4], F32)

    st_ofm = ExitStack()
    o_fm = [st_ofm.enter_context(
        nc.sbuf_tensor(f"o_fm{h}", [128, T], BF16, side="left")).ap()
        for h in range(NH)]

    st_ps = ExitStack()

    def pst(name, shape, dt):
        return st_ps.enter_context(nc.psum_tensor(name, shape, dt)).ap()

    PB1 = pst("PB1", [128, 1024], F32)
    PB2 = pst("PB2", [128, 1024], F32)
    PB3 = pst("PB3", [128, 1024], F32)
    PT = pst("PTT", [128, 1024], BF16)
    PT2 = pst("PTT2", [128, 1024], BF16)
    PA = PB1[:, 0:512]
    PB = PB1[:, 512:1024]
    PC = PB2[:, 0:512]
    PD = PB2[:, 512:1024]
    PS_e = PB1[:, 0:768]
    PS_o = PB2[:, 0:768]
    PO2 = PB3[:, 0:256]

    for dst, src in ((ident, ident_d), (ones_r, ones_d),
                     (bq_s, bq_d), (bk_s, bk_d), (bv_s, bv_d), (b1_s, b1_d)):
        P.emit("sync", lambda e, d=dst, s=src: e.dma_start(out=d, in_=s),
               sig="dCst", inc=16)
    P.emit("sync", lambda e: e.dma_start(out=clip_s[:, 0:1], in_=clipx_d),
           sig="dCst", inc=16)
    tok_consts = P.emit("sync",
                        lambda e: e.dma_start(out=clip_s[:, 1:2], in_=clipd_d),
                        sig="dCst", inc=16)
    t_gm0 = P.emit("vector", lambda e: e.memset(gmv[:], 0.0), sig="vector")
    t_gm20 = P.emit("vector", lambda e: e.memset(gmv2[:], 0.0), sig="vector")

    st_setup = ExitStack()
    wtmp = [st_setup.enter_context(
        nc.sbuf_tensor(f"wtmp{i}", [128, D], BF16, side="right")).ap()
        for i in range(2)]
    wst = [None, None]
    for oc in range(FC):
        b = oc % 2
        tin = P.emit("gpsimd", lambda e, oc=oc, b=b: e.dma_start(
            out=wtmp[b].rearrange("p (c o) -> p c o", o=128),
            in_=W1_d.rearrange("(c p) o -> p c o", p=128)[:, :, oc * 128:(oc + 1) * 128],
            allow_slow_non_contiguous=True),
            waits=[wst[b]], sig=f"dWc{b}", inc=16)
        wst[b] = P.emit("gpsimd", lambda e, oc=oc, b=b: e.dma_start(
            out=w1_scr[oc], in_=wtmp[b][:]), waits=[tin], sig=f"dWs{b}", inc=16)
    for fc in range(FC):
        b = fc % 2
        tin = P.emit("gpsimd", lambda e, fc=fc, b=b: e.dma_start(
            out=wtmp[b][:], in_=W2_d[fc * 128:(fc + 1) * 128, :]),
            waits=[wst[b]], sig=f"dWc{b}", inc=16)
        wst[b] = P.emit("gpsimd", lambda e, fc=fc, b=b: e.dma_start(
            out=w2_scr[fc * 128:(fc + 1) * 128, :], in_=wtmp[b][:]),
            waits=[tin], sig=f"dWs{b}", inc=16)
    tok_w12 = P.join("sync", [wst[0], wst[1]])

    def ln_stats(xap, width, mu_ap, r_ap, nmr_ap, stg, wait_toks, war_tok):
        t1 = P.emit("vector", lambda e: e.tensor_reduce(
            stg[:, 0:1], xap, AX.X, ALU.add),
            waits=list(wait_toks) + [war_tok], sig="vector")
        t2 = P.emit("scalar", lambda e: e.activation(
            scrap[:, 0:width], xap, AF.Square, accum_out=stg[:, 1:2]),
            waits=list(wait_toks) + [war_tok], sig="scalar")
        t3 = P.emit("vector", lambda e: e.tensor_scalar(
            mu_ap, stg[:, 0:1], 1.0 / width, None, ALU.mult),
            waits=[t1], sig="vector")
        t4 = P.emit("vector", lambda e: e.tensor_scalar(
            stg[:, 2:3], stg[:, 1:2], 1.0 / width, None, ALU.mult),
            waits=[t2], sig="vector")
        t5 = P.emit("vector", lambda e: e.tensor_tensor(
            stg[:, 3:4], mu_ap, mu_ap, ALU.mult), waits=[t3], sig="vector")
        t6 = P.emit("vector", lambda e: e.tensor_tensor(
            stg[:, 2:3], stg[:, 2:3], stg[:, 3:4], ALU.subtract),
            waits=[t4, t5], sig="vector")
        t7 = P.emit("scalar", lambda e: e.activation(
            stg[:, 4:5], stg[:, 2:3], AF.Ln, bias=f32(EPS)),
            waits=[t6], sig="scalar")
        t8 = P.emit("scalar", lambda e: e.activation(
            r_ap, stg[:, 4:5], AF.Exp, scale=f32(-0.5)), waits=[t7],
            sig="scalar")
        t9 = P.emit("vector", lambda e: e.tensor_scalar(
            nmr_ap, mu_ap, r_ap, -1.0, ALU.mult, ALU.mult),
            waits=[t8, t3], sig="vector")
        return t9

    st_x1fm = ExitStack()
    xh1_fm = st_x1fm.enter_context(
        nc.sbuf_tensor("xh1_fm", [128, DC * T], BF16, side="right")).ap()
    xh1_3d = xh1_fm.rearrange("p (c t) -> p c t", t=T)
    st_p1a = ExitStack()
    x0b = [st_p1a.enter_context(
        nc.sbuf_tensor(f"x0b{i}", [128, D], F32, side="right")).ap()
        for i in range(2)]
    xhb = [st_p1a.enter_context(
        nc.sbuf_tensor(f"xhb{i}", [128, D], BF16, side="right")).ap()
        for i in range(2)]

    x0_ld = [None] * 2
    apply_tok = [None] * 2
    fm_evac = None
    war1 = None
    war1p = [None] * 2
    for tt in range(NT):
        b = tt % 2
        x0_ld[b] = P.emit("sync", lambda e, tt=tt, b=b: e.dma_start(
            out=x0b[b][:], in_=x0_d[tt * 128:(tt + 1) * 128, :]),
            waits=[apply_tok[b]], sig=f"dX{b}", inc=16)
        war1p[b] = ln_stats(x0b[b][:], D, mu1[:, tt:tt + 1], r1[:, tt:tt + 1],
                            nmr1[:, tt:tt + 1], stg1[:, b * 5:b * 5 + 5],
                            [x0_ld[b]], war1p[b])
        war1 = war1p[b]
        tap = P.emit("scalar", lambda e, tt=tt, b=b: e.activation(
            xhb[b][:], x0b[b][:], AF.Identity,
            bias=nmr1[:, tt:tt + 1], scale=r1[:, tt:tt + 1]),
            waits=[war1], sig="scalar")
        apply_tok[b] = tap
        tp = None
        for j in range(DC):
            tp = P.emit("tensor", lambda e, b=b, j=j: e.transpose(
                PT[:, j * 128:(j + 1) * 128], xhb[b][:, j * 128:(j + 1) * 128],
                ident), waits=[tap, fm_evac, tok_consts], sig="tensor")
        fm_evac = P.emit("scalar", lambda e, tt=tt: e.copy(
            xh1_3d[:, :, tt * 128:(tt + 1) * 128],
            PT[:].rearrange("p (c t) -> p c t", t=128)), waits=[tp], sig="scalar")
    tok_p1a = fm_evac

    if level <= 2:
        fins = []
        fins.append(P.emit("sync", lambda e: e.dma_start(
            out=out_d[0:128, 0:NT], in_=mu1[:]), waits=[tok_p1a], sig="dDmp", inc=16))
        fins.append(P.emit("sync", lambda e: e.dma_start(
            out=out_d[128:256, 0:NT], in_=r1[:]), waits=[tok_p1a], sig="dDmp", inc=16))
        fins.append(P.emit("sync", lambda e: e.dma_start(
            out=out_d[256:384, 0:NT], in_=nmr1[:]), waits=[tok_p1a], sig="dDmp", inc=16))
        fins.append(P.emit("gpsimd", lambda e: e.dma_start(
            out=out_d[384:896, :].rearrange("(p f) o -> p (f o)", p=128),
            in_=xh1_fm[:, 0:T]), waits=[tok_p1a], sig="dDmp", inc=16))
        P.join("sync", fins)
        P.replay()
        return nc

    st_p1b = ExitStack()

    def sbr(name, shape, dt):
        return st_p1b.enter_context(
            nc.sbuf_tensor(name, shape, dt, side="right")).ap()

    masks = sbr("masks", [128, 6 * 768], BF16)
    q_fm = sbr("q_fm", [128, T], BF16)
    k_fm = sbr("k_fm", [128, TK], BF16)
    v_fm = sbr("v_fm", [128, T], BF16)
    v_tm = sbr("v_tm", [128, TK], BF16)
    wsl = sbr("wsl", [128, 3 * D], BF16)
    p_sb = [sbr(f"p_sb{i}", [128, 2 * 768], BF16) for i in range(2)]
    pT_sb = sbr("pT_sb", [128, 2 * 768], BF16)

    P.emit("sync", lambda e: e.dma_start(out=masks, in_=masks_d), sig="dMsk", inc=16)
    P.emit("vector", lambda e: e.memset(k_fm[:, 0:KPAD], 0.0), sig="vector")
    P.emit("vector", lambda e: e.memset(k_fm[:, T + KPAD:TK], 0.0), sig="vector")
    P.emit("vector", lambda e: e.memset(v_tm[:, 0:KPAD], 0.0), sig="vector")
    P.emit("vector", lambda e: e.memset(pT_sb[:, 5 * 256:5 * 256 + 128], 0.0),
           sig="vector")
    P.emit("vector", lambda e: e.memset(pT_sb[:, 128:256], 0.0), sig="vector")
    tok_pad = P.emit("vector", lambda e: e.memset(v_tm[:, T + KPAD:TK], 0.0),
                     sig="vector")
    tok_mask = ("dMsk", P.cnt["dMsk"])

    wsl_free = None
    attn_prev = None
    ev = {}

    for h in range(NH if level >= 4 else 1):
        wtoks = []
        for wi, W_d in enumerate((Wq_d, Wk_d, Wv_d)):
            wtoks.append(P.emit("gpsimd", lambda e, wi=wi, W_d=W_d, h=h: e.dma_start(
                out=wsl.rearrange("p (w c o) -> p w c o", w=3, c=DC)[:, wi],
                in_=W_d.rearrange("(c p) o -> p c o", p=128)[
                    :, :, h * 128:(h + 1) * 128],
                allow_slow_non_contiguous=True),
                waits=[wsl_free], sig="dWsl", inc=16))
        wsl_ld = wtoks[-1]

        for wofs, dst, bias_s, base in ((0, q_fm, bq_s, 0),
                                        (D, k_fm, bk_s, KPAD),
                                        (2 * D, v_fm, bv_s, 0)):
            for g in range(NG):
                pn = ("PA", PA) if g % 2 == 0 else ("PB", PB)
                mm = None
                for fc in range(DC):
                    mm = P.emit("tensor", lambda e, fc=fc, g=g, pn=pn, wofs=wofs: e.matmul(
                        pn[1][:, :],
                        wsl[:, wofs + fc * 128:wofs + (fc + 1) * 128],
                        xh1_fm[:, fc * T + g * TG:fc * T + (g + 1) * TG],
                        start=(fc == 0), stop=(fc == DC - 1)),
                        waits=[wsl_ld, tok_p1a, ev.get(pn[0]), attn_prev],
                        sig="tensor")
                ev[pn[0]] = P.emit("scalar", lambda e, g=g, pn=pn, dst=dst,
                                   bias_s=bias_s, base=base, h=h: e.activation(
                    dst[:, base + g * TG:base + (g + 1) * TG], pn[1][:, :],
                    AF.Identity, bias=bias_s[:, h:h + 1]),
                    waits=[mm], sig="scalar")
        vev = (ev.get("PA"), ev.get("PB"))
        pt_prev2 = None
        for t8 in range(NT // 8):
            tp = None
            for j in range(8):
                tt = t8 * 8 + j
                tp = P.emit("tensor", lambda e, tt=tt, j=j: e.transpose(
                    PT[:, j * 128:(j + 1) * 128],
                    v_fm[:, tt * 128:(tt + 1) * 128], ident),
                    waits=[vev[0], vev[1], pt_prev2, tok_consts], sig="tensor")
            pt_prev2 = P.emit("scalar", lambda e, t8=t8: e.copy(
                v_tm[:, KPAD + t8 * 1024:KPAD + (t8 + 1) * 1024], PT[:]),
                waits=[tp], sig="scalar")
        qk_done = (ev.get("PA"), ev.get("PB"))
        v_done = (pt_prev2, pt_prev2)
        wsl_free = P.join("scalar", list(qk_done) + list(v_done))

        ps_free = {0: None, 1: None}
        pt_free = v_done[0]
        po_free = {0: None, 1: None}
        pend = None
        tails_done = []

        def emit_tail(ctx):
            nonlocal pt_free
            par, qbase, vblk, pn_toks = ctx
            ev_toks = []
            for qs in range(2):
                kc0 = qs
                tp = None
                for i5, kc in enumerate(range(kc0, kc0 + 5)):
                    tp = P.emit("tensor", lambda e, par=par, qs=qs, kc=kc,
                                i5=i5: e.transpose(
                        PT[:, i5 * 128:(i5 + 1) * 128],
                        p_sb[par][:, qs * 768 + kc * 128:qs * 768 + (kc + 1) * 128],
                        ident), waits=[pn_toks[qs], pt_free], sig="tensor")
                pt_free = P.emit("scalar", lambda e, qs=qs, kc0=kc0: e.copy(
                    pT_sb.rearrange("p (kc q) -> p kc q", q=256)[
                        :, kc0:kc0 + 5, qs * 128:(qs + 1) * 128],
                    PT[:, 0:640].rearrange("p (kc q) -> p kc q", q=128)),
                    waits=[tp], sig="scalar")
                ev_toks.append(pt_free)
            mm = None
            for kc in range(6):
                mm = P.emit("tensor", lambda e, kc=kc, vblk=vblk: e.matmul(
                    PO2[:, :],
                    v_tm[:, (vblk + kc) * 128:(vblk + kc + 1) * 128],
                    pT_sb[:, kc * 256:(kc + 1) * 256],
                    start=(kc == 0), stop=(kc == 5)),
                    waits=[ev_toks[0], ev_toks[1], po_free[0],
                           v_done[0], v_done[1]], sig="tensor")
            po_free[0] = P.emit("scalar", lambda e, h=h, qbase=qbase: e.copy(
                o_fm[h][:, qbase:qbase + 256], PO2[:, :]),
                waits=[mm], sig="scalar")
            tails_done.append(po_free[0])

        idx = 0
        for bb in range(BLOC):
            for c in range(NCH):
                var = 0 if c == 0 else (2 if c == NCH - 1 else 1)
                qbase = bb * SEQ + c * WIN
                wbase = bb * SEQ + c * WIN
                vblk = wbase // 128
                par = idx % 2
                exp_toks = []
                for qs in range(2):
                    psc = PS_e if qs == 0 else PS_o
                    nzc = MASK_NZ[(var, qs)]
                    P.emit("tensor", lambda e, qbase=qbase, qs=qs, wbase=wbase,
                           psc=psc: e.matmul(
                        psc[:, 0:512],
                        q_fm[:, qbase + qs * 128:qbase + (qs + 1) * 128],
                        k_fm[:, wbase:wbase + 512],
                        start=True, stop=False),
                        waits=[qk_done[0], qk_done[1], ps_free[qs], tok_pad,
                               tok_mask], sig="tensor")
                    mlast = P.emit("tensor", lambda e, qbase=qbase, qs=qs,
                                   wbase=wbase, psc=psc: e.matmul(
                        psc[:, 512:768],
                        q_fm[:, qbase + qs * 128:qbase + (qs + 1) * 128],
                        k_fm[:, wbase + 512:wbase + 768],
                        start=True, stop=False), sig="tensor")
                    for kc in nzc:
                        mlast = P.emit("tensor", lambda e, var=var, qs=qs,
                                       kc=kc, psc=psc: e.matmul(
                            psc[:, kc * 128:(kc + 1) * 128], ident,
                            masks[:, (var * 2 + qs) * 768 + kc * 128:
                                  (var * 2 + qs) * 768 + (kc + 1) * 128],
                            start=False, stop=True), sig="tensor")
                    col = par * 2 + qs
                    t1 = P.emit("vector", lambda e, col=col, psc=psc: e.tensor_reduce(
                        rm_c[:, col:col + 1], psc[:, :], AX.X, ALU.max),
                        waits=[mlast], sig="vector")
                    t2n = P.emit("vector", lambda e, col=col: e.tensor_scalar(
                        rm_c[:, col:col + 1], rm_c[:, col:col + 1], -SCALE,
                        None, ALU.mult), waits=[t1], sig="vector")
                    t3 = P.emit("scalar", lambda e, par=par, qs=qs, col=col,
                                psc=psc: e.activation(
                        p_sb[par][:, qs * 768:(qs + 1) * 768], psc[:, :],
                        AF.Exp, bias=rm_c[:, col:col + 1], scale=f32(SCALE),
                        accum_out=sums_c[:, col:col + 1]),
                        waits=[t2n], sig="scalar")
                    exp_toks.append(t3)
                    ps_free[qs] = t3
                tln = P.emit("scalar", lambda e, par=par: e.activation(
                    recip_c[:, par * 2:par * 2 + 2], sums_c[:, par * 2:par * 2 + 2],
                    AF.Ln), waits=exp_toks, sig="scalar")
                t5 = P.emit("scalar", lambda e, par=par: e.activation(
                    recip_c[:, par * 2:par * 2 + 2], recip_c[:, par * 2:par * 2 + 2],
                    AF.Exp, scale=f32(-1.0)), waits=[tln], sig="scalar")
                pn_toks = []
                for qs in range(2):
                    pn_toks.append(P.emit("vector", lambda e, par=par, qs=qs: e.tensor_scalar(
                        p_sb[par][:, qs * 768:(qs + 1) * 768],
                        p_sb[par][:, qs * 768:(qs + 1) * 768],
                        recip_c[:, par * 2 + qs:par * 2 + qs + 1], None,
                        ALU.mult), waits=[t5], sig="vector"))
                if pend is not None:
                    emit_tail(pend)
                pend = (par, qbase, vblk, pn_toks)
                idx += 1
        emit_tail(pend)
        pend = None
        attn_prev = P.join("scalar", tails_done[-2:] +
                           [ps_free[0], ps_free[1]])
    tok_attn = attn_prev

    if level <= 4:
        fins = []
        for h in range(NH if level >= 4 else 1):
            fins.append(P.emit("gpsimd", lambda e, h=h: e.dma_start(
                out=out_d[h * 512:(h + 1) * 512, :].rearrange(
                    "(p f) o -> p (f o)", p=128),
                in_=o_fm[h][:, :]), waits=[tok_attn], sig="dDmp", inc=16))
        P.join("sync", fins)
        P.replay()
        return nc

    st_p1b.close()
    st_p1a.close()
    st_x1fm.close()
    P.barrier()

    st_p1c = ExitStack()

    def sbc(name, shape, dt):
        return st_p1c.enter_context(
            nc.sbuf_tensor(name, shape, dt, side="right")).ap()

    gam_b = sbc("gam_b", [128, D], F32)
    bet_b = sbc("bet_b", [128, D], F32)
    b2_b = sbc("b2_b", [128, D], F32)
    bo_b = sbc("bo_b", [128, D], F32)
    Wo_s = sbc("Wo_s", [128, DC * D], BF16)
    w1b = [sbc(f"w1b{i}", [128, D], BF16) for i in range(2)]
    w2b = [sbc(f"w2b{i}", [128, 512], BF16) for i in range(2)]
    h_fm = [sbc(f"h_fm{i}", [128, (FC // 2) * TG], BF16) for i in range(2)]
    x0c = sbc("x0c", [128, 4 * D], F32)
    t1c = sbc("t1c", [128, D], F32)
    xhc = sbc("xhc", [128, D], F32)
    xhbf = sbc("xhbf", [128, D], BF16)
    t2tm = sbc("t2tm", [128, 4 * D], F32)
    t2fm = sbc("t2fm", [128, DC * TG], BF16)
    t2_3d = t2fm.rearrange("p (c t) -> p c t", t=TG)
    w2e = sbc("w2e", [128, 512], F32)

    bct = None
    rows_ld = []
    for i, src_d in enumerate((gamma_d, beta_d, b2_d, bo_row_d)):
        rows_ld.append(P.emit("sync", lambda e, i=i, src_d=src_d: e.dma_start(
            out=x0c[0:1, i * D:(i + 1) * D], in_=src_d), sig="dRow", inc=16))
    for i, dst in enumerate((gam_b, bet_b, b2_b, bo_b)):
        for half in range(2):
            m = P.emit("tensor", lambda e, i=i, half=half: e.matmul(
                PA[:, :], ones_r[:, :],
                x0c[0:1, i * D + half * 512:i * D + (half + 1) * 512],
                start=True, stop=True),
                waits=[tok_consts, rows_ld[i], bct], sig="tensor")
            bct = P.emit("vector", lambda e, dst=dst, half=half: e.tensor_copy(
                dst[:, half * 512:(half + 1) * 512], PA[:, :]),
                waits=[m], sig="vector")
    tok_bcast = bct

    wo_tok = None
    for ccc in range(DC):
        wo_tok = P.emit("gpsimd", lambda e, ccc=ccc: e.dma_start(
            out=Wo_s[:, ccc * D:(ccc + 1) * D],
            in_=Wo_d[ccc * 128:(ccc + 1) * 128, :]), sig="dWoL", inc=16)

    evt = {"PA": tok_bcast, "PB": None, "PC": None, "PD": None}
    x0_free = [None] * 4
    t1_free = None
    xh_free = None
    t2fm_evac = None
    h_free = None
    w1_free = [None, None]
    w2_free = [None, None]
    war_i = None
    war_2 = None
    x1_st = [None] * 4

    for g in range(NG):
        t2_done = []
        x0_ld2 = [None] * 4
        x0acc = [None] * 4
        for ti in range(4):
            tt = g * 4 + ti
            bx = ti % 2
            x0_ld2[ti] = P.emit("sync", lambda e, tt=tt, ti=ti: e.dma_start(
                out=x0c[:, ti * D:(ti + 1) * D],
                in_=x0_d[tt * 128:(tt + 1) * 128, :]),
                waits=[x0_free[ti], tok_bcast], sig=f"dxc{ti}", inc=16)
            for half, pname, pp in ((0, "PA", PA), (1, "PB", PB)):
                mm = None
                for hc in range(DC):
                    mm = P.emit("tensor", lambda e, tt=tt, hc=hc, half=half,
                                pp=pp: e.matmul(
                        pp[:, :],
                        o_fm[hc][:, tt * 128:(tt + 1) * 128],
                        Wo_s[:, hc * D + half * 512:hc * D + (half + 1) * 512],
                        start=(hc == 0), stop=(hc == DC - 1)),
                        waits=[wo_tok, tok_attn, evt[pname], tok_bcast],
                        sig="tensor")
                evt[pname] = P.emit("vector", lambda e, half=half,
                                    pp=pp: e.tensor_tensor(
                    t1c[:, half * 512:(half + 1) * 512], pp[:, :],
                    bo_b[:, half * 512:(half + 1) * 512], ALU.add),
                    waits=[mm, t1_free], sig="vector")
            ta = P.emit("scalar", lambda e, tt=tt, ti=ti: e.activation(
                xhc[:], x0c[:, ti * D:(ti + 1) * D], AF.Identity,
                bias=nmr1[:, tt:tt + 1], scale=r1[:, tt:tt + 1]),
                waits=[x0_ld2[ti], xh_free], sig="scalar")
            tb = P.emit("vector", lambda e: e.tensor_tensor(
                t1c[:], t1c[:], xhc[:], ALU.add),
                waits=[ta, evt["PA"], evt["PB"]], sig="vector")
            if level <= 5 and g == 7:
                P.emit("gpsimd", lambda e, ti=ti: e.dma_start(
                    out=dbg_scr[ti * 128:(ti + 1) * 128, :], in_=t1c[:]),
                    waits=[tb], sig="dDbg", inc=16)
            war_i = ln_stats(t1c[:], D, lni[:, 0:1], lni[:, 1:2],
                             lni[:, 2:3], stg1,
                             [tb, ("dDbg", P.cnt.get("dDbg", 0))], war_i)
            tc = P.emit("scalar", lambda e: e.activation(
                xhc[:], t1c[:], AF.Identity,
                bias=lni[:, 2:3], scale=lni[:, 1:2]), waits=[war_i],
                sig="scalar")
            t1_free = tc
            td = P.emit("vector", lambda e, ti=ti: e.tensor_tensor(
                t2tm[:, ti * D:(ti + 1) * D], xhc[:], gam_b[:], ALU.mult),
                waits=[tc], sig="vector")
            te = P.emit("vector", lambda e, ti=ti: e.tensor_tensor(
                t2tm[:, ti * D:(ti + 1) * D], t2tm[:, ti * D:(ti + 1) * D],
                bet_b[:], ALU.add), sig="vector")
            xh_free = td
            if level <= 5 and g == 7:
                P.emit("gpsimd", lambda e, ti=ti: e.dma_start(
                    out=dbg_scr[512 + ti * 128:512 + (ti + 1) * 128, :],
                    in_=t2tm[:, ti * D:(ti + 1) * D]), waits=[te],
                    sig="dDbg", inc=16)
            tf = P.emit("scalar", lambda e, ti=ti: e.copy(
                xhbf[:], t2tm[:, ti * D:(ti + 1) * D]),
                waits=[te], sig="scalar")
            tp = None
            for j in range(DC):
                tp = P.emit("tensor", lambda e, j=j: e.transpose(
                    PT[:, j * 128:(j + 1) * 128], xhbf[:, j * 128:(j + 1) * 128],
                    ident), waits=[tf, t2fm_evac], sig="tensor")
            t2fm_evac = P.emit("scalar", lambda e, ti=ti: e.copy(
                t2_3d[:, :, ti * 128:(ti + 1) * 128],
                PT[:].rearrange("p (c t) -> p c t", t=128)),
                waits=[tp], sig="scalar")
            t2_done.append(t2fm_evac)
            x0acc[ti] = P.emit("vector", lambda e, ti=ti: e.tensor_tensor(
                x0c[:, ti * D:(ti + 1) * D], x0c[:, ti * D:(ti + 1) * D],
                t2tm[:, ti * D:(ti + 1) * D], ALU.add),
                waits=[te, ta, x0_ld2[ti], ("dDbg", P.cnt.get("dDbg", 0))],
                sig="vector")

        for oc in range(FC):
            bw = oc % 2
            w1_ld = P.emit("sync", lambda e, oc=oc, bw=bw: e.dma_start(
                out=w1b[bw][:], in_=w1_scr[oc]),
                waits=[w1_free[bw], tok_w12], sig=f"dw1b{bw}", inc=16)
            pn = ("PC", PC) if oc % 2 == 0 else ("PD", PD)
            mm = None
            for fc in range(DC):
                mm = P.emit("tensor", lambda e, fc=fc, pn=pn, bw=bw: e.matmul(
                    pn[1][:, :], w1b[bw][:, fc * 128:(fc + 1) * 128],
                    t2fm[:, fc * TG:(fc + 1) * TG],
                    start=(fc == 0), stop=(fc == DC - 1)),
                    waits=[w1_ld, t2_done[-1], evt[pn[0]], h_free], sig="tensor")
            w1_free[bw] = mm
            evt[pn[0]] = P.emit("scalar", lambda e, oc=oc, pn=pn: e.activation(
                h_fm[oc // 16][:, (oc % 16) * TG:(oc % 16 + 1) * TG],
                pn[1][:, :], AF.Gelu, bias=b1_s[:, oc:oc + 1]),
                waits=[mm], sig="scalar")
        h_done = (evt["PC"], evt["PD"])

        PX = (("PC", PC), ("PD", PD), ("PA", PA), ("PB", PB))
        x0add = [None] * 4
        for oh in range(2):
            mm_t = [None] * 4
            for fc in range(FC):
                bw = fc % 2
                w2_ld = P.emit("sync", lambda e, fc=fc, oh=oh, bw=bw: e.dma_start(
                    out=w2b[bw][:],
                    in_=w2_scr[fc * 128:(fc + 1) * 128, oh * 512:(oh + 1) * 512]),
                    waits=[w2_free[bw], tok_w12], sig=f"dw2b{bw}", inc=16)
                for ti in range(4):
                    mm_t[ti] = P.emit("tensor", lambda e, fc=fc, ti=ti, oh=oh,
                                      bw=bw, PXi=PX[ti][1]: e.matmul(
                        PXi[:, :],
                        h_fm[fc // 16][:, (fc % 16) * TG + ti * 128:
                                       (fc % 16) * TG + (ti + 1) * 128],
                        w2b[bw][:],
                        start=(fc == 0), stop=(fc == FC - 1)),
                        waits=[w2_ld, h_done[0], h_done[1],
                               evt[PX[ti][0]] if fc == 0 else None],
                        sig="tensor")
                w2_free[bw] = mm_t[3]
            for ti in range(4):
                ew = P.emit("vector", lambda e, ti=ti, oh=oh, PXi=PX[ti][1]: e.tensor_tensor(
                    x0c[:, ti * D + oh * 512:ti * D + (oh + 1) * 512],
                    x0c[:, ti * D + oh * 512:ti * D + (oh + 1) * 512],
                    PXi[:, :], ALU.add), waits=[mm_t[ti], x0acc[ti]], sig="vector")
                evt[PX[ti][0]] = ew
                x0add[ti] = P.emit("vector", lambda e, ti=ti, oh=oh: e.tensor_tensor(
                    x0c[:, ti * D + oh * 512:ti * D + (oh + 1) * 512],
                    x0c[:, ti * D + oh * 512:ti * D + (oh + 1) * 512],
                    b2_b[:, oh * 512:(oh + 1) * 512], ALU.add),
                    waits=[ew], sig="vector")
        h_free = w2_free[1]
        for ti in range(4):
            tt = g * 4 + ti
            tg2 = x0add[ti]
            war_2 = ln_stats(x0c[:, ti * D:(ti + 1) * D], D, mu2[:, tt:tt + 1],
                             r2[:, tt:tt + 1], nmr2[:, tt:tt + 1], stg2,
                             [tg2], war_2)
            tmx = P.emit("vector", lambda e, ti=ti: e.tensor_reduce(
                gtmp[:, 0:1], x0c[:, ti * D:(ti + 1) * D], AX.X, ALU.max),
                waits=[tg2], sig="vector")
            tmn = P.emit("vector", lambda e, ti=ti: e.tensor_reduce(
                gtmp[:, 1:2], x0c[:, ti * D:(ti + 1) * D], AX.X, ALU.min),
                sig="vector")
            u1 = P.emit("vector", lambda e, tt=tt: e.tensor_tensor(
                gtmp[:, 0:1], gtmp[:, 0:1], mu2[:, tt:tt + 1], ALU.subtract),
                waits=[war_2, tmx], sig="vector")
            u2 = P.emit("vector", lambda e, tt=tt: e.tensor_tensor(
                gtmp[:, 1:2], mu2[:, tt:tt + 1], gtmp[:, 1:2], ALU.subtract),
                waits=[tmn], sig="vector")
            u3 = P.emit("vector", lambda e: e.tensor_tensor(
                gtmp[:, 0:1], gtmp[:, 0:1], gtmp[:, 1:2], ALU.max),
                waits=[u1, u2], sig="vector")
            u4 = P.emit("vector", lambda e, tt=tt: e.tensor_tensor(
                gtmp[:, 0:1], gtmp[:, 0:1], r2[:, tt:tt + 1], ALU.mult),
                waits=[u3], sig="vector")
            u5 = P.emit("vector", lambda e: e.tensor_tensor(
                gmv[:], gmv[:], gtmp[:, 0:1], ALU.max),
                waits=[t_gm0, u4], sig="vector")
            js = P.join("scalar", [u5, tg2])
            x1_st[ti] = P.emit("gpsimd", lambda e, tt=tt, ti=ti: e.dma_start(
                out=x1_scr[tt * 128:(tt + 1) * 128, :],
                in_=x0c[:, ti * D:(ti + 1) * D]),
                waits=[u5, tg2, js], sig=f"dx1s{ti}", inc=16)
            x0_free[ti] = x1_st[ti]
            if level <= 5 and g == 7:
                x0_free[ti] = P.emit("sync", lambda e, ti=ti: e.dma_start(
                    out=dbg_scr[ti * 128:(ti + 1) * 128, :],
                    in_=x0c[:, ti * D:(ti + 1) * D]),
                    waits=[x1_st[ti]], sig="dDbg2", inc=16)

    tok_x1 = P.join("sync", x1_st)

    if level <= 5:
        fin = P.emit("sync", lambda e: e.dma_start(
            out=out_d, in_=x1_scr), waits=[tok_x1], sig="dDmp", inc=16)
        P.join("sync", [fin])
        P.replay()
        return nc

    st_p1c.close()
    st_ofm.close()
    st_setup.close()
    P.barrier()

    st_wq = ExitStack()
    wdT = st_wq.enter_context(
        nc.sbuf_tensor("wdT", [128, IC * D], BF16, side="left")).ap()
    wgT = [st_wq.enter_context(
        nc.sbuf_tensor(f"wgT{i}", [128, II], BF16, side="left")).ap()
        for i in range(DC)]
    wuT = [st_wq.enter_context(
        nc.sbuf_tensor(f"wuT{i}", [128, II], BF16, side="left")).ap()
        for i in range(DC)]
    st_wql = ExitStack()
    wqld0 = st_wql.enter_context(
        nc.sbuf_tensor("wqld0", [128, II], F32, side="right")).ap()
    wqld1 = st_wql.enter_context(
        nc.sbuf_tensor("wqld1", [128, 1024], F32, side="right")).ap()
    wqq0 = st_wql.enter_context(
        nc.sbuf_tensor("wqq0", [128, II], BF16, side="right")).ap()
    wqq1 = st_wql.enter_context(
        nc.sbuf_tensor("wqq1", [128, 1024], BF16, side="right")).ap()
    wqi0 = st_wql.enter_context(
        nc.sbuf_tensor("wqi0", [128, 1024], mybir.dt.int32, side="right")).ap()
    wq_ld = [wqld0, wqld1]
    wq_q = [wqq0, wqq1]
    wqi = [wqi0, wqi0]
    qw_ld_tok = [None] * 2
    qw_q_tok = [None] * 2

    def quant_weight(src_d, n_tiles, width, dstf, pt_prev, PTX=PT, slots=2,
                     alt=False):
        ld_tok = qw_ld_tok
        q_tok = qw_q_tok
        tp_done = pt_prev
        tpl = {0: pt_prev, 1: pt_prev}
        ptx_done = {0: pt_prev, 1: pt_prev}
        for t in range(n_tiles):
            b = t % slots if alt else 0
            k = t % 2 if alt else 0
            PTXt = (PT, PT2)[k] if alt else PTX
            ld_tok[b] = P.emit("sync", lambda e, t=t, b=b: e.dma_start(
                out=wq_ld[b][:, 0:width], in_=src_d[t * 128:(t + 1) * 128, :]),
                waits=[q_tok[b], tpl[b] if alt else tp_done],
                sig=f"dQl{b}", inc=16)
            t1 = P.emit("vector", lambda e, b=b, width=width: e.tensor_reduce(
                rmx[:, b:b + 1], wq_ld[b][:, 0:width], AX.X, ALU.max,
                apply_absolute_value=True), waits=[ld_tok[b]], sig="vector")
            t2 = P.emit("scalar", lambda e, b=b: e.activation(
                rsc[:, b:b + 1], rmx[:, b:b + 1], AF.Ln), waits=[t1],
                sig="scalar")
            t3 = P.emit("scalar", lambda e, b=b: e.activation(
                rsc[:, b:b + 1], rsc[:, b:b + 1], AF.Exp,
                scale=f32(-1.0), bias=f32(LN127)), waits=[t2], sig="scalar")
            t4 = P.emit("vector", lambda e, b=b: e.tensor_scalar(
                ssc[:, b:b + 1], rmx[:, b:b + 1], 1.0 / 127.0, None, ALU.mult),
                waits=[t1], sig="vector")
            P.emit("vector", lambda e, b=b, width=width: e.tensor_scalar(
                wq_ld[b][:, 0:width], wq_ld[b][:, 0:width],
                rsc[:, b:b + 1], -(QMAX + 1.0), ALU.mult, ALU.max),
                waits=[t3], sig="vector")
            for c0 in range(0, width, 1024):
                cw = min(1024, width - c0)
                P.emit("vector", lambda e, b=b, c0=c0, cw=cw: e.tensor_scalar(
                    wqi[b][:, 0:cw], wq_ld[b][:, c0:c0 + cw],
                    QMAX, None, ALU.min), sig="vector")
                q_tok[b] = P.emit("vector", lambda e, b=b, c0=c0,
                                  cw=cw: e.tensor_scalar(
                    wq_q[b][:, c0:c0 + cw], wqi[b][:, 0:cw],
                    ssc[:, b:b + 1], None, ALU.mult),
                    waits=[t4], sig="vector")
            nj = width // 128
            for j0 in range(0, nj, 8):
                jn = min(8, nj - j0)
                tp = None
                for j in range(jn):
                    tp = P.emit("tensor", lambda e, b=b, j0=j0, j=j,
                                PTX=PTXt: e.transpose(
                        PTX[:, j * 128:(j + 1) * 128],
                        wq_q[b][:, (j0 + j) * 128:(j0 + j + 1) * 128], ident),
                        waits=[q_tok[b],
                               ptx_done[k] if alt else tp_done], sig="tensor")
                tpl[b] = tp
                for j in range(jn):
                    tp_done = P.emit("scalar", lambda e, t=t, j0=j0, j=j,
                                     dstf=dstf, PTX=PTXt: e.copy(
                        dstf(j0 + j)[:, t * 128:(t + 1) * 128],
                        PTX[:, j * 128:(j + 1) * 128]),
                        waits=[tp], sig="scalar")
                ptx_done[k] = tp_done
        return tp_done

    wd3 = wdT.rearrange("p (c i) -> p c i", i=D)
    tok_wg = quant_weight(wg_d, IC, D, lambda c: wgT[c], None,
                          alt=True)
    tok_wu = quant_weight(wu_d, IC, D, lambda c: wuT[c], tok_wg,
                          alt=True)

    t1 = P.emit("gpsimd", lambda e: e.dma_start(
        out=gmx_scr, in_=gmv[:]), waits=[tok_x1], sig="dGm1", inc=16)
    t2 = P.emit("sync", lambda e: e.dma_start(
        out=grow[:], in_=gmx_scr.rearrange("p one -> one p")),
        waits=[t1], sig="dGr1", inc=16)
    t3 = P.emit("vector", lambda e: e.tensor_reduce(
        srow[:, 0:1], grow[:], AX.X, ALU.max), waits=[t2], sig="vector")
    t4 = P.emit("gpsimd", lambda e: e.dma_start(out=cc1_in, in_=srow[:, 0:1]),
                waits=[t3], sig="dCi1", inc=16)
    t5 = P.emit("gpsimd", lambda e: e.collective_compute(
        "AllReduce", ALU.max, replica_groups=RG8, ins=[cc1_in], outs=[cc1_out]),
        waits=[t4], sig="pool")
    t6 = P.emit("sync", lambda e: e.dma_start(out=srow[:, 0:1], in_=cc1_out),
                waits=[t5], sig="dCo1", inc=16)
    t7 = P.emit("scalar", lambda e: e.activation(
        srow[:, 1:2], clip_s[0:1, 0:1], AF.Sigmoid),
        waits=[tok_consts, t6], sig="scalar")
    t8 = P.emit("vector", lambda e: e.tensor_tensor(
        srow[:, 2:3], srow[:, 0:1], srow[:, 1:2], ALU.mult),
        waits=[t7], sig="vector")
    t9 = P.emit("vector", lambda e: e.tensor_scalar(
        srow[:, 2:3], srow[:, 2:3], 1.0 / 127.0, None, ALU.mult),
        waits=[t8], sig="vector")
    t10 = P.emit("scalar", lambda e: e.activation(
        srow[:, 3:4], srow[:, 2:3], AF.Ln), waits=[t9], sig="scalar")
    t11 = P.emit("scalar", lambda e: e.activation(
        srow[:, 3:4], srow[:, 3:4], AF.Exp, scale=f32(-1.0)), waits=[t10],
        sig="scalar")
    t12 = P.emit("tensor", lambda e: e.matmul(
        PB3[:, 0:2], ones_r[:, :], srow[:, 2:4], start=True, stop=True),
        waits=[t11], sig="tensor")
    tok_s1 = P.emit("vector", lambda e: e.tensor_copy(qsc[:, 0:2], PB3[:, 0:2]),
                    waits=[t12], sig="vector")
    tqa = P.emit("vector", lambda e: e.tensor_scalar(
        qa[:], r2[:], qsc[:, 1:2], None, ALU.mult), waits=[tok_s1], sig="vector")
    tqb = P.emit("vector", lambda e: e.tensor_tensor(
        qb[:], mu2[:], qa[:], ALU.mult), waits=[tqa], sig="vector")
    tok_qc = P.emit("vector", lambda e: e.tensor_scalar(
        qb[:], qb[:], -1.0, None, ALU.mult), waits=[tqb], sig="vector")

    if level <= 6:
        fin = P.emit("sync", lambda e: e.dma_start(
            out=out_d[0:128, 0:4], in_=qsc[:]), waits=[tok_qc, tok_wu],
            sig="dDmp", inc=16)
        P.join("sync", [fin])
        P.replay()
        return nc

    st_p2 = ExitStack()

    def sb2(name, shape, dt):
        return st_p2.enter_context(
            nc.sbuf_tensor(name, shape, dt, side="right")).ap()

    x1b = [sb2(f"x1b{i}", [128, D], F32) for i in range(2)]
    xqb = [sb2(f"xqb{i}", [128, D], F32) for i in range(2)]
    xqbf0 = sb2("xqbf0", [128, D], BF16)
    xqbf = [xqbf0, xqbf0]
    xqi = [sb2(f"xqi{i}", [128, D], mybir.dt.int32) for i in range(2)]
    xq_fm = sb2("xq_fm", [128, DC * TG], BF16)
    xq_3d = xq_fm.rearrange("p (c t) -> p c t", t=TG)
    sg_f = [sb2(f"sg_f{i}", [128, TG], F32) for i in range(2)]
    up_f = [sb2(f"up_f{i}", [128, TG], F32) for i in range(2)]
    ha_bf = [sb2(f"ha_bf{i}", [128, TG], BF16) for i in range(2)]

    x1_ld = [None, None]
    xq_free = None
    fm_ev = None
    xq4_free = [None, None]
    ev2 = {"PA": None, "PB": None, "PC": None, "PD": None}
    ha_free = [None, None]
    q_free = [None, None]

    for g in range(NG):
        fm_toks = []
        for ti in range(4):
            tt = g * 4 + ti
            bx = ti % 2
            x1_ld[bx] = P.emit("sync", lambda e, tt=tt, bx=bx: e.dma_start(
                out=x1b[bx][:], in_=x1_scr[tt * 128:(tt + 1) * 128, :]),
                waits=[q_free[bx], tok_x1], sig=f"dx1b{bx}", inc=16)
            q1 = P.emit("scalar", lambda e, tt=tt, bx=bx: e.activation(
                xqb[bx][:], x1b[bx][:], AF.Identity,
                bias=qb[:, tt:tt + 1], scale=qa[:, tt:tt + 1]),
                waits=[x1_ld[bx], tok_qc, q_free[bx]], sig="scalar")
            q_free[bx] = q1
            P.emit("vector", lambda e, bx=bx: e.tensor_scalar(
                xqb[bx][:], xqb[bx][:], -(QMAX + 1.0), QMAX, ALU.max, ALU.min),
                waits=[q1], sig="vector")
            P.emit("vector", lambda e, bx=bx: e.tensor_scalar(
                xqi[bx][:], xqb[bx][:], 0.0, None, ALU.add),
                waits=[xq4_free[bx]], sig="vector")
            q3c = P.join("vector", [])
            q4 = P.emit("scalar", lambda e, bx=bx: e.activation(
                xqbf[bx][:], xqi[bx][:], AF.Copy, scale=qsc[:, 0:1]),
                waits=[q3c, xq4_free[bx]], sig="scalar")
            xq4_free[bx] = q4
            tp = None
            for j in range(DC):
                tp = P.emit("tensor", lambda e, j=j, bx=bx: e.transpose(
                    PT[:, j * 128:(j + 1) * 128], xqbf[bx][:, j * 128:(j + 1) * 128],
                    ident), waits=[q4, xq_free, fm_ev], sig="tensor")
            fm_ev = P.emit("scalar", lambda e, ti=ti: e.copy(
                xq_3d[:, :, ti * 128:(ti + 1) * 128],
                PT[:].rearrange("p (c t) -> p c t", t=128)),
                waits=[tp], sig="scalar")
            fm_toks.append(fm_ev)
        tok_xq = fm_toks[-1]
        mmu = None
        for icx in range(IC):
            bh = icx % 2
            pg = ("PA", PA) if icx % 2 == 0 else ("PB", PB)
            pu = ("PC", PC) if icx % 2 == 0 else ("PD", PD)
            mmg = None
            for fc in range(DC):
                mmg = P.emit("tensor", lambda e, icx=icx, fc=fc, pg=pg: e.matmul(
                    pg[1][:, :], wgT[fc][:, icx * 128:(icx + 1) * 128],
                    xq_fm[:, fc * TG:(fc + 1) * TG],
                    start=(fc == 0), stop=(fc == DC - 1)),
                    waits=[tok_xq, tok_wg, ev2[pg[0]]], sig="tensor")
            for fc in range(DC):
                mmu = P.emit("tensor", lambda e, icx=icx, fc=fc, pu=pu: e.matmul(
                    pu[1][:, :], wuT[fc][:, icx * 128:(icx + 1) * 128],
                    xq_fm[:, fc * TG:(fc + 1) * TG],
                    start=(fc == 0), stop=(fc == DC - 1)),
                    waits=[tok_wu, ev2[pu[0]]], sig="tensor")
            e1 = P.emit("scalar", lambda e, bh=bh, pg=pg: e.activation(
                sg_f[bh][:], pg[1][:, :], AF.Silu),
                waits=[mmg, ha_free[bh]], sig="scalar")
            ev2[pg[0]] = e1
            e2 = P.emit("scalar", lambda e, bh=bh, pu=pu: e.activation(
                up_f[bh][:], pu[1][:, :], AF.Copy),
                waits=[mmu, ha_free[bh]], sig="scalar")
            ev2[pu[0]] = e2
            e3 = P.emit("vector", lambda e, bh=bh: e.tensor_tensor(
                ha_bf[bh][:], sg_f[bh][:], up_f[bh][:], ALU.mult),
                waits=[e1, e2], sig="vector")
            if level <= 7 and g == 0 and icx == 0:
                P.emit("gpsimd", lambda e, bh=bh: e.dma_start(
                    out=dbg_scr[512:640, :].rearrange(
                        "p o -> p o")[:, 0:TG], in_=ha_bf[bh][:]),
                    waits=[e3], sig="dDbg", inc=16)
                P.emit("gpsimd", lambda e, bh=bh: e.dma_start(
                    out=dbg_scr[640:768, 0:TG], in_=sg_f[bh][:]),
                    waits=[e3], sig="dDbg", inc=16)
                P.emit("gpsimd", lambda e, bh=bh: e.dma_start(
                    out=dbg_scr[768:896, 0:TG], in_=up_f[bh][:]),
                    waits=[e3], sig="dDbg", inc=16)
                P.emit("gpsimd", lambda e: e.dma_start(
                    out=dbg_scr[896:1024, :], in_=wgT[0][:, 0:D]),
                    waits=[tok_wg], sig="dDbg", inc=16)
            if level <= 7 and g == 0 and icx == 2:
                P.emit("gpsimd", lambda e: e.dma_start(
                    out=dbg_scr[0:512, :].rearrange("(a p) o -> p a o", p=128),
                    in_=xq_fm[:].rearrange("p (a o) -> p a o", o=1024)),
                    waits=[tok_xq], sig="dDbg", inc=16)
            e4 = P.emit("vector", lambda e, bh=bh: e.tensor_reduce(
                gtmp[:, 0:1], ha_bf[bh][:], AX.X, ALU.max,
                apply_absolute_value=True), waits=[e3], sig="vector")
            e5 = P.emit("vector", lambda e: e.tensor_tensor(
                gmv2[:], gmv2[:], gtmp[:, 0:1], ALU.max),
                waits=[t_gm20, e4], sig="vector")
            ha_free[bh] = P.emit("gpsimd", lambda e, icx=icx, g=g, bh=bh: e.dma_start(
                out=hact_scr[icx * 128:(icx + 1) * 128, g * TG:(g + 1) * TG],
                in_=ha_bf[bh][:]), waits=[e5], sig=f"dHa{bh}", inc=16)
        xq_free = mmu
    tok_wd = quant_weight(wd_d, DC, II, lambda c: wd3[:, c], tok_wu, PTX=PT2,
                          slots=1)
    tok_p2 = P.join("sync", [ha_free[0], ha_free[1]])

    st_p2.close()
    st_wql.close()
    P.barrier()

    t1 = P.emit("gpsimd", lambda e: e.dma_start(
        out=gmx_scr, in_=gmv2[:]), waits=[tok_p2], sig="dGm2", inc=16)
    t2 = P.emit("sync", lambda e: e.dma_start(
        out=grow[:], in_=gmx_scr.rearrange("p one -> one p")),
        waits=[t1], sig="dGr2", inc=16)
    t3 = P.emit("vector", lambda e: e.tensor_reduce(
        srow[:, 4:5], grow[:], AX.X, ALU.max), waits=[t2], sig="vector")
    t4 = P.emit("gpsimd", lambda e: e.dma_start(out=cc2_in, in_=srow[:, 4:5]),
                waits=[t3], sig="dCi2", inc=16)
    t5 = P.emit("gpsimd", lambda e: e.collective_compute(
        "AllReduce", ALU.max, replica_groups=RG8, ins=[cc2_in], outs=[cc2_out]),
        waits=[t4], sig="pool")
    t6 = P.emit("sync", lambda e: e.dma_start(out=srow[:, 4:5], in_=cc2_out),
                waits=[t5], sig="dCo2", inc=16)
    t7 = P.emit("scalar", lambda e: e.activation(
        srow[:, 5:6], clip_s[0:1, 1:2], AF.Sigmoid), waits=[t6], sig="scalar")
    t8 = P.emit("vector", lambda e: e.tensor_tensor(
        srow[:, 6:7], srow[:, 4:5], srow[:, 5:6], ALU.mult),
        waits=[t7], sig="vector")
    t9 = P.emit("vector", lambda e: e.tensor_scalar(
        srow[:, 6:7], srow[:, 6:7], 1.0 / 127.0, None, ALU.mult),
        waits=[t8], sig="vector")
    t10 = P.emit("scalar", lambda e: e.activation(
        srow[:, 7:8], srow[:, 6:7], AF.Ln), waits=[t9], sig="scalar")
    t11 = P.emit("scalar", lambda e: e.activation(
        srow[:, 7:8], srow[:, 7:8], AF.Exp, scale=f32(-1.0)), waits=[t10],
        sig="scalar")
    t12 = P.emit("tensor", lambda e: e.matmul(
        PB3[:, 0:2], ones_r[:, :], srow[:, 6:8], start=True, stop=True),
        waits=[t11], sig="tensor")
    tok_s2 = P.emit("vector", lambda e: e.tensor_copy(qsc[:, 2:4], PB3[:, 0:2]),
                    waits=[t12], sig="vector")

    if level <= 7:
        fin = P.emit("sync", lambda e: e.dma_start(
            out=out_d[0:128, 0:4], in_=qsc[:]), waits=[tok_s2],
            sig="dDmp", inc=16)
        P.join("sync", [fin])
        P.replay()
        return nc

    st_p3 = ExitStack()

    def sb3(name, shape, dt):
        return st_p3.enter_context(
            nc.sbuf_tensor(name, shape, dt, side="right")).ap()

    ha_ld = [sb3(f"ha_ld{i}", [128, TG], BF16) for i in range(2)]
    hq_fm = sb3("hq_fm", [128, IC * TG], BF16)
    x1p = [sb3(f"x1p{i}", [128, D], F32) for i in range(2)]
    otile = [sb3(f"otile{i}", [128, D], F32) for i in range(2)]
    qstg = sb3("qstg", [128, TG], F32)
    qsti = [sb3(f"qsti{i}", [128, TG], mybir.dt.int32) for i in range(2)]

    ha_tok = [None, None]
    ha_used = [None, None]
    hq3_free = [None, None]
    hq_free = None
    out_st = [None, None]
    ev3 = {"PA": None, "PB": None}
    x1_ld3 = [None, None]
    out_toks = []

    for g in range(NG):
        hq_done = []
        for icx in range(IC):
            bh = icx % 2
            ha_tok[bh] = P.emit("sync", lambda e, icx=icx, g=g, bh=bh: e.dma_start(
                out=ha_ld[bh][:],
                in_=hact_scr[icx * 128:(icx + 1) * 128, g * TG:(g + 1) * TG]),
                waits=[ha_used[bh], tok_p2], sig=f"dHl{bh}", inc=16)
            q1 = P.emit("vector", lambda e, bh=bh: e.tensor_scalar(
                qstg[:], ha_ld[bh][:], qsc[:, 3:4], -(QMAX + 1.0),
                ALU.mult, ALU.max),
                waits=[ha_tok[bh], tok_s2, hq_free, hq3_free[bh]], sig="vector")
            ha_used[bh] = q1
            q2v = P.emit("vector", lambda e, bh=bh: e.tensor_scalar(
                qsti[bh][:], qstg[:], QMAX, None, ALU.min), sig="vector")
            q3s = P.emit("scalar", lambda e, icx=icx, bh=bh: e.activation(
                hq_fm[:, icx * TG:(icx + 1) * TG], qsti[bh][:],
                AF.Copy, scale=qsc[:, 2:3]), waits=[q2v], sig="scalar")
            hq3_free[bh] = q3s
            hq_done.append(q3s)
        for ti in range(4):
            tt = g * 4 + ti
            bx = ti % 2
            x1_ld3[bx] = P.emit("sync", lambda e, tt=tt, bx=bx: e.dma_start(
                out=x1p[bx][:], in_=x1_scr[tt * 128:(tt + 1) * 128, :]),
                waits=[out_st[bx]], sig=f"dxp{bx}", inc=16)
            mm_l = None
            for oh, pname, pp in ((0, "PA", PA), (1, "PB", PB)):
                mm = None
                for icx in range(IC):
                    mm = P.emit("tensor", lambda e, icx=icx, ti=ti, oh=oh,
                                pp=pp: e.matmul(
                        pp[:, :],
                        hq_fm[:, icx * TG + ti * 128:icx * TG + (ti + 1) * 128],
                        wd3[:, icx, oh * 512:(oh + 1) * 512],
                        start=(icx == 0), stop=(icx == IC - 1)),
                        waits=[hq_done[icx], tok_wd, ev3[pname]], sig="tensor")
                mm_l = mm
                ev3[pname] = P.emit("scalar", lambda e, oh=oh, pp=pp,
                                    bx=bx: e.activation(
                    otile[bx][:, oh * 512:(oh + 1) * 512], pp[:, :], AF.Copy),
                    waits=[mm, out_st[bx]], sig="scalar")
            if ti == 3:
                hq_free = mm_l
            o1 = P.emit("vector", lambda e, bx=bx: e.tensor_tensor(
                otile[bx][:], otile[bx][:], x1p[bx][:], ALU.add),
                waits=[ev3["PA"], ev3["PB"], x1_ld3[bx]], sig="vector")
            out_st[bx] = P.emit("sync", lambda e, tt=tt, bx=bx: e.dma_start(
                out=out_d[tt * 128:(tt + 1) * 128, :], in_=otile[bx][:]),
                waits=[o1], sig=f"dOut{bx}", inc=16)
            out_toks.append(out_st[bx])

    P.join("sync", [out_toks[-1], out_toks[-2]])

    P.replay()
    st_wq.close()
    st_p3.close()
    st_ps.close()
    st_ofm_dummy = None
    st_perm.close()
    return nc


def _shard_inputs(inputs):
    x = np.asarray(inputs["L0_attn_x"])

    def lay(v, shape):
        return np.ascontiguousarray(np.asarray(v, np.float32).reshape(shape))

    def chunked(v, n):
        return np.ascontiguousarray(
            np.asarray(v, np.float32).reshape(n, 128).T)

    ins = {
        "Wq": lay(inputs["L0_attn_Wq"], (D, D)),
        "Wk": lay(inputs["L0_attn_Wk"], (D, D)),
        "Wv": lay(inputs["L0_attn_Wv"], (D, D)),
        "Wo": lay(inputs["L0_attn_Wo"], (D, D)),
        "bq": chunked(inputs["L0_attn_bq"], DC),
        "bk": chunked(inputs["L0_attn_bk"], DC),
        "bv": chunked(inputs["L0_attn_bv"], DC),
        "bo_row": lay(inputs["L0_attn_bo"], (1, D)),
        "gamma": lay(inputs["L0_attn_gamma"], (1, D)),
        "beta": lay(inputs["L0_attn_beta"], (1, D)),
        "W1": lay(inputs["L0_attn_W1"], (D, FF)),
        "b1": chunked(inputs["L0_attn_b1"], FC),
        "W2": lay(inputs["L0_attn_W2"], (FF, D)),
        "b2": lay(inputs["L0_attn_b2"], (1, D)),
        "clip_x": lay(inputs["L0_ffn_clip_x"], (1, 1)),
        "clip_down": lay(inputs["L0_ffn_clip_down"], (1, 1)),
        "w_gate": lay(inputs["L0_ffn_w_gate"], (II, D)),
        "w_up": lay(inputs["L0_ffn_w_up"], (II, D)),
        "w_down": lay(inputs["L0_ffn_w_down"], (D, II)),
    }
    in_maps = []
    for c in range(8):
        m = dict(ins)
        m["x0"] = np.ascontiguousarray(
            x[BLOC * c:BLOC * (c + 1)].reshape(T, D), np.float32)
        in_maps.append(m)
    return in_maps


def kernel(**inputs):
    nc = build_program()
    in_maps = _shard_inputs(inputs)
    res = run_bass_kernel_spmd(nc, in_maps, core_ids=list(range(8)))
    out = np.concatenate(
        [np.asarray(r["out"]).reshape(BLOC, SEQ, D) for r in res.results], axis=0)
    return out


if __name__ == "__main__":
    nc = build_program()
    print("built ok; instructions:", nc.n_instructions())
